# version 07
# baseline (speedup 1.0000x reference)
import numpy as np
import concourse.bass as bass
import concourse.mybir as mybir
from concourse.bass_utils import run_bass_kernel_spmd
from contextlib import ExitStack

F32 = mybir.dt.float32
BF16 = mybir.dt.bfloat16
AX = mybir.AxisListType
ALU = mybir.AluOpType
ACT = mybir.ActivationFunctionType

D = 4096
NIF = 32
H = 32
DH = 128
OWN = 1024
EXT = 1536
NKC = EXT // 128
SCALE = 1.0 / float(np.sqrt(128.0))
AW = 49152
ENGS = ("sync", "gpsimd", "scalar", "vector", "tensor")


STREAM_ENG = {"wload0": "sync", "wload1": "sync", "wmisc": "sync",
              "xload0": "gpsimd", "xload1": "gpsimd", "xstore": "gpsimd"}
SEMN = tuple(ENGS) + tuple(STREAM_ENG)


class Prog:
    def __init__(self):
        self.ops = {e: [] for e in ENGS}
        self.cnt = {s: 0 for s in SEMN}

    def emit(self, e, fn, waits=(), inc=0, isem=None):
        isem = isem or e
        self.ops[e].append((fn, tuple(waits), inc, isem))
        self.cnt[isem] += inc
        return self.cnt[isem]

    def dma(self, stream, out, in_, waits=()):
        e = STREAM_ENG[stream]
        return self.emit(e, (lambda q, o=out, i=in_: q.dma_start(out=o, in_=i)),
                         waits, 16, isem=stream)

    def drain(self):
        self.emit("vector", lambda v: v.drain(), (), 0)


class Arena:
    def __init__(self, t, off=0):
        self.t = t
        self.off = off

    def alloc(self, words):
        o = self.off
        self.off += words
        assert self.off <= AW, f"arena overflow {self.off} > {AW}"
        return o

    def f32(self, words, shape=None):
        o = self.alloc(words)
        ap = self.t[:, o:o + words]
        if shape is not None:
            ap = _resh(ap, shape)
        return ap

    def bf16(self, n, shape=None):
        o = self.alloc((n + 1) // 2)
        ap = self.t[:, o:o + (n + 1) // 2].bitcast(BF16)[:, 0:n]
        if shape is not None:
            ap = _resh(ap, shape)
        return ap

    def row(self, words):
        o = self.alloc(words)
        return self.t[0:1, o:o + words]


def _resh(ap, shape):
    names = [chr(97 + i) for i in range(len(shape))]
    return ap.rearrange("p (" + " ".join(names) + ") -> p " + " ".join(names),
                        **{n: s for n, s in zip(names, shape)})


def build():
    nc = bass.Bass("TRN2", target_bir_lowering=False)
    dt = lambda n, s, d=F32, k="ExternalInput": nc.dram_tensor(n, s, d, kind=k).ap()

    xin = dt("xin", [D, EXT])
    wbaseT = dt("wbaseT", [D, D])
    wqT = dt("wqT", [D, D])
    woT = dt("woT", [D, D])
    wkT = dt("wkT", [D, DH])
    wvT = dt("wvT", [D, DH])
    aT = dt("aT", [D, 128])
    bstk = dt("bstk", [128, D])
    wrT = dt("wrT", [D, 8])
    bbc = dt("bbc", [128, NIF])
    cosx_h = dt("cosx", [128, EXT])
    s2x_h = dt("s2x", [128, EXT])
    a0_h = dt("a0", [128, 8])
    a1_h = dt("a1", [128, 8])
    vm_h = dt("vm", [128, NKC])
    ones1_h = dt("ones1", [1, 128])
    onesc_h = dt("onesc", [128, 1])
    eye_h = dt("eye", [128, 128])
    out_h = dt("out", [D, OWN], F32, "ExternalOutput")
    x1h = dt("x1h", [D, EXT], F32, "Internal")
    x2h = dt("x2h", [D, OWN], F32, "Internal")

    xin_r = xin.rearrange("(c p) t -> p c t", p=128)
    x1h_r = x1h.rearrange("(c p) t -> p c t", p=128)
    x2h_r = x2h.rearrange("(c p) t -> p c t", p=128)
    out_r = out_h.rearrange("(c p) t -> p c t", p=128)
    wbaseT_r = wbaseT.rearrange("(c p) o -> p c o", p=128)
    wqT_r = wqT.rearrange("(c p) o -> p c o", p=128)
    woT_r = woT.rearrange("(c p) o -> p c o", p=128)
    wkT_r = wkT.rearrange("(c p) o -> p c o", p=128)
    wvT_r = wvT.rearrange("(c p) o -> p c o", p=128)
    aT_r = aT.rearrange("(c p) o -> p c o", p=128)
    wrT_r = wrT.rearrange("(c p) o -> p c o", p=128)

    ctx = ExitStack()
    E = ctx.enter_context
    arena_t = E(nc.sbuf_tensor("arena", [128, AW], F32))
    ar = Arena(arena_t)

    PS1 = E(nc.psum_tensor("PS1", [128, 1024], F32))
    PS2 = E(nc.psum_tensor("PS2", [128, 1024], F32))
    PS3 = E(nc.psum_tensor("PS3", [128, 1024], F32))
    PS4 = E(nc.psum_tensor("PS4", [128, 512], F32))
    PSM = E(nc.psum_tensor("PSM", [128, 512], F32))
    PS4b = PS4[:].bitcast(BF16)
    PS3v = PS3[:].rearrange("p (c t) -> p c t", c=4)

    sems = {e: E(nc.semaphore("s_" + e)) for e in SEMN}
    pr = Prog()

    xlnb = ar.bf16(NIF * EXT, (NIF, EXT))
    bbt = ar.f32(NIF)
    bbh2 = ar.f32(NIF)
    id32 = ar.f32(128)
    idb16 = ar.bf16(128)
    onesb = ar.bf16(2)
    vmb = ar.bf16(NKC)
    vmf = ar.f32(NKC)
    a0t = ar.f32(8)
    a1t = ar.f32(8)
    RO = ar.alloc(7 * 128)
    ones1 = arena_t[0:1, RO:RO + 128]
    murow = arena_t[0:1, RO + 128:RO + 256]
    msrow = arena_t[0:1, RO + 256:RO + 384]
    t1row = arena_t[0:1, RO + 384:RO + 512]
    varrow = arena_t[0:1, RO + 512:RO + 640]
    invsrow = arena_t[0:1, RO + 640:RO + 768]
    misrow = arena_t[0:1, RO + 768:RO + 896]
    PBASE = ar.off

    sa = Arena(arena_t, PBASE)
    eyetmp = sa.f32(128)
    onesctmp = sa.f32(1)
    c_eye = pr.dma("wmisc", eyetmp, eye_h)
    c_b = pr.dma("wmisc", bbt, bbc)
    c_vm = pr.dma("wmisc", vmf, vm_h)
    c_a0 = pr.dma("wmisc", a0t, a0_h)
    c_a1 = pr.dma("wmisc", a1t, a1_h)
    c_o1 = pr.dma("wmisc", ones1, ones1_h)
    c_oc = pr.dma("wmisc", onesctmp, onesc_h)
    c_id = pr.dma("wmisc", id32, eye_h)
    pr.emit("scalar", lambda s: s.activation(idb16, eyetmp, ACT.Copy), [("wmisc", c_id)], 1)
    pr.emit("scalar", lambda s: s.activation(onesb[:, 0:1], onesctmp, ACT.Copy), [("wmisc", c_id)], 1)
    setup_sc = pr.emit("scalar", lambda s: s.activation(vmb, vmf, ACT.Copy), [("wmisc", c_id)], 1)
    setup_ve = pr.emit("vector", lambda v: v.tensor_scalar(bbh2, bbt, 0.5, None, ALU.mult),
                       [("wmisc", c_id)], 1)

    def ln_prep(src_r, ntok, base, entry=()):
        a = Arena(arena_t, base)
        xf = [a.f32(NIF * 128, (NIF, 128)), a.f32(NIF * 128, (NIF, 128))]
        xb = [a.bf16(NIF * 256, (NIF, 256)), a.bf16(NIF * 256, (NIF, 256))]
        nch = ntok // 128
        apply2 = [0] * nch
        rankc = [0] * nch
        stats = [0] * nch
        rowrd = [0] * nch
        s1l = [0] * nch
        for i in range(nch):
            par = i % 2
            ps_st = [PSM[0:1, 0:256], PS1[0:1, 0:256]][par]
            ps_iv = [PS4[:, 0:128], PS2[:, 0:128]][par]
            ps_mi = [PSM[:, 0:128], PS1[:, 0:128]][par]
            wld = list(entry) if i == 0 else ([("vector", apply2[i - 2])] if i >= 2 else [])
            wld = wld + [("xstore", pr.cnt["xstore"])]
            sem = "xload%d" % par
            g = pr.dma(sem, xf[par][:, 0:16, :], src_r[:, 0:16, i * 128:(i + 1) * 128], wld)
            g = pr.dma(sem, xf[par][:, 16:32, :], src_r[:, 16:32, i * 128:(i + 1) * 128], [])
            wsc = [(sem, g)]
            if i >= 2:
                wsc.append(("tensor", stats[i - 2]))
            pr.emit("scalar", lambda s, p=par: s.activation(xb[p][:, :, 0:128], xf[p], ACT.Copy), wsc, 1)
            s1l[i] = s1 = pr.emit("scalar", lambda s, p=par: s.activation(
                xb[p][:, :, 128:256], xf[p], ACT.Square), [], 1)
            wmm = [("scalar", s1)]
            if i >= 2:
                wmm.append(("vector", apply2[i - 2]))
            for c in range(NIF):
                stats[i] = pr.emit("tensor", lambda t, c=c, p=par, ps=ps_st: t.matmul(
                    ps, onesb[:, 0:1], xb[p][:, c, :],
                    start=(c == 0), stop=(c == NIF - 1)), wmm if c == 0 else [],
                    1 if c == NIF - 1 else 0)
            wv = [("tensor", stats[i])]
            if i >= 1:
                wv.append(("tensor", rankc[i - 1]))
            pr.emit("vector", lambda v, ps=ps_st: v.tensor_scalar(
                murow, ps[0:1, 0:128], 1.0 / D, None, ALU.mult), wv, 1)
            pr.emit("vector", lambda v, ps=ps_st: v.tensor_scalar(
                msrow, ps[0:1, 128:256], 1.0 / D, None, ALU.mult), [], 1)
            pr.drain()
            pr.emit("vector", lambda v: v.tensor_tensor(t1row, murow, murow, ALU.mult), [], 1)
            pr.drain()
            pr.emit("vector", lambda v: v.tensor_tensor(varrow, msrow, t1row, ALU.subtract), [], 1)
            pr.drain()
            rv = pr.emit("vector", lambda v: v.tensor_scalar(varrow, varrow, 1e-5, None, ALU.add), [], 1)
            si = pr.emit("scalar", lambda s: s.activation(t1row, varrow, ACT.Sqrt),
                         [("vector", rv)], 1)
            vr = pr.emit("vector", lambda v: v.reciprocal(invsrow, t1row), [("scalar", si)], 1)
            pr.drain()
            rowrd[i] = pr.emit("vector", lambda v: v.tensor_tensor(misrow, murow, invsrow, ALU.mult),
                               [], 1)
            pr.emit("tensor", lambda t, ps=ps_iv: t.matmul(ps, ones1, invsrow, start=True, stop=True),
                    [("vector", rowrd[i])], 1)
            rankc[i] = pr.emit("tensor", lambda t, ps=ps_mi: t.matmul(
                ps, ones1, misrow, start=True, stop=True), [], 1)
            pr.emit("vector", lambda v, p=par, ps=ps_iv: v.tensor_tensor(
                xf[p], xf[p], ps.unsqueeze(1).broadcast_to([128, NIF, 128]), ALU.mult),
                [("tensor", rankc[i])], 1)
            pr.drain()
            apply2[i] = pr.emit("vector", lambda v, p=par, i=i, ps=ps_mi: v.tensor_tensor(
                xlnb[:, :, i * 128:(i + 1) * 128], xf[p],
                ps.unsqueeze(1).broadcast_to([128, NIF, 128]), ALU.subtract), [], 1)
        return {"vector": apply2[nch - 1], "tensor": max(stats[nch - 1], rankc[nch - 1]),
                "scalar": s1l[nch - 1]}

    def make_stager(wstage):
        state = {"i": 0, "cast": {}}

        def load_cast(src_ap, dst_bf_ap, extra_dma_waits=(), extra_cast_waits=()):
            i = state["i"]
            state["i"] += 1
            wdma = list(extra_dma_waits)
            if i - 2 in state["cast"]:
                wdma.append(("scalar", state["cast"][i - 2]))
            sem = "wload%d" % (i % 2)
            g = pr.dma(sem, wstage[i % 2], src_ap, wdma)
            wc = [(sem, g)] + list(extra_cast_waits)
            c = pr.emit("scalar", lambda s, si=i % 2, d=dst_bf_ap: s.activation(d, wstage[si], ACT.Copy), wc, 1)
            state["cast"][i] = c
            return c
        return load_cast

    def ffn(src_r, dst_r, ntok):
        snap0 = [("scalar", pr.cnt["scalar"]), ("vector", pr.cnt["vector"]),
                 ("tensor", pr.cnt["tensor"])]
        a0a = Arena(arena_t, PBASE)
        ab16 = a0a.bf16(NIF * 128, (NIF, 128))
        bb16 = a0a.bf16(D)
        rb16 = a0a.bf16(NIF * 8, (NIF, 8))
        hwb = a0a.bf16(EXT)
        PREP_B = a0a.off
        st_a = Arena(arena_t, PREP_B)
        atmp = st_a.f32(NIF * 128, (NIF, 128))
        bstmp = st_a.f32(D)
        rtmp = st_a.f32(NIF * 8, (NIF, 8))
        c_a = pr.dma("wmisc", atmp, aT_r, snap0)
        c_bs = pr.dma("wmisc", bstmp, bstk, [])
        c_r = pr.dma("wmisc", rtmp, wrT_r, [])
        pr.emit("scalar", lambda s: s.activation(ab16[:], atmp, ACT.Copy), [("wmisc", c_r)] + snap0, 1)
        pr.emit("scalar", lambda s: s.activation(bb16[:], bstmp, ACT.Copy), [("wmisc", c_r)], 1)
        sm_cast = pr.emit("scalar", lambda s: s.activation(rb16[:], rtmp, ACT.Copy), [("wmisc", c_r)], 1)
        snap1 = [("scalar", sm_cast), ("vector", pr.cnt["vector"]), ("tensor", pr.cnt["tensor"])]
        pd = ln_prep(src_r, ntok, PREP_B, snap1)
        prep_v, prep_t = pd["vector"], pd["tensor"]
        a = Arena(arena_t, PREP_B)
        wstage = [a.f32(8 * 128, (8, 128)), a.f32(8 * 128, (8, 128))]
        wb = [a.bf16(NIF * 128, (NIF, 128)), a.bf16(NIF * 128, (NIF, 128))]
        sl_sizes = []
        t0 = 0
        while t0 < ntok:
            n = min(512, ntok - t0)
            sl_sizes.append((t0, n))
            t0 += n
        slabs = [[a.f32(n) for (s0, n) in sl_sizes] for _ in range(2)]
        gxb = a.f32(512)
        lg = arena_t[0:8, a.alloc(512) + 0:a.off]
        ltm = a.f32(32, (4, 8))
        gate = a.f32(32, (4, 8))
        eq1 = a.f32(32, (4, 8))
        eq2 = a.f32(32, (4, 8))
        mskb = a.f32(32)
        m1c = a.f32(4)
        m2c = a.f32(4)
        p1c = a.f32(4)
        p2c = a.f32(4)
        stage = make_stager(wstage)

        NTS = ntok // 512
        hw_cnt = 0
        hw_hist = [0] * (NTS + 2)
        se_prev = 0
        sg_prev = 0
        for ts in range(NTS):
            cols = slice(ts * 512, (ts + 1) * 512)
            hslot = PS3[:, (ts % 2) * 512:(ts % 2) * 512 + 512]
            wmm = [("vector", prep_v)]
            if ts >= 2:
                wmm.append(("vector", hw_hist[ts - 2]))
            for c in range(NIF):
                pr.emit("tensor", lambda t, c=c, h=hslot, cc=cols: t.matmul(
                    h, ab16[:, c, :], xlnb[:, c, cc], start=(c == 0), stop=(c == NIF - 1)),
                    wmm if c == 0 else [], 0)
            wrt = [("scalar", se_prev)] if ts > 0 else [("vector", prep_v)]
            h_cnt = 0
            for c in range(NIF):
                h_cnt = pr.emit("tensor", lambda t, c=c, cc=cols: t.matmul(
                    PSM[0:8, 0:512], rb16[:, c, :], xlnb[:, c, cc],
                    start=(c == 0), stop=(c == NIF - 1)), wrt if c == 0 else [],
                    1 if c == NIF - 1 else 0)
            se = pr.emit("scalar", lambda s: s.activation(lg, PSM[0:8, 0:512], ACT.Copy),
                         [("tensor", h_cnt)], 1)
            se_prev = se
            tp = 0
            for qc in range(4):
                tp = pr.emit("tensor", lambda t, qc=qc: t.transpose(
                    PS4[:, qc * 8:(qc + 1) * 8], lg[0:8, qc * 128:(qc + 1) * 128], id32[0:8, 0:8]),
                    ([("scalar", se), ("scalar", sg_prev)] if qc == 0 else []), 1 if qc == 3 else 0)
            se2 = pr.emit("scalar", lambda s: s.activation(
                ltm, _resh(PS4[:, 0:32], (4, 8)), ACT.Copy), [("tensor", tp)], 1)
            pr.emit("vector", lambda v: v.reduce_max(m1c, ltm, axis=AX.X), [("scalar", se2)], 1)
            pr.drain()
            for qc in range(4):
                pr.emit("vector", lambda v, qc=qc: v.tensor_scalar(
                    eq1[:, qc, :], ltm[:, qc, :], m1c[:, qc:qc + 1], None, ALU.is_equal), [], 1)
            pr.drain()
            pr.emit("vector", lambda v: v.scalar_tensor_tensor(
                mskb, eq1.rearrange("p c e -> p (c e)"), -30.0, ltm.rearrange("p c e -> p (c e)"),
                ALU.mult, ALU.add), [], 1)
            pr.drain()
            vm2 = pr.emit("vector", lambda v: v.reduce_max(m2c, _resh(mskb, (4, 8)), axis=AX.X), [], 1)
            sp = 0
            for qc in range(4):
                sp = pr.emit("scalar", lambda s, qc=qc: s.activation(
                    p1c[:, qc:qc + 1], m2c[:, qc:qc + 1], ACT.Sigmoid, bias=m1c[:, qc:qc + 1], scale=-1.0),
                    [("vector", vm2)] if qc == 0 else [], 1)
                sp = pr.emit("scalar", lambda s, qc=qc: s.activation(
                    p2c[:, qc:qc + 1], m1c[:, qc:qc + 1], ACT.Sigmoid, bias=m2c[:, qc:qc + 1], scale=-1.0),
                    [], 1)
            for qc in range(4):
                pr.emit("vector", lambda v, qc=qc: v.tensor_scalar(
                    gate[:, qc, :], eq1[:, qc, :], p1c[:, qc:qc + 1], 2.0, ALU.mult, ALU.mult),
                    [("scalar", sp)] if qc == 0 else [], 1)
            for qc in range(4):
                pr.emit("vector", lambda v, qc=qc: v.tensor_scalar(
                    eq2[:, qc, :], _resh(mskb, (4, 8))[:, qc, :], m2c[:, qc:qc + 1], None, ALU.is_equal),
                    [], 1)
            pr.drain()
            for qc in range(4):
                pr.emit("vector", lambda v, qc=qc: v.tensor_scalar(
                    eq2[:, qc, :], eq2[:, qc, :], p2c[:, qc:qc + 1], 2.0, ALU.mult, ALU.mult), [], 1)
            pr.drain()
            pr.emit("vector", lambda v: v.tensor_tensor(
                gate.rearrange("p c e -> p (c e)"), gate.rearrange("p c e -> p (c e)"),
                eq2.rearrange("p c e -> p (c e)"), ALU.add), [], 1)
            pr.drain()
            vgx = pr.emit("vector", lambda v: v.tensor_copy(
                _resh(gxb, (4, 8, 16)), gate.unsqueeze(3).broadcast_to([128, 4, 8, 16])), [], 1)
            tg = 0
            for qc in range(4):
                tg = pr.emit("tensor", lambda t, qc=qc: t.transpose(
                    PS4[:, qc * 128:(qc + 1) * 128], gxb[:, qc * 128:(qc + 1) * 128], id32),
                    [("vector", vgx), ("scalar", se2)] if qc == 0 else [], 1 if qc == 3 else 0)
            sg = pr.emit("scalar", lambda s: s.activation(gxb, PS4[:, 0:512], ACT.Copy),
                         [("tensor", tg)], 1)
            sg_prev = sg
            hw_hist[ts] = hw_cnt = pr.emit("vector", lambda v, h=hslot, cc=cols: v.tensor_tensor(
                hwb[:, cc], h, gxb, ALU.mult), [("scalar", sg)], 1)

        grp_cnt = {}
        evac_cnt = {}
        first_slab = True
        for of in range(NIF):
            extra = [("vector", prep_v)] if of == 0 else []
            ecast = [("tensor", prep_t)] if of == 0 else []
            if of >= 2:
                ecast = [("tensor", grp_cnt[(of - 2, len(sl_sizes) - 1)])]
            for s4 in range(4):
                stage(wbaseT_r[:, s4 * 8:(s4 + 1) * 8, of * 128:(of + 1) * 128],
                      wb[of % 2][:, s4 * 8:(s4 + 1) * 8, :],
                      extra if s4 == 0 else (), ecast if s4 == 0 else ())
            cast_done = pr.cnt["scalar"]
            xsem = "xload%d" % (of % 2)
            for tsl, (t0, n) in enumerate(sl_sizes):
                wld = [("vector", prep_v)] if first_slab else []
                first_slab = False
                wld = wld + [("xstore", pr.cnt["xstore"])]
                gld = pr.dma(xsem, slabs[of % 2][tsl], src_r[:, of, t0:t0 + n], wld)
            for tsl, (t0, n) in enumerate(sl_sizes):
                gidx = of * len(sl_sizes) + tsl
                pspair = [PS1, PS2][(gidx // 2) % 2]
                off = (gidx % 2) * 512
                slot = pspair[:, off:off + n]
                wmm = [("scalar", cast_done), ("vector", hw_cnt)]
                if gidx >= 4:
                    wmm.append(("vector", evac_cnt[gidx - 4]))
                for c in range(NIF):
                    pr.emit("tensor", lambda t, c=c, sl=slot, of=of, t0=t0, n=n: t.matmul(
                        sl, wb[of % 2][:, c, :], xlnb[:, c, t0:t0 + n],
                        start=(c == 0), stop=False), wmm if c == 0 else [], 0)
                gmm = pr.emit("tensor", lambda t, sl=slot, of=of, t0=t0, n=n: t.matmul(
                    sl, bb16[:, of * 128:(of + 1) * 128], hwb[:, t0:t0 + n],
                    start=False, stop=True), [], 1)
                grp_cnt[(of, tsl)] = gmm
                pr.emit("vector", lambda v, of=of, tsl=tsl: v.tensor_scalar(
                    slabs[of % 2][tsl], slabs[of % 2][tsl], 1.0, bbh2[:, of:of + 1],
                    ALU.mult, ALU.add), [(xsem, gld)], 1)
                pr.drain()
                ev = pr.emit("vector", lambda v, of=of, tsl=tsl, sl=slot: v.scalar_tensor_tensor(
                    slabs[of % 2][tsl], sl, 0.5, slabs[of % 2][tsl],
                    ALU.mult, ALU.add), [("tensor", gmm)], 1)
                evac_cnt[gidx] = ev
                pr.dma("xstore", dst_r[:, of, t0:t0 + n], slabs[of % 2][tsl], [("vector", ev)])

    def attn():
        snap0 = [("scalar", pr.cnt["scalar"]), ("vector", pr.cnt["vector"]),
                 ("tensor", pr.cnt["tensor"])]
        pd = ln_prep(x1h_r, EXT, PBASE, snap0)
        prep_v, prep_t = pd["vector"], pd["tensor"]
        a = Arena(arena_t, PBASE)
        cosx = a.f32(EXT)
        s2x = a.f32(EXT)
        krb = a.bf16(EXT)
        vext = a.bf16(NKC * 132, (NKC, 132))
        wstage = [a.f32(4 * 128, (4, 128)), a.f32(4 * 128, (4, 128))]
        wqb = [a.bf16(NIF * 128, (NIF, 128)), a.bf16(NIF * 128, (NIF, 128))]
        qt1 = a.f32(1024)
        qt2 = a.f32(1024)
        qrb = [a.bf16(1024), a.bf16(1024)]
        vbst = a.bf16(1024)
        sgb = [a.bf16(4 * 512, (4, 512)), a.bf16(4 * 512, (4, 512))]
        wnr = [a.f32(4 * 129, (4, 129)), a.f32(4 * 129, (4, 129))]
        rca = [a.f32(4), a.f32(4)]
        wout8 = a.bf16(8 * 1024, (8, 1024))
        slabs = [a.f32(512), a.f32(512)]
        diag = a.f32(4 * 128, (4, 128))
        stage = make_stager(wstage)

        cl = pr.dma("wmisc", cosx, cosx_h, [("vector", prep_v)])
        cos_l = pr.dma("wmisc", s2x, s2x_h, [])
        pr.drain()
        pr.emit("vector", lambda v: v.memset(vext[:, :, 129:132], 0.0), [], 1)

        for s8 in range(8):
            stage(wkT_r[:, s8 * 4:(s8 + 1) * 4, :], wqb[0][:, s8 * 4:(s8 + 1) * 4, :],
                  [("vector", prep_v)] if s8 == 0 else (), [("tensor", prep_t)] if s8 == 0 else ())
        for s8 in range(8):
            stage(wvT_r[:, s8 * 4:(s8 + 1) * 4, :], wqb[1][:, s8 * 4:(s8 + 1) * 4, :])
        kv_cast = pr.cnt["scalar"]
        kvw = pr.cnt["wmisc"]

        kv_ev = 0
        vmm = 0
        kev = 0
        for grp in range(3):
            t0 = grp * 512
            n = 512
            cols = slice(t0, t0 + n)
            off = (grp % 2) * 512
            wmm = [("scalar", kv_cast), ("vector", prep_v)]
            if grp >= 1:
                wmm.append(("vector", kev))
            kmm = 0
            for c in range(NIF):
                kmm = pr.emit("tensor", lambda t, c=c, cc=cols, off=off: t.matmul(
                    PS1[:, off:off + 512], wqb[0][:, c, :], xlnb[:, c, cc],
                    start=(c == 0), stop=(c == NIF - 1)),
                    wmm if c == 0 else [], 1 if c == NIF - 1 else 0)
            for c in range(NIF):
                vmm = pr.emit("tensor", lambda t, c=c, cc=cols, off=off: t.matmul(
                    PS2[:, off:off + 512], wqb[1][:, c, :], xlnb[:, c, cc],
                    start=(c == 0), stop=(c == NIF - 1)),
                    [], 1 if c == NIF - 1 else 0)
            pr.emit("vector", lambda v, cc=cols, off=off: v.tensor_tensor(
                qt1[:, 0:512], PS1[:, off:off + 512], cosx[:, cc], ALU.mult),
                [("tensor", kmm), ("wmisc", kvw)], 1)
            pr.emit("vector", lambda v, cc=cols, off=off: v.tensor_tensor(
                qt2[0:64, 0:512], PS1[64:128, off:off + 512], s2x[0:64, cc], ALU.mult), [], 1)
            pr.emit("vector", lambda v, cc=cols, off=off: v.tensor_tensor(
                qt2[64:128, 0:512], PS1[0:64, off:off + 512], s2x[64:128, cc], ALU.mult), [], 1)
            pr.drain()
            kev = pr.emit("vector", lambda v, cc=cols: v.tensor_tensor(
                krb[:, cc], qt1[:, 0:512], qt2[:, 0:512], ALU.add), [], 1)
            sv = pr.emit("scalar", lambda s, off=off: s.activation(
                vbst[:, 0:512], PS2[:, off:off + 512], ACT.Copy), [("tensor", vmm)], 1)
            tv = 0
            for k2 in range(4):
                tv = pr.emit("tensor", lambda t, k2=k2: t.transpose(
                    PS4b[:, k2 * 128:(k2 + 1) * 128], vbst[:, k2 * 128:(k2 + 1) * 128],
                    idb16), [("scalar", sv), ("vector", kv_ev)] if k2 == 0 else [],
                    1 if k2 == 3 else 0)
            kcg0 = t0 // 128
            pr.emit("vector", lambda v, kcg0=kcg0: v.tensor_tensor(
                vext[:, kcg0:kcg0 + 4, 0:128], _resh(PS4b, (8, 128))[:, 0:4, :],
                vmb[:, kcg0:kcg0 + 4].unsqueeze(2).broadcast_to([128, 4, 128]), ALU.mult),
                [("tensor", tv)], 1)
            kv_ev = pr.emit("vector", lambda v, kcg0=kcg0: v.tensor_copy(
                vext[:, kcg0:kcg0 + 4, 128:129], vmb[:, kcg0:kcg0 + 4].unsqueeze(2)), [], 1)

        qmm = [0] * (H + 2)
        rope = [0] * (H + 2)
        wqc = [0] * (H + 2)
        sig_cnt = 0
        sig2_prev = 0
        av_hist = {}
        wn_last = kv_ev
        woutev = 0
        womm_prev_grp = max(vmm, 1)
        wo_ev = {}
        wo_done = 0
        dg_guard = 0

        def stage_wq(h, extra=()):
            for s8 in range(8):
                ew = []
                if h < 2 and s8 == 0:
                    ew = [("tensor", vmm)]
                elif s8 == 0:
                    ew = [("tensor", qmm[h - 2])]
                ew = ew + (list(extra) if s8 == 0 else [])
                stage(wqT_r[:, s8 * 4:(s8 + 1) * 4, h * 128:(h + 1) * 128],
                      wqb[h % 2][:, s8 * 4:(s8 + 1) * 4, :], (), ew)
            wqc[h] = pr.cnt["scalar"]

        def make_qfeeder(h, extra_first=()):
            stepctr = {"n": 0}

            def feed(k):
                while k > 0 and stepctr["n"] < 64:
                    j = stepctr["n"]
                    hf, c = j // NIF, j % NIF
                    w = []
                    if j == 0:
                        w = [("scalar", wqc[h]), ("vector", prep_v)] + list(extra_first)
                        if h >= 1:
                            w.append(("vector", rope[h - 1]))
                        if h % 8 == 0 and h > 0:
                            w.append(("vector", wo_done))
                        if h == 0:
                            w.append(("vector", kv_ev))
                    qmm[h] = pr.emit("tensor", lambda t, c=c, par=h % 2, hf=hf: t.matmul(
                        PS1[:, hf * 512:(hf + 1) * 512], wqb[par][:, c, :],
                        xlnb[:, c, 256 + hf * 512:768 + hf * 512],
                        start=(c == 0), stop=(c == NIF - 1)), w,
                        1 if (c == NIF - 1 and hf == 1) else 0)
                    stepctr["n"] += 1
                    k -= 1
            return feed

        stage_wq(0)
        feed = make_qfeeder(0)
        feed(64)
        for hp in range(4):
            womm_last = 0
            for hh in range(8):
                h = hp * 8 + hh
                par = h % 2
                pr.emit("vector", lambda v, par=par: v.tensor_tensor(
                    qt1, PS1[:, 0:1024], cosx[:, 256:1280], ALU.mult), [("tensor", qmm[h])], 1)
                pr.emit("vector", lambda v: v.tensor_tensor(qt2[0:64, :], PS1[64:128, 0:1024],
                                                            s2x[0:64, 256:1280], ALU.mult), [], 1)
                pr.emit("vector", lambda v: v.tensor_tensor(qt2[64:128, :], PS1[0:64, 0:1024],
                                                            s2x[64:128, 256:1280], ALU.mult), [], 1)
                pr.drain()
                rope[h] = pr.emit("vector", lambda v, par=par: v.tensor_tensor(
                    qrb[par], qt1, qt2, ALU.add), [], 1)
                if hh < 7:
                    stage_wq(h + 1)
                    feed = make_qfeeder(h + 1)
                else:
                    feed = lambda k: None
                for wi in range(5):
                    q0w = max(0, wi * 256 - 256)
                    q1w = min(1024, wi * 256 + 256)
                    qn = q1w - q0w
                    npp = qn // 128
                    kc0 = wi * 2
                    sp = wi % 2
                    w01 = [("vector", rope[h]), ("scalar", sig2_prev)]
                    for kc in (0, 1):
                        s01 = pr.emit("tensor", lambda t, kc=kc, q0w=q0w, qn=qn, kc0=kc0, par=par: t.matmul(
                            PS2[:, kc * 512:kc * 512 + qn], krb[:, (kc0 + kc) * 128:(kc0 + kc + 1) * 128],
                            qrb[par][:, q0w:q0w + qn], start=True, stop=True),
                            w01 if kc == 0 else [], 1 if kc == 1 else 0)
                    feed(6)
                    wsg1 = [("tensor", s01)]
                    if (h, wi - 2) in av_hist:
                        wsg1.append(("tensor", av_hist[(h, wi - 2)]))
                    elif wi < 2 and h >= 1:
                        wsg1.append(("tensor", av_hist[(h - 1, 4 - wi)]))
                    pr.emit("scalar", lambda s, qn=qn, sp=sp: s.activation(
                        sgb[sp][:, 0:2, 0:qn], PS2[:].rearrange("p (c t) -> p c t", c=2)[:, :, 0:qn],
                        ACT.Sigmoid, scale=SCALE), wsg1, 1)
                    sig1 = pr.cnt["scalar"]
                    for kc in (2, 3):
                        s23 = pr.emit("tensor", lambda t, kc=kc, q0w=q0w, qn=qn, kc0=kc0, par=par: t.matmul(
                            PS2[:, (kc - 2) * 512:(kc - 2) * 512 + qn],
                            krb[:, (kc0 + kc) * 128:(kc0 + kc + 1) * 128],
                            qrb[par][:, q0w:q0w + qn], start=True, stop=True),
                            [("scalar", sig1)] if kc == 2 else [], 1 if kc == 3 else 0)
                    feed(6)
                    sig_cnt = pr.emit("scalar", lambda s, qn=qn, sp=sp: s.activation(
                        sgb[sp][:, 2:4, 0:qn], PS2[:].rearrange("p (c t) -> p c t", c=2)[:, :, 0:qn],
                        ACT.Sigmoid, scale=SCALE), [("tensor", s23)], 1)
                    sig2_prev = sig_cnt
                    avm = 0
                    for p in range(npp):
                        for kc in range(4):
                            st = (p % 2 == 0 and kc == 0)
                            sp_ = (p % 2 == 1 and kc == 3)
                            avm = pr.emit("tensor", lambda t, p=p, kc=kc, kc0=kc0, st=st, sp_=sp_, sp=sp: t.matmul(
                                PS3[:, p * 256:p * 256 + 130], sgb[sp][:, kc, p * 128:(p + 1) * 128],
                                vext[:, kc0 + kc, 0:130], start=st, stop=sp_),
                                ([("scalar", sig_cnt), ("vector", wn_last)] if (p == 0 and kc == 0) else []),
                                1 if (p == npp - 1 and kc == 3) else 0)
                    av_hist[(h, wi)] = avm
                    feed(4)
                    wn_last = pr.emit("vector", lambda v, npp=npp, wi=wi: v.tensor_copy(
                        wnr[wi % 2][:, 0:npp, :], PS3v[:, 0:npp, 0:129]), [("tensor", avm)], 1)
                    pr.drain()
                    pr.emit("vector", lambda v, npp=npp, wi=wi: v.reciprocal(
                        rca[wi % 2][:, 0:npp], wnr[wi % 2][:, 0:npp, 128:129][:, :, 0]), [], 1)
                    pr.drain()
                    if wi >= 1:
                        b = wi - 1
                        sA = 0 if b == 0 else 2
                        dgc = 0
                        for j in range(2):
                            pr.emit("vector", lambda v, b=b, j=j, sA=sA: v.tensor_scalar(
                                diag[:, j, :], id32, a0t[:, 2 * b + j:2 * b + j + 1],
                                rca[b % 2][:, sA + j:sA + j + 1], ALU.mult, ALU.mult),
                                [("tensor", dg_guard)] if j == 0 else [], 1)
                            dgc = pr.emit("vector", lambda v, b=b, j=j, wi=wi: v.tensor_scalar(
                                diag[:, 2 + j, :], id32, a1t[:, 2 * b + j:2 * b + j + 1],
                                rca[wi % 2][:, j:j + 1], ALU.mult, ALU.mult), [], 1)
                        tw = 0
                        for j in range(2):
                            pr.emit("tensor", lambda t, b=b, j=j, sA=sA: t.matmul(
                                PS4[:, j * 128:(j + 1) * 128], wnr[b % 2][:, sA + j, 0:128],
                                diag[:, j, :], start=True, stop=False),
                                [("vector", dgc), ("scalar", woutev)] if j == 0 else [], 0)
                            tw = pr.emit("tensor", lambda t, b=b, j=j, wi=wi: t.matmul(
                                PS4[:, j * 128:(j + 1) * 128], wnr[wi % 2][:, j, 0:128],
                                diag[:, 2 + j, :], start=False, stop=True), [], 1)
                        dg_guard = tw
                        woutev = pr.emit("scalar", lambda s, hh=hh, b=b: s.activation(
                            wout8[:, hh, b * 256:(b + 1) * 256], PS4[:, 0:256], ACT.Copy),
                            [("tensor", tw), ("tensor", womm_prev_grp)], 1)
                feed(64)
            for of in range(NIF):
                for s2_ in range(2):
                    ew = []
                    if of == 0 and s2_ == 0:
                        ew = [("tensor", qmm[hp * 8 + 7])]
                    elif of >= 2 and s2_ == 0:
                        ew = [("tensor", wo_grp)]
                    stage(woT_r[:, hp * 8 + s2_ * 4:hp * 8 + (s2_ + 1) * 4, of * 128:(of + 1) * 128],
                          wqb[of % 2][:, s2_ * 4:(s2_ + 1) * 4, :], (), ew)
                wo_cast = pr.cnt["scalar"]
                for ts2 in range(2):
                    gidx = of * 2 + ts2
                    slot = [PS1, PS2][gidx % 2]
                    sli = gidx % 2
                    xsem = "xload%d" % sli
                    wld = [("vector", prep_v)] if (hp == 0 and gidx == 0) else []
                    wld = wld + [("xstore", pr.cnt["xstore"])]
                    gld = pr.dma(xsem, slabs[sli],
                                 (x1h_r[:, of, 256 + ts2 * 512:256 + (ts2 + 1) * 512] if hp == 0
                                  else x2h_r[:, of, ts2 * 512:(ts2 + 1) * 512]), wld)
                    wmm = [("scalar", wo_cast), ("scalar", woutev), ("scalar", sig_cnt)]
                    if gidx == 0:
                        wmm.append(("vector", rope[hp * 8 + 7]))
                    if gidx >= 2:
                        wmm.append(("vector", wo_ev[gidx - 2]))
                    elif hp > 0:
                        wmm.append(("vector", wo_ev[62 + gidx]))
                    mmc = 0
                    for gg in range(8):
                        mmc = pr.emit("tensor", lambda t, gg=gg, of=of, ts2=ts2, sl=slot: t.matmul(
                            sl[:, 0:512], wqb[of % 2][:, gg, :],
                            wout8[:, gg, ts2 * 512:(ts2 + 1) * 512],
                            start=(gg == 0), stop=(gg == 7)), wmm if gg == 0 else [],
                            1 if gg == 7 else 0)
                    wo_grp = mmc
                    womm_last = mmc
                    ev = pr.emit("vector", lambda v, sli=sli, sl=slot: v.tensor_tensor(
                        slabs[sli], sl[:, 0:512], slabs[sli], ALU.add),
                        [("tensor", mmc), (xsem, gld)], 1)
                    wo_ev[gidx] = ev
                    pr.dma("xstore", x2h_r[:, of, ts2 * 512:(ts2 + 1) * 512], slabs[sli],
                           [("vector", ev)])
            womm_prev_grp = womm_last
            wo_done = pr.cnt["vector"]
            wo_ev = {62: wo_ev[62], 63: wo_ev[63]} if 63 in wo_ev else wo_ev
            if hp < 3:
                stage_wq(hp * 8 + 8, extra=[("tensor", womm_last)])
                feed = make_qfeeder(hp * 8 + 8, extra_first=[("vector", wo_ev[63])])
                feed(64)

    ffn(xin_r, x1h_r, EXT)
    attn()
    ffn(x2h_r, out_r, OWN)

    total_gps = pr.cnt["xstore"]

    with nc.Block() as block:
        def mk(e):
            def run(eng):
                hw = {}
                for fn, waits, inc, isem in pr.ops[e]:
                    for (we, wv) in waits:
                        if we == e:
                            continue
                        if wv > 0 and hw.get(we, -1) < wv:
                            eng.wait_ge(sems[we], wv)
                            hw[we] = wv
                    ins = fn(eng)
                    if inc:
                        ins.then_inc(sems[isem], inc)
                if e == "gpsimd":
                    eng.wait_ge(sems["xstore"], total_gps)
            return run
        block.sync(mk("sync"))
        block.gpsimd(mk("gpsimd"))
        block.scalar(mk("scalar"))
        block.vector(mk("vector"))
        block.tensor(mk("tensor"))
    return nc, ctx


def _rope_tables(pos):
    inv = 1.0 / (10000.0 ** (np.arange(0, DH, 2, dtype=np.float32) / np.float32(DH)))
    f = np.outer(pos.astype(np.float32), inv)
    cos = np.zeros((len(pos), DH), dtype=np.float32)
    sin = np.zeros((len(pos), DH), dtype=np.float32)
    cos[:, 0::2] = np.cos(f); cos[:, 1::2] = np.cos(f)
    sin[:, 0::2] = np.sin(f); sin[:, 1::2] = np.sin(f)
    return cos, sin


def kernel(L0_attn_x, L0_attn_Wq, L0_attn_Wk, L0_attn_Wv, L0_attn_Wo,
           L0_ffn_W_base, L0_ffn_b_base, L0_ffn_W_router, L0_ffn_A, L0_ffn_B):
    x = np.asarray(L0_attn_x, np.float32)
    B, S, _ = x.shape
    shared = dict(
        wbaseT=np.ascontiguousarray(np.asarray(L0_ffn_W_base, np.float32).T),
        wqT=np.ascontiguousarray(np.asarray(L0_attn_Wq, np.float32).T),
        woT=np.ascontiguousarray(np.asarray(L0_attn_Wo, np.float32).T),
        wkT=np.ascontiguousarray(np.asarray(L0_attn_Wk, np.float32).T),
        wvT=np.ascontiguousarray(np.asarray(L0_attn_Wv, np.float32).T),
        aT=np.ascontiguousarray(np.asarray(L0_ffn_A, np.float32).reshape(128, D).T),
        bstk=np.ascontiguousarray(np.transpose(np.asarray(L0_ffn_B, np.float32), (0, 2, 1)).reshape(128, D)),
        wrT=np.ascontiguousarray(np.asarray(L0_ffn_W_router, np.float32).T),
        bbc=np.ascontiguousarray(np.asarray(L0_ffn_b_base, np.float32).reshape(NIF, 128).T),
        ones1=np.ones((1, 128), np.float32),
        onesc=np.ones((128, 1), np.float32),
        eye=np.eye(128, dtype=np.float32),
    )
    lin = np.linspace(0.0, 1.0, 256, dtype=np.float32)
    nc, ctx = build()
    in_maps = []
    for c in range(8):
        b, qc = c // 4, c % 4
        q0 = qc * OWN
        lo, hi = q0 - 256, q0 + OWN + 256
        xp = np.zeros((EXT, D), np.float32)
        slo, shi = max(0, lo), min(S, hi)
        xp[slo - lo:shi - lo] = x[b, slo:shi]
        pos = np.clip(np.arange(lo, hi), 0, S - 1)
        cos, sin = _rope_tables(pos)
        s2 = np.empty((DH, EXT), np.float32)
        s2[0:64] = -sin.T[0:64]
        s2[64:128] = sin.T[64:128]
        vm = np.zeros((128, NKC), np.float32)
        for k in range(NKC):
            p0 = lo + k * 128
            vm[:, k] = 1.0 if (p0 >= 0 and p0 + 128 <= S) else 0.0
        a1 = np.empty((128, 8), np.float32)
        for qch in range(8):
            a1[:, qch] = lin[np.arange(128) + (qch % 2) * 128]
        a0 = (np.float32(1.0) - a1).astype(np.float32)
        if qc == 0:
            a0[:, 0:2] = 0.0
            a1[:, 0:2] = 1.0
        m = dict(shared)
        m.update(xin=np.ascontiguousarray(xp.T),
                 cosx=np.ascontiguousarray(cos.T), s2x=s2,
                 a0=a0, a1=a1, vm=vm)
        in_maps.append(m)
    res = run_bass_kernel_spmd(nc, in_maps, core_ids=list(range(8)))
    out = np.empty((B, S, D), np.float32)
    for c in range(8):
        b, qc = c // 4, c % 4
        out[b, qc * OWN:(qc + 1) * OWN, :] = res.results[c]["out"].T
    ctx.close()
    return out


# version 08
# speedup vs baseline: 1.1994x; 1.1994x over previous
import numpy as np
import concourse.bass as bass
import concourse.mybir as mybir
from concourse.bass_utils import run_bass_kernel_spmd
from contextlib import ExitStack

F32 = mybir.dt.float32
BF16 = mybir.dt.bfloat16
AX = mybir.AxisListType
ALU = mybir.AluOpType
ACT = mybir.ActivationFunctionType

D = 4096
NIF = 32
H = 32
DH = 128
OWN = 1024
EXT = 1536
NKC = EXT // 128
SCALE = 1.0 / float(np.sqrt(128.0))
AW = 49152
ENGS = ("sync", "gpsimd", "scalar", "vector", "tensor")


STREAM_ENG = {"wload0": "sync", "wload1": "sync", "wmisc": "sync",
              "xload0": "gpsimd", "xload1": "gpsimd", "xstore": "gpsimd",
              "xsta": "gpsimd", "xstb": "gpsimd"}
SEMN = tuple(ENGS) + tuple(STREAM_ENG)


class Prog:
    def __init__(self):
        self.ops = {e: [] for e in ENGS}
        self.cnt = {s: 0 for s in SEMN}

    def emit(self, e, fn, waits=(), inc=0, isem=None):
        isem = isem or e
        self.ops[e].append((fn, tuple(waits), inc, isem))
        self.cnt[isem] += inc
        return self.cnt[isem]

    def dma(self, stream, out, in_, waits=(), accum=False):
        e = STREAM_ENG[stream]
        if accum:
            fn = lambda q, o=out, i=in_: q.dma_start(out=o, in_=i, accum_op=ALU.add)
        else:
            fn = lambda q, o=out, i=in_: q.dma_start(out=o, in_=i)
        return self.emit(e, fn, waits, 16, isem=stream)

    def drain(self):
        self.emit("vector", lambda v: v.drain(), (), 0)


class Arena:
    def __init__(self, t, off=0):
        self.t = t
        self.off = off

    def alloc(self, words):
        o = self.off
        self.off += words
        assert self.off <= AW, f"arena overflow {self.off} > {AW}"
        return o

    def f32(self, words, shape=None):
        o = self.alloc(words)
        ap = self.t[:, o:o + words]
        if shape is not None:
            ap = _resh(ap, shape)
        return ap

    def bf16(self, n, shape=None):
        o = self.alloc((n + 1) // 2)
        ap = self.t[:, o:o + (n + 1) // 2].bitcast(BF16)[:, 0:n]
        if shape is not None:
            ap = _resh(ap, shape)
        return ap

    def row(self, words):
        o = self.alloc(words)
        return self.t[0:1, o:o + words]


def _resh(ap, shape):
    names = [chr(97 + i) for i in range(len(shape))]
    return ap.rearrange("p (" + " ".join(names) + ") -> p " + " ".join(names),
                        **{n: s for n, s in zip(names, shape)})


def build():
    nc = bass.Bass("TRN2", target_bir_lowering=False)
    dt = lambda n, s, d=F32, k="ExternalInput": nc.dram_tensor(n, s, d, kind=k).ap()

    xin = dt("xin", [D, EXT])
    wbaseT = dt("wbaseT", [D, D])
    wqT = dt("wqT", [D, D])
    woT = dt("woT", [D, D])
    wkT = dt("wkT", [D, DH])
    wvT = dt("wvT", [D, DH])
    aT = dt("aT", [D, 128])
    bstk = dt("bstk", [128, D])
    wrT = dt("wrT", [D, 8])
    bbc = dt("bbc", [128, NIF])
    cosx_h = dt("cosx", [128, EXT])
    s2x_h = dt("s2x", [128, EXT])
    a0_h = dt("a0", [128, 8])
    a1_h = dt("a1", [128, 8])
    vm_h = dt("vm", [128, NKC])
    ones1_h = dt("ones1", [1, 128])
    onesc_h = dt("onesc", [128, 1])
    eye_h = dt("eye", [128, 128])
    out_h = dt("out", [D, OWN], F32, "ExternalOutput")
    x1h = dt("x1h", [D, EXT], F32, "Internal")
    x2h = dt("x2h", [D, OWN], F32, "Internal")

    xin_r = xin.rearrange("(c p) t -> p c t", p=128)
    x1h_r = x1h.rearrange("(c p) t -> p c t", p=128)
    x2h_r = x2h.rearrange("(c p) t -> p c t", p=128)
    out_r = out_h.rearrange("(c p) t -> p c t", p=128)
    wbaseT_r = wbaseT.rearrange("(c p) o -> p c o", p=128)
    wqT_r = wqT.rearrange("(c p) o -> p c o", p=128)
    woT_r = woT.rearrange("(c p) o -> p c o", p=128)
    wkT_r = wkT.rearrange("(c p) o -> p c o", p=128)
    wvT_r = wvT.rearrange("(c p) o -> p c o", p=128)
    aT_r = aT.rearrange("(c p) o -> p c o", p=128)
    wrT_r = wrT.rearrange("(c p) o -> p c o", p=128)

    ctx = ExitStack()
    E = ctx.enter_context
    arena_t = E(nc.sbuf_tensor("arena", [128, AW], F32))
    ar = Arena(arena_t)

    PS1 = E(nc.psum_tensor("PS1", [128, 1024], F32))
    PS2 = E(nc.psum_tensor("PS2", [128, 1024], F32))
    PS3 = E(nc.psum_tensor("PS3", [128, 1024], F32))
    PS4 = E(nc.psum_tensor("PS4", [128, 512], F32))
    PSM = E(nc.psum_tensor("PSM", [128, 512], F32))
    PS4b = PS4[:].bitcast(BF16)
    PS3v = PS3[:].rearrange("p (c t) -> p c t", c=4)

    sems = {e: E(nc.semaphore("s_" + e)) for e in SEMN}
    pr = Prog()

    def store_q():
        return [(s, pr.cnt[s]) for s in ("xstore", "xsta", "xstb") if pr.cnt[s]]

    xlnb = ar.bf16(NIF * EXT, (NIF, EXT))
    bbt = ar.f32(NIF)
    bbh2 = ar.f32(NIF)
    id32 = ar.f32(128)
    idb16 = ar.bf16(128)
    onesb = ar.bf16(2)
    vmb = ar.bf16(NKC)
    vmf = ar.f32(NKC)
    a0t = ar.f32(8)
    a1t = ar.f32(8)
    RO = ar.alloc(7 * 128)
    ones1 = arena_t[0:1, RO:RO + 128]
    murow = arena_t[0:1, RO + 128:RO + 256]
    msrow = arena_t[0:1, RO + 256:RO + 384]
    t1row = arena_t[0:1, RO + 384:RO + 512]
    varrow = arena_t[0:1, RO + 512:RO + 640]
    invsrow = arena_t[0:1, RO + 640:RO + 768]
    misrow = arena_t[0:1, RO + 768:RO + 896]
    PBASE = ar.off

    sa = Arena(arena_t, PBASE)
    eyetmp = sa.f32(128)
    onesctmp = sa.f32(1)
    c_eye = pr.dma("wmisc", eyetmp, eye_h)
    c_b = pr.dma("wmisc", bbt, bbc)
    c_vm = pr.dma("wmisc", vmf, vm_h)
    c_a0 = pr.dma("wmisc", a0t, a0_h)
    c_a1 = pr.dma("wmisc", a1t, a1_h)
    c_o1 = pr.dma("wmisc", ones1, ones1_h)
    c_oc = pr.dma("wmisc", onesctmp, onesc_h)
    c_id = pr.dma("wmisc", id32, eye_h)
    pr.emit("scalar", lambda s: s.activation(idb16, eyetmp, ACT.Copy), [("wmisc", c_id)], 1)
    pr.emit("scalar", lambda s: s.activation(onesb[:, 0:1], onesctmp, ACT.Copy), [("wmisc", c_id)], 1)
    setup_sc = pr.emit("scalar", lambda s: s.activation(vmb, vmf, ACT.Copy), [("wmisc", c_id)], 1)
    setup_ve = pr.emit("vector", lambda v: v.tensor_scalar(bbh2, bbt, 0.5, None, ALU.mult),
                       [("wmisc", c_id)], 1)

    def ln_prep(src_r, ntok, base, entry=()):
        a = Arena(arena_t, base)
        xf = [a.f32(NIF * 128, (NIF, 128)), a.f32(NIF * 128, (NIF, 128))]
        xb = [a.bf16(NIF * 256, (NIF, 256)), a.bf16(NIF * 256, (NIF, 256))]
        nch = ntok // 128
        apply2 = [0] * nch
        rankc = [0] * nch
        stats = [0] * nch
        rowrd = [0] * nch
        s1l = [0] * nch
        for i in range(nch):
            par = i % 2
            ps_st = [PSM[0:1, 0:256], PS1[0:1, 0:256]][par]
            ps_iv = [PS4[:, 0:128], PS2[:, 0:128]][par]
            ps_mi = [PSM[:, 0:128], PS1[:, 0:128]][par]
            wld = list(entry) if i == 0 else ([("vector", apply2[i - 2])] if i >= 2 else [])
            wld = wld + store_q()
            sem = "xload%d" % par
            g = pr.dma(sem, xf[par][:, 0:16, :], src_r[:, 0:16, i * 128:(i + 1) * 128], wld)
            g = pr.dma(sem, xf[par][:, 16:32, :], src_r[:, 16:32, i * 128:(i + 1) * 128], [])
            wsc = [(sem, g)]
            if i >= 2:
                wsc.append(("tensor", stats[i - 2]))
            pr.emit("scalar", lambda s, p=par: s.activation(xb[p][:, :, 0:128], xf[p], ACT.Copy), wsc, 1)
            s1l[i] = s1 = pr.emit("scalar", lambda s, p=par: s.activation(
                xb[p][:, :, 128:256], xf[p], ACT.Square), [], 1)
            wmm = [("scalar", s1)]
            if i >= 2:
                wmm.append(("vector", apply2[i - 2]))
            for c in range(NIF):
                stats[i] = pr.emit("tensor", lambda t, c=c, p=par, ps=ps_st: t.matmul(
                    ps, onesb[:, 0:1], xb[p][:, c, :],
                    start=(c == 0), stop=(c == NIF - 1)), wmm if c == 0 else [],
                    1 if c == NIF - 1 else 0)
            wv = [("tensor", stats[i])]
            if i >= 1:
                wv.append(("tensor", rankc[i - 1]))
            pr.emit("vector", lambda v, ps=ps_st: v.tensor_scalar(
                murow, ps[0:1, 0:128], 1.0 / D, None, ALU.mult), wv, 1)
            pr.emit("vector", lambda v, ps=ps_st: v.tensor_scalar(
                msrow, ps[0:1, 128:256], 1.0 / D, None, ALU.mult), [], 1)
            pr.drain()
            pr.emit("vector", lambda v: v.tensor_tensor(t1row, murow, murow, ALU.mult), [], 1)
            pr.drain()
            pr.emit("vector", lambda v: v.tensor_tensor(varrow, msrow, t1row, ALU.subtract), [], 1)
            pr.drain()
            rv = pr.emit("vector", lambda v: v.tensor_scalar(varrow, varrow, 1e-5, None, ALU.add), [], 1)
            si = pr.emit("scalar", lambda s: s.activation(t1row, varrow, ACT.Sqrt),
                         [("vector", rv)], 1)
            vr = pr.emit("vector", lambda v: v.reciprocal(invsrow, t1row), [("scalar", si)], 1)
            pr.drain()
            rowrd[i] = pr.emit("vector", lambda v: v.tensor_tensor(misrow, murow, invsrow, ALU.mult),
                               [], 1)
            pr.emit("tensor", lambda t, ps=ps_iv: t.matmul(ps, ones1, invsrow, start=True, stop=True),
                    [("vector", rowrd[i])], 1)
            rankc[i] = pr.emit("tensor", lambda t, ps=ps_mi: t.matmul(
                ps, ones1, misrow, start=True, stop=True), [], 1)
            pr.emit("vector", lambda v, p=par, ps=ps_iv: v.tensor_tensor(
                xf[p], xf[p], ps.unsqueeze(1).broadcast_to([128, NIF, 128]), ALU.mult),
                [("tensor", rankc[i])], 1)
            pr.drain()
            apply2[i] = pr.emit("vector", lambda v, p=par, i=i, ps=ps_mi: v.tensor_tensor(
                xlnb[:, :, i * 128:(i + 1) * 128], xf[p],
                ps.unsqueeze(1).broadcast_to([128, NIF, 128]), ALU.subtract), [], 1)
        return {"vector": apply2[nch - 1], "tensor": max(stats[nch - 1], rankc[nch - 1]),
                "scalar": s1l[nch - 1]}

    def make_stager(wstage):
        state = {"i": 0, "cast": {}}

        def load_cast(src_ap, dst_bf_ap, extra_dma_waits=(), extra_cast_waits=()):
            i = state["i"]
            state["i"] += 1
            wdma = list(extra_dma_waits)
            if i - 2 in state["cast"]:
                wdma.append(("scalar", state["cast"][i - 2]))
            sem = "wload%d" % (i % 2)
            g = pr.dma(sem, wstage[i % 2], src_ap, wdma)
            wc = [(sem, g)] + list(extra_cast_waits)
            c = pr.emit("scalar", lambda s, si=i % 2, d=dst_bf_ap: s.activation(d, wstage[si], ACT.Copy), wc, 1)
            state["cast"][i] = c
            return c
        return load_cast

    def ffn(src_r, dst_r, ntok):
        snap0 = [("scalar", pr.cnt["scalar"]), ("vector", pr.cnt["vector"]),
                 ("tensor", pr.cnt["tensor"])]
        a0a = Arena(arena_t, PBASE)
        ab16 = a0a.bf16(NIF * 128, (NIF, 128))
        bb16 = a0a.bf16(D)
        rb16 = a0a.bf16(NIF * 8, (NIF, 8))
        hwb = a0a.bf16(EXT)
        PREP_B = a0a.off
        st_a = Arena(arena_t, PREP_B)
        atmp = st_a.f32(NIF * 128, (NIF, 128))
        bstmp = st_a.f32(D)
        rtmp = st_a.f32(NIF * 8, (NIF, 8))
        c_a = pr.dma("wmisc", atmp, aT_r, snap0)
        c_bs = pr.dma("wmisc", bstmp, bstk, [])
        c_r = pr.dma("wmisc", rtmp, wrT_r, [])
        pr.emit("scalar", lambda s: s.activation(ab16[:], atmp, ACT.Copy), [("wmisc", c_r)] + snap0, 1)
        pr.emit("scalar", lambda s: s.activation(bb16[:], bstmp, ACT.Copy), [("wmisc", c_r)], 1)
        sm_cast = pr.emit("scalar", lambda s: s.activation(rb16[:], rtmp, ACT.Copy), [("wmisc", c_r)], 1)
        snap1 = [("scalar", sm_cast), ("vector", pr.cnt["vector"]), ("tensor", pr.cnt["tensor"])]
        pd = ln_prep(src_r, ntok, PREP_B, snap1)
        prep_v, prep_t = pd["vector"], pd["tensor"]
        a = Arena(arena_t, PREP_B)
        wstage = [a.f32(8 * 128, (8, 128)), a.f32(8 * 128, (8, 128))]
        wb = [a.bf16(NIF * 128, (NIF, 128)), a.bf16(NIF * 128, (NIF, 128))]
        sl_sizes = []
        t0 = 0
        while t0 < ntok:
            n = min(512, ntok - t0)
            sl_sizes.append((t0, n))
            t0 += n
        slabs = [[a.f32(n) for (s0, n) in sl_sizes] for _ in range(2)]
        gxb = a.f32(512)
        lg = arena_t[0:8, a.alloc(512) + 0:a.off]
        ltm = a.f32(32, (4, 8))
        gate = a.f32(32, (4, 8))
        eq1 = a.f32(32, (4, 8))
        eq2 = a.f32(32, (4, 8))
        mskb = a.f32(32)
        m1c = a.f32(4)
        m2c = a.f32(4)
        p1c = a.f32(4)
        p2c = a.f32(4)
        stage = make_stager(wstage)

        NTS = ntok // 512
        hw_cnt = 0
        hw_hist = [0] * (NTS + 2)
        se_prev = 0
        sg_prev = 0
        for ts in range(NTS):
            cols = slice(ts * 512, (ts + 1) * 512)
            hslot = PS3[:, (ts % 2) * 512:(ts % 2) * 512 + 512]
            wmm = [("vector", prep_v)]
            if ts >= 2:
                wmm.append(("vector", hw_hist[ts - 2]))
            for c in range(NIF):
                pr.emit("tensor", lambda t, c=c, h=hslot, cc=cols: t.matmul(
                    h, ab16[:, c, :], xlnb[:, c, cc], start=(c == 0), stop=(c == NIF - 1)),
                    wmm if c == 0 else [], 0)
            wrt = [("scalar", se_prev)] if ts > 0 else [("vector", prep_v)]
            h_cnt = 0
            for c in range(NIF):
                h_cnt = pr.emit("tensor", lambda t, c=c, cc=cols: t.matmul(
                    PSM[0:8, 0:512], rb16[:, c, :], xlnb[:, c, cc],
                    start=(c == 0), stop=(c == NIF - 1)), wrt if c == 0 else [],
                    1 if c == NIF - 1 else 0)
            se = pr.emit("scalar", lambda s: s.activation(lg, PSM[0:8, 0:512], ACT.Copy),
                         [("tensor", h_cnt)], 1)
            se_prev = se
            tp = 0
            for qc in range(4):
                tp = pr.emit("tensor", lambda t, qc=qc: t.transpose(
                    PS4[:, qc * 8:(qc + 1) * 8], lg[0:8, qc * 128:(qc + 1) * 128], id32[0:8, 0:8]),
                    ([("scalar", se), ("scalar", sg_prev)] if qc == 0 else []), 1 if qc == 3 else 0)
            se2 = pr.emit("scalar", lambda s: s.activation(
                ltm, _resh(PS4[:, 0:32], (4, 8)), ACT.Copy), [("tensor", tp)], 1)
            pr.emit("vector", lambda v: v.reduce_max(m1c, ltm, axis=AX.X), [("scalar", se2)], 1)
            pr.drain()
            for qc in range(4):
                pr.emit("vector", lambda v, qc=qc: v.tensor_scalar(
                    eq1[:, qc, :], ltm[:, qc, :], m1c[:, qc:qc + 1], None, ALU.is_equal), [], 1)
            pr.drain()
            pr.emit("vector", lambda v: v.scalar_tensor_tensor(
                mskb, eq1.rearrange("p c e -> p (c e)"), -30.0, ltm.rearrange("p c e -> p (c e)"),
                ALU.mult, ALU.add), [], 1)
            pr.drain()
            vm2 = pr.emit("vector", lambda v: v.reduce_max(m2c, _resh(mskb, (4, 8)), axis=AX.X), [], 1)
            sp = 0
            for qc in range(4):
                sp = pr.emit("scalar", lambda s, qc=qc: s.activation(
                    p1c[:, qc:qc + 1], m2c[:, qc:qc + 1], ACT.Sigmoid, bias=m1c[:, qc:qc + 1], scale=-1.0),
                    [("vector", vm2)] if qc == 0 else [], 1)
                sp = pr.emit("scalar", lambda s, qc=qc: s.activation(
                    p2c[:, qc:qc + 1], m1c[:, qc:qc + 1], ACT.Sigmoid, bias=m2c[:, qc:qc + 1], scale=-1.0),
                    [], 1)
            for qc in range(4):
                pr.emit("vector", lambda v, qc=qc: v.tensor_scalar(
                    gate[:, qc, :], eq1[:, qc, :], p1c[:, qc:qc + 1], 2.0, ALU.mult, ALU.mult),
                    [("scalar", sp)] if qc == 0 else [], 1)
            for qc in range(4):
                pr.emit("vector", lambda v, qc=qc: v.tensor_scalar(
                    eq2[:, qc, :], _resh(mskb, (4, 8))[:, qc, :], m2c[:, qc:qc + 1], None, ALU.is_equal),
                    [], 1)
            pr.drain()
            for qc in range(4):
                pr.emit("vector", lambda v, qc=qc: v.tensor_scalar(
                    eq2[:, qc, :], eq2[:, qc, :], p2c[:, qc:qc + 1], 2.0, ALU.mult, ALU.mult), [], 1)
            pr.drain()
            pr.emit("vector", lambda v: v.tensor_tensor(
                gate.rearrange("p c e -> p (c e)"), gate.rearrange("p c e -> p (c e)"),
                eq2.rearrange("p c e -> p (c e)"), ALU.add), [], 1)
            pr.drain()
            vgx = pr.emit("vector", lambda v: v.tensor_copy(
                _resh(gxb, (4, 8, 16)), gate.unsqueeze(3).broadcast_to([128, 4, 8, 16])), [], 1)
            tg = 0
            for qc in range(4):
                tg = pr.emit("tensor", lambda t, qc=qc: t.transpose(
                    PS4[:, qc * 128:(qc + 1) * 128], gxb[:, qc * 128:(qc + 1) * 128], id32),
                    [("vector", vgx), ("scalar", se2)] if qc == 0 else [], 1 if qc == 3 else 0)
            sg = pr.emit("scalar", lambda s: s.activation(gxb, PS4[:, 0:512], ACT.Copy),
                         [("tensor", tg)], 1)
            sg_prev = sg
            hw_hist[ts] = hw_cnt = pr.emit("vector", lambda v, h=hslot, cc=cols: v.tensor_tensor(
                hwb[:, cc], h, gxb, ALU.mult), [("scalar", sg)], 1)

        grp_cnt = {}
        evac_cnt = {}
        first_slab = True
        for of in range(NIF):
            extra = [("vector", prep_v)] if of == 0 else []
            ecast = [("tensor", prep_t)] if of == 0 else []
            if of >= 2:
                ecast = [("tensor", grp_cnt[(of - 2, len(sl_sizes) - 1)])]
            for s4 in range(4):
                stage(wbaseT_r[:, s4 * 8:(s4 + 1) * 8, of * 128:(of + 1) * 128],
                      wb[of % 2][:, s4 * 8:(s4 + 1) * 8, :],
                      extra if s4 == 0 else (), ecast if s4 == 0 else ())
            cast_done = pr.cnt["scalar"]
            xsem = "xload%d" % (of % 2)
            for tsl, (t0, n) in enumerate(sl_sizes):
                wld = [("vector", prep_v)] if first_slab else []
                first_slab = False
                wld = wld + store_q()
                gld = pr.dma(xsem, slabs[of % 2][tsl], src_r[:, of, t0:t0 + n], wld)
            for tsl, (t0, n) in enumerate(sl_sizes):
                gidx = of * len(sl_sizes) + tsl
                pspair = [PS1, PS2][(gidx // 2) % 2]
                off = (gidx % 2) * 512
                slot = pspair[:, off:off + n]
                wmm = [("scalar", cast_done), ("vector", hw_cnt)]
                if gidx >= 4:
                    wmm.append(("vector", evac_cnt[gidx - 4]))
                for c in range(NIF):
                    pr.emit("tensor", lambda t, c=c, sl=slot, of=of, t0=t0, n=n: t.matmul(
                        sl, wb[of % 2][:, c, :], xlnb[:, c, t0:t0 + n],
                        start=(c == 0), stop=False), wmm if c == 0 else [], 0)
                gmm = pr.emit("tensor", lambda t, sl=slot, of=of, t0=t0, n=n: t.matmul(
                    sl, bb16[:, of * 128:(of + 1) * 128], hwb[:, t0:t0 + n],
                    start=False, stop=True), [], 1)
                grp_cnt[(of, tsl)] = gmm
                pr.emit("vector", lambda v, of=of, tsl=tsl: v.tensor_scalar(
                    slabs[of % 2][tsl], slabs[of % 2][tsl], 1.0, bbh2[:, of:of + 1],
                    ALU.mult, ALU.add), [(xsem, gld)], 1)
                pr.drain()
                ev = pr.emit("vector", lambda v, of=of, tsl=tsl, sl=slot: v.scalar_tensor_tensor(
                    slabs[of % 2][tsl], sl, 0.5, slabs[of % 2][tsl],
                    ALU.mult, ALU.add), [("tensor", gmm)], 1)
                evac_cnt[gidx] = ev
                pr.dma("xstore", dst_r[:, of, t0:t0 + n], slabs[of % 2][tsl], [("vector", ev)])

    def attn():
        snap0 = [("scalar", pr.cnt["scalar"]), ("vector", pr.cnt["vector"]),
                 ("tensor", pr.cnt["tensor"])]
        pd = ln_prep(x1h_r, EXT, PBASE, snap0)
        prep_v, prep_t = pd["vector"], pd["tensor"]
        a = Arena(arena_t, PBASE)
        cosx = a.f32(EXT)
        s2x = a.f32(EXT)
        krb = a.bf16(EXT)
        vext = a.bf16(NKC * 132, (NKC, 132))
        wstage = [a.f32(4 * 128, (4, 128)), a.f32(4 * 128, (4, 128))]
        wqb = [a.bf16(NIF * 128, (NIF, 128)), a.bf16(NIF * 128, (NIF, 128))]
        qt1 = a.f32(1024)
        qt2 = a.f32(1024)
        qrb = [a.bf16(1024), a.bf16(1024)]
        vbst = a.bf16(1024)
        sgb = [a.bf16(4 * 512, (4, 512)), a.bf16(4 * 512, (4, 512))]
        wnr = [a.f32(4 * 129, (4, 129)), a.f32(4 * 129, (4, 129))]
        rca = [a.f32(4), a.f32(4)]
        wout8 = a.bf16(8 * 1024, (8, 1024))
        slabs = [a.f32(512), a.f32(512)]
        diag = a.f32(4 * 128, (4, 128))
        stage = make_stager(wstage)

        cl = pr.dma("wmisc", cosx, cosx_h, [("vector", prep_v)])
        cos_l = pr.dma("wmisc", s2x, s2x_h, [])
        pr.drain()
        pr.emit("vector", lambda v: v.memset(vext[:, :, 129:132], 0.0), [], 1)

        for s8 in range(8):
            stage(wkT_r[:, s8 * 4:(s8 + 1) * 4, :], wqb[0][:, s8 * 4:(s8 + 1) * 4, :],
                  [("vector", prep_v)] if s8 == 0 else (), [("tensor", prep_t)] if s8 == 0 else ())
        for s8 in range(8):
            stage(wvT_r[:, s8 * 4:(s8 + 1) * 4, :], wqb[1][:, s8 * 4:(s8 + 1) * 4, :])
        kv_cast = pr.cnt["scalar"]
        kvw = pr.cnt["wmisc"]

        kv_ev = 0
        vmm = 0
        kev = 0
        for grp in range(3):
            t0 = grp * 512
            n = 512
            cols = slice(t0, t0 + n)
            off = (grp % 2) * 512
            wmm = [("scalar", kv_cast), ("vector", prep_v)]
            if grp >= 1:
                wmm.append(("vector", kev))
            kmm = 0
            for c in range(NIF):
                kmm = pr.emit("tensor", lambda t, c=c, cc=cols, off=off: t.matmul(
                    PS1[:, off:off + 512], wqb[0][:, c, :], xlnb[:, c, cc],
                    start=(c == 0), stop=(c == NIF - 1)),
                    wmm if c == 0 else [], 1 if c == NIF - 1 else 0)
            for c in range(NIF):
                vmm = pr.emit("tensor", lambda t, c=c, cc=cols, off=off: t.matmul(
                    PS2[:, off:off + 512], wqb[1][:, c, :], xlnb[:, c, cc],
                    start=(c == 0), stop=(c == NIF - 1)),
                    [], 1 if c == NIF - 1 else 0)
            pr.emit("vector", lambda v, cc=cols, off=off: v.tensor_tensor(
                qt1[:, 0:512], PS1[:, off:off + 512], cosx[:, cc], ALU.mult),
                [("tensor", kmm), ("wmisc", kvw)], 1)
            pr.emit("vector", lambda v, cc=cols, off=off: v.tensor_tensor(
                qt2[0:64, 0:512], PS1[64:128, off:off + 512], s2x[0:64, cc], ALU.mult), [], 1)
            pr.emit("vector", lambda v, cc=cols, off=off: v.tensor_tensor(
                qt2[64:128, 0:512], PS1[0:64, off:off + 512], s2x[64:128, cc], ALU.mult), [], 1)
            pr.drain()
            kev = pr.emit("vector", lambda v, cc=cols: v.tensor_tensor(
                krb[:, cc], qt1[:, 0:512], qt2[:, 0:512], ALU.add), [], 1)
            sv = pr.emit("scalar", lambda s, off=off: s.activation(
                vbst[:, 0:512], PS2[:, off:off + 512], ACT.Copy), [("tensor", vmm)], 1)
            tv = 0
            for k2 in range(4):
                tv = pr.emit("tensor", lambda t, k2=k2: t.transpose(
                    PS4b[:, k2 * 128:(k2 + 1) * 128], vbst[:, k2 * 128:(k2 + 1) * 128],
                    idb16), [("scalar", sv), ("vector", kv_ev)] if k2 == 0 else [],
                    1 if k2 == 3 else 0)
            kcg0 = t0 // 128
            pr.emit("vector", lambda v, kcg0=kcg0: v.tensor_tensor(
                vext[:, kcg0:kcg0 + 4, 0:128], _resh(PS4b, (8, 128))[:, 0:4, :],
                vmb[:, kcg0:kcg0 + 4].unsqueeze(2).broadcast_to([128, 4, 128]), ALU.mult),
                [("tensor", tv)], 1)
            kv_ev = pr.emit("vector", lambda v, kcg0=kcg0: v.tensor_copy(
                vext[:, kcg0:kcg0 + 4, 128:129], vmb[:, kcg0:kcg0 + 4].unsqueeze(2)), [], 1)

        pc = pr.cnt["xstore"]
        for i2 in range(64):
            of, ts2 = i2 // 2, i2 % 2
            par = i2 % 2
            w = [("xstore", pc), ("vector", prep_v)] if i2 == 0 else ([("xstore", pr.cnt["xstore"])] if i2 >= 2 else [])
            gl = pr.dma("xload%d" % par, slabs[par],
                        x1h_r[:, of, 256 + ts2 * 512:256 + (ts2 + 1) * 512], w)
            pr.dma("xstore", x2h_r[:, of, ts2 * 512:(ts2 + 1) * 512], slabs[par],
                   [("xload%d" % par, gl)])
        precopy_done = pr.cnt["xstore"]

        qmm = [0] * (H + 2)
        rope = [0] * (H + 2)
        wqc = [0] * (H + 2)
        sig_cnt = 0
        sig2_prev = 0
        av_hist = {}
        wn_last = kv_ev
        woutev = 0
        womm_prev_grp = max(vmm, 1)
        wo_ev = {}
        wo_done = 0
        dg_guard = 0

        def stage_wq(h, extra=()):
            for s8 in range(8):
                ew = []
                if h < 2 and s8 == 0:
                    ew = [("tensor", vmm)]
                elif s8 == 0:
                    ew = [("tensor", qmm[h - 2])]
                ew = ew + (list(extra) if s8 == 0 else [])
                stage(wqT_r[:, s8 * 4:(s8 + 1) * 4, h * 128:(h + 1) * 128],
                      wqb[h % 2][:, s8 * 4:(s8 + 1) * 4, :], (), ew)
            wqc[h] = pr.cnt["scalar"]

        def make_qfeeder(h, extra_first=()):
            stepctr = {"n": 0}

            def feed(k):
                while k > 0 and stepctr["n"] < 64:
                    j = stepctr["n"]
                    hf, c = j // NIF, j % NIF
                    w = []
                    if j == 0:
                        w = [("scalar", wqc[h]), ("vector", prep_v)] + list(extra_first)
                        if h >= 1:
                            w.append(("vector", rope[h - 1]))
                        if h % 8 == 0 and h > 0:
                            w.append(("scalar", wo_done))
                        if h == 0:
                            w.append(("vector", kv_ev))
                    qmm[h] = pr.emit("tensor", lambda t, c=c, par=h % 2, hf=hf: t.matmul(
                        PS1[:, hf * 512:(hf + 1) * 512], wqb[par][:, c, :],
                        xlnb[:, c, 256 + hf * 512:768 + hf * 512],
                        start=(c == 0), stop=(c == NIF - 1)), w,
                        1 if (c == NIF - 1 and hf == 1) else 0)
                    stepctr["n"] += 1
                    k -= 1
            return feed

        stage_wq(0)
        feed = make_qfeeder(0)
        feed(64)
        for hp in range(4):
            womm_last = 0
            for hh in range(8):
                h = hp * 8 + hh
                par = h % 2
                pr.emit("vector", lambda v, par=par: v.tensor_tensor(
                    qt1, PS1[:, 0:1024], cosx[:, 256:1280], ALU.mult), [("tensor", qmm[h])], 1)
                pr.emit("vector", lambda v: v.tensor_tensor(qt2[0:64, :], PS1[64:128, 0:1024],
                                                            s2x[0:64, 256:1280], ALU.mult), [], 1)
                pr.emit("vector", lambda v: v.tensor_tensor(qt2[64:128, :], PS1[0:64, 0:1024],
                                                            s2x[64:128, 256:1280], ALU.mult), [], 1)
                pr.drain()
                rope[h] = pr.emit("vector", lambda v, par=par: v.tensor_tensor(
                    qrb[par], qt1, qt2, ALU.add), [], 1)
                if hh < 7:
                    stage_wq(h + 1)
                    feed = make_qfeeder(h + 1)
                else:
                    feed = lambda k: None
                for wi in range(5):
                    q0w = max(0, wi * 256 - 256)
                    q1w = min(1024, wi * 256 + 256)
                    qn = q1w - q0w
                    npp = qn // 128
                    kc0 = wi * 2
                    sp = wi % 2
                    w01 = [("vector", rope[h]), ("scalar", sig2_prev)]
                    for kc in (0, 1):
                        s01 = pr.emit("tensor", lambda t, kc=kc, q0w=q0w, qn=qn, kc0=kc0, par=par: t.matmul(
                            PS2[:, kc * 512:kc * 512 + qn], krb[:, (kc0 + kc) * 128:(kc0 + kc + 1) * 128],
                            qrb[par][:, q0w:q0w + qn], start=True, stop=True),
                            w01 if kc == 0 else [], 1 if kc == 1 else 0)
                    feed(6)
                    wsg1 = [("tensor", s01)]
                    if (h, wi - 2) in av_hist:
                        wsg1.append(("tensor", av_hist[(h, wi - 2)]))
                    elif wi < 2 and h >= 1:
                        wsg1.append(("tensor", av_hist[(h - 1, 4 - wi)]))
                    pr.emit("scalar", lambda s, qn=qn, sp=sp: s.activation(
                        sgb[sp][:, 0:2, 0:qn], PS2[:].rearrange("p (c t) -> p c t", c=2)[:, :, 0:qn],
                        ACT.Sigmoid, scale=SCALE), wsg1, 1)
                    sig1 = pr.cnt["scalar"]
                    for kc in (2, 3):
                        s23 = pr.emit("tensor", lambda t, kc=kc, q0w=q0w, qn=qn, kc0=kc0, par=par: t.matmul(
                            PS2[:, (kc - 2) * 512:(kc - 2) * 512 + qn],
                            krb[:, (kc0 + kc) * 128:(kc0 + kc + 1) * 128],
                            qrb[par][:, q0w:q0w + qn], start=True, stop=True),
                            [("scalar", sig1)] if kc == 2 else [], 1 if kc == 3 else 0)
                    feed(6)
                    sig_cnt = pr.emit("scalar", lambda s, qn=qn, sp=sp: s.activation(
                        sgb[sp][:, 2:4, 0:qn], PS2[:].rearrange("p (c t) -> p c t", c=2)[:, :, 0:qn],
                        ACT.Sigmoid, scale=SCALE), [("tensor", s23)], 1)
                    sig2_prev = sig_cnt
                    avm = 0
                    for p in range(npp):
                        for kc in range(4):
                            st = (p % 2 == 0 and kc == 0)
                            sp_ = (p % 2 == 1 and kc == 3)
                            avm = pr.emit("tensor", lambda t, p=p, kc=kc, kc0=kc0, st=st, sp_=sp_, sp=sp: t.matmul(
                                PS3[:, p * 256:p * 256 + 130], sgb[sp][:, kc, p * 128:(p + 1) * 128],
                                vext[:, kc0 + kc, 0:130], start=st, stop=sp_),
                                ([("scalar", sig_cnt), ("vector", wn_last)] if (p == 0 and kc == 0) else []),
                                1 if (p == npp - 1 and kc == 3) else 0)
                    av_hist[(h, wi)] = avm
                    feed(4)
                    wn_last = pr.emit("vector", lambda v, npp=npp, wi=wi: v.tensor_copy(
                        wnr[wi % 2][:, 0:npp, :], PS3v[:, 0:npp, 0:129]), [("tensor", avm)], 1)
                    pr.drain()
                    pr.emit("vector", lambda v, npp=npp, wi=wi: v.reciprocal(
                        rca[wi % 2][:, 0:npp], wnr[wi % 2][:, 0:npp, 128:129][:, :, 0]), [], 1)
                    pr.drain()
                    if wi >= 1:
                        b = wi - 1
                        sA = 0 if b == 0 else 2
                        dgc = 0
                        for j in range(2):
                            pr.emit("vector", lambda v, b=b, j=j, sA=sA: v.tensor_scalar(
                                diag[:, j, :], id32, a0t[:, 2 * b + j:2 * b + j + 1],
                                rca[b % 2][:, sA + j:sA + j + 1], ALU.mult, ALU.mult),
                                [("tensor", dg_guard)] if j == 0 else [], 1)
                            dgc = pr.emit("vector", lambda v, b=b, j=j, wi=wi: v.tensor_scalar(
                                diag[:, 2 + j, :], id32, a1t[:, 2 * b + j:2 * b + j + 1],
                                rca[wi % 2][:, j:j + 1], ALU.mult, ALU.mult), [], 1)
                        tw = 0
                        for j in range(2):
                            pr.emit("tensor", lambda t, b=b, j=j, sA=sA: t.matmul(
                                PS4[:, j * 128:(j + 1) * 128], wnr[b % 2][:, sA + j, 0:128],
                                diag[:, j, :], start=True, stop=False),
                                [("vector", dgc), ("scalar", woutev)] if j == 0 else [], 0)
                            tw = pr.emit("tensor", lambda t, b=b, j=j, wi=wi: t.matmul(
                                PS4[:, j * 128:(j + 1) * 128], wnr[wi % 2][:, j, 0:128],
                                diag[:, 2 + j, :], start=False, stop=True), [], 1)
                        dg_guard = tw
                        woutev = pr.emit("scalar", lambda s, hh=hh, b=b: s.activation(
                            wout8[:, hh, b * 256:(b + 1) * 256], PS4[:, 0:256], ACT.Copy),
                            [("tensor", tw), ("tensor", womm_prev_grp)], 1)
                feed(64)
            for of in range(NIF):
                for s2_ in range(2):
                    ew = []
                    if of == 0 and s2_ == 0:
                        ew = [("tensor", qmm[hp * 8 + 7])]
                    elif of >= 2 and s2_ == 0:
                        ew = [("tensor", wo_grp)]
                    stage(woT_r[:, hp * 8 + s2_ * 4:hp * 8 + (s2_ + 1) * 4, of * 128:(of + 1) * 128],
                          wqb[of % 2][:, s2_ * 4:(s2_ + 1) * 4, :], (), ew)
                wo_cast = pr.cnt["scalar"]
                for ts2 in range(2):
                    gidx = of * 2 + ts2
                    slot = [PS1, PS2][gidx % 2]
                    sli = gidx % 2
                    wmm = [("scalar", wo_cast), ("scalar", woutev), ("scalar", sig_cnt)]
                    if gidx == 0:
                        wmm.append(("vector", rope[hp * 8 + 7]))
                    if gidx >= 2:
                        wmm.append(("scalar", wo_ev[gidx - 2]))
                    elif hp > 0:
                        wmm.append(("scalar", wo_ev[62 + gidx]))
                    mmc = 0
                    for gg in range(8):
                        mmc = pr.emit("tensor", lambda t, gg=gg, of=of, ts2=ts2, sl=slot: t.matmul(
                            sl[:, 0:512], wqb[of % 2][:, gg, :],
                            wout8[:, gg, ts2 * 512:(ts2 + 1) * 512],
                            start=(gg == 0), stop=(gg == 7)), wmm if gg == 0 else [],
                            1 if gg == 7 else 0)
                    wo_grp = mmc
                    womm_last = mmc
                    xsem = ["xsta", "xstb"][gidx % 2]
                    ev = pr.emit("scalar", lambda s, sli=sli, sl=slot: s.activation(
                        slabs[sli], sl[:, 0:512], ACT.Copy),
                        [("tensor", mmc), (xsem, pr.cnt[xsem]), ("xstore", precopy_done)], 1)
                    wo_ev[gidx] = ev
                    pr.dma(xsem, x2h_r[:, of, ts2 * 512:(ts2 + 1) * 512], slabs[sli],
                           [("scalar", ev)], accum=True)
            womm_prev_grp = womm_last
            wo_done = pr.cnt["scalar"]
            wo_ev = {62: wo_ev[62], 63: wo_ev[63]} if 63 in wo_ev else wo_ev
            if hp < 3:
                stage_wq(hp * 8 + 8, extra=[("tensor", womm_last)])
                feed = make_qfeeder(hp * 8 + 8, extra_first=[("scalar", wo_ev[63])])
                feed(64)

    ffn(xin_r, x1h_r, EXT)
    attn()
    ffn(x2h_r, out_r, OWN)

    total_st = {s: pr.cnt[s] for s in ("xstore", "xsta", "xstb")}

    with nc.Block() as block:
        def mk(e):
            def run(eng):
                hw = {}
                for fn, waits, inc, isem in pr.ops[e]:
                    for (we, wv) in waits:
                        if we == e:
                            continue
                        if wv > 0 and hw.get(we, -1) < wv:
                            eng.wait_ge(sems[we], wv)
                            hw[we] = wv
                    ins = fn(eng)
                    if inc:
                        ins.then_inc(sems[isem], inc)
                if e == "gpsimd":
                    for s, v in total_st.items():
                        if v:
                            eng.wait_ge(sems[s], v)
            return run
        block.sync(mk("sync"))
        block.gpsimd(mk("gpsimd"))
        block.scalar(mk("scalar"))
        block.vector(mk("vector"))
        block.tensor(mk("tensor"))
    return nc, ctx


def _rope_tables(pos):
    inv = 1.0 / (10000.0 ** (np.arange(0, DH, 2, dtype=np.float32) / np.float32(DH)))
    f = np.outer(pos.astype(np.float32), inv)
    cos = np.zeros((len(pos), DH), dtype=np.float32)
    sin = np.zeros((len(pos), DH), dtype=np.float32)
    cos[:, 0::2] = np.cos(f); cos[:, 1::2] = np.cos(f)
    sin[:, 0::2] = np.sin(f); sin[:, 1::2] = np.sin(f)
    return cos, sin


def kernel(L0_attn_x, L0_attn_Wq, L0_attn_Wk, L0_attn_Wv, L0_attn_Wo,
           L0_ffn_W_base, L0_ffn_b_base, L0_ffn_W_router, L0_ffn_A, L0_ffn_B):
    x = np.asarray(L0_attn_x, np.float32)
    B, S, _ = x.shape
    shared = dict(
        wbaseT=np.ascontiguousarray(np.asarray(L0_ffn_W_base, np.float32).T),
        wqT=np.ascontiguousarray(np.asarray(L0_attn_Wq, np.float32).T),
        woT=np.ascontiguousarray(np.asarray(L0_attn_Wo, np.float32).T),
        wkT=np.ascontiguousarray(np.asarray(L0_attn_Wk, np.float32).T),
        wvT=np.ascontiguousarray(np.asarray(L0_attn_Wv, np.float32).T),
        aT=np.ascontiguousarray(np.asarray(L0_ffn_A, np.float32).reshape(128, D).T),
        bstk=np.ascontiguousarray(np.transpose(np.asarray(L0_ffn_B, np.float32), (0, 2, 1)).reshape(128, D)),
        wrT=np.ascontiguousarray(np.asarray(L0_ffn_W_router, np.float32).T),
        bbc=np.ascontiguousarray(np.asarray(L0_ffn_b_base, np.float32).reshape(NIF, 128).T),
        ones1=np.ones((1, 128), np.float32),
        onesc=np.ones((128, 1), np.float32),
        eye=np.eye(128, dtype=np.float32),
    )
    lin = np.linspace(0.0, 1.0, 256, dtype=np.float32)
    nc, ctx = build()
    in_maps = []
    for c in range(8):
        b, qc = c // 4, c % 4
        q0 = qc * OWN
        lo, hi = q0 - 256, q0 + OWN + 256
        xp = np.zeros((EXT, D), np.float32)
        slo, shi = max(0, lo), min(S, hi)
        xp[slo - lo:shi - lo] = x[b, slo:shi]
        pos = np.clip(np.arange(lo, hi), 0, S - 1)
        cos, sin = _rope_tables(pos)
        s2 = np.empty((DH, EXT), np.float32)
        s2[0:64] = -sin.T[0:64]
        s2[64:128] = sin.T[64:128]
        vm = np.zeros((128, NKC), np.float32)
        for k in range(NKC):
            p0 = lo + k * 128
            vm[:, k] = 1.0 if (p0 >= 0 and p0 + 128 <= S) else 0.0
        a1 = np.empty((128, 8), np.float32)
        for qch in range(8):
            a1[:, qch] = lin[np.arange(128) + (qch % 2) * 128]
        a0 = (np.float32(1.0) - a1).astype(np.float32)
        if qc == 0:
            a0[:, 0:2] = 0.0
            a1[:, 0:2] = 1.0
        m = dict(shared)
        m.update(xin=np.ascontiguousarray(xp.T),
                 cosx=np.ascontiguousarray(cos.T), s2x=s2,
                 a0=a0, a1=a1, vm=vm)
        in_maps.append(m)
    res = run_bass_kernel_spmd(nc, in_maps, core_ids=list(range(8)))
    out = np.empty((B, S, D), np.float32)
    for c in range(8):
        b, qc = c // 4, c % 4
        out[b, qc * OWN:(qc + 1) * OWN, :] = res.results[c]["out"].T
    ctx.close()
    return out


# version 09
# speedup vs baseline: 1.2264x; 1.0224x over previous
import numpy as np
import concourse.bass as bass
import concourse.mybir as mybir
from concourse.bass_utils import run_bass_kernel_spmd
from contextlib import ExitStack

F32 = mybir.dt.float32
BF16 = mybir.dt.bfloat16
AX = mybir.AxisListType
ALU = mybir.AluOpType
ACT = mybir.ActivationFunctionType

D = 4096
NIF = 32
H = 32
DH = 128
OWN = 1024
EXT = 1536
NKC = EXT // 128
SCALE = 1.0 / float(np.sqrt(128.0))
AW = 49152
ENGS = ("sync", "gpsimd", "scalar", "vector", "tensor")


STREAM_ENG = {"wload0": "sync", "wload1": "sync", "wload2": "sync", "wmisc": "sync",
              "xload0": "gpsimd", "xload1": "gpsimd", "xstore": "gpsimd",
              "xsta": "gpsimd", "xstb": "gpsimd"}
SEMN = tuple(ENGS) + tuple(STREAM_ENG)


class Prog:
    def __init__(self):
        self.ops = {e: [] for e in ENGS}
        self.cnt = {s: 0 for s in SEMN}

    def emit(self, e, fn, waits=(), inc=0, isem=None):
        isem = isem or e
        self.ops[e].append((fn, tuple(waits), inc, isem))
        self.cnt[isem] += inc
        return self.cnt[isem]

    def dma(self, stream, out, in_, waits=(), accum=False):
        e = STREAM_ENG[stream]
        if accum:
            fn = lambda q, o=out, i=in_: q.dma_start(out=o, in_=i, accum_op=ALU.add)
        else:
            fn = lambda q, o=out, i=in_: q.dma_start(out=o, in_=i)
        return self.emit(e, fn, waits, 16, isem=stream)

    def drain(self):
        self.emit("vector", lambda v: v.drain(), (), 0)


class Arena:
    def __init__(self, t, off=0):
        self.t = t
        self.off = off

    def alloc(self, words):
        o = self.off
        self.off += words
        assert self.off <= AW, f"arena overflow {self.off} > {AW}"
        return o

    def f32(self, words, shape=None):
        o = self.alloc(words)
        ap = self.t[:, o:o + words]
        if shape is not None:
            ap = _resh(ap, shape)
        return ap

    def bf16(self, n, shape=None):
        o = self.alloc((n + 1) // 2)
        ap = self.t[:, o:o + (n + 1) // 2].bitcast(BF16)[:, 0:n]
        if shape is not None:
            ap = _resh(ap, shape)
        return ap

    def row(self, words):
        o = self.alloc(words)
        return self.t[0:1, o:o + words]


def _resh(ap, shape):
    names = [chr(97 + i) for i in range(len(shape))]
    return ap.rearrange("p (" + " ".join(names) + ") -> p " + " ".join(names),
                        **{n: s for n, s in zip(names, shape)})


def build():
    nc = bass.Bass("TRN2", target_bir_lowering=False)
    dt = lambda n, s, d=F32, k="ExternalInput": nc.dram_tensor(n, s, d, kind=k).ap()

    xin = dt("xin", [D, EXT])
    wbaseT = dt("wbaseT", [D, D])
    wqT = dt("wqT", [D, D])
    woT = dt("woT", [D, D])
    wkT = dt("wkT", [D, DH])
    wvT = dt("wvT", [D, DH])
    aT = dt("aT", [D, 128])
    bstk = dt("bstk", [128, D])
    wrT = dt("wrT", [D, 8])
    bbc = dt("bbc", [128, NIF])
    cosx_h = dt("cosx", [128, EXT])
    s2x_h = dt("s2x", [128, EXT])
    a0_h = dt("a0", [128, 8])
    a1_h = dt("a1", [128, 8])
    vm_h = dt("vm", [128, NKC])
    ones1_h = dt("ones1", [1, 128])
    onesc_h = dt("onesc", [128, 1])
    eye_h = dt("eye", [128, 128])
    out_h = dt("out", [D, OWN], F32, "ExternalOutput")
    x1h = dt("x1h", [D, EXT], F32, "Internal")
    x2h = dt("x2h", [D, OWN], F32, "Internal")

    xin_r = xin.rearrange("(c p) t -> p c t", p=128)
    x1h_r = x1h.rearrange("(c p) t -> p c t", p=128)
    x2h_r = x2h.rearrange("(c p) t -> p c t", p=128)
    out_r = out_h.rearrange("(c p) t -> p c t", p=128)
    wbaseT_r = wbaseT.rearrange("(c p) o -> p c o", p=128)
    wqT_r = wqT.rearrange("(c p) o -> p c o", p=128)
    woT_r = woT.rearrange("(c p) o -> p c o", p=128)
    wkT_r = wkT.rearrange("(c p) o -> p c o", p=128)
    wvT_r = wvT.rearrange("(c p) o -> p c o", p=128)
    aT_r = aT.rearrange("(c p) o -> p c o", p=128)
    wrT_r = wrT.rearrange("(c p) o -> p c o", p=128)

    ctx = ExitStack()
    E = ctx.enter_context
    arena_t = E(nc.sbuf_tensor("arena", [128, AW], F32))
    ar = Arena(arena_t)

    PS1 = E(nc.psum_tensor("PS1", [128, 1024], F32))
    PS2 = E(nc.psum_tensor("PS2", [128, 1024], F32))
    PS3 = E(nc.psum_tensor("PS3", [128, 1024], F32))
    PS4 = E(nc.psum_tensor("PS4", [128, 512], F32))
    PSM = E(nc.psum_tensor("PSM", [128, 512], F32))
    PS4b = PS4[:].bitcast(BF16)
    PS3v = PS3[:].rearrange("p (c t) -> p c t", c=4)

    sems = {e: E(nc.semaphore("s_" + e)) for e in SEMN}
    pr = Prog()

    def store_q():
        return [(s, pr.cnt[s]) for s in ("xstore", "xsta", "xstb") if pr.cnt[s]]

    xlnb = ar.bf16(NIF * EXT, (NIF, EXT))
    bbt = ar.f32(NIF)
    bbh2 = ar.f32(NIF)
    id32 = ar.f32(128)
    idb16 = ar.bf16(128)
    onesb = ar.bf16(2)
    vmb = ar.bf16(NKC)
    vmf = ar.f32(NKC)
    a0t = ar.f32(8)
    a1t = ar.f32(8)
    RO = ar.alloc(7 * 128)
    ones1 = arena_t[0:1, RO:RO + 128]
    murow = arena_t[0:1, RO + 128:RO + 256]
    msrow = arena_t[0:1, RO + 256:RO + 384]
    t1row = arena_t[0:1, RO + 384:RO + 512]
    varrow = arena_t[0:1, RO + 512:RO + 640]
    invsrow = arena_t[0:1, RO + 640:RO + 768]
    misrow = arena_t[0:1, RO + 768:RO + 896]
    PBASE = ar.off

    sa = Arena(arena_t, PBASE)
    eyetmp = sa.f32(128)
    onesctmp = sa.f32(1)
    c_eye = pr.dma("wmisc", eyetmp, eye_h)
    c_b = pr.dma("wmisc", bbt, bbc)
    c_vm = pr.dma("wmisc", vmf, vm_h)
    c_a0 = pr.dma("wmisc", a0t, a0_h)
    c_a1 = pr.dma("wmisc", a1t, a1_h)
    c_o1 = pr.dma("wmisc", ones1, ones1_h)
    c_oc = pr.dma("wmisc", onesctmp, onesc_h)
    c_id = pr.dma("wmisc", id32, eye_h)
    pr.emit("scalar", lambda s: s.activation(idb16, eyetmp, ACT.Copy), [("wmisc", c_id)], 1)
    pr.emit("scalar", lambda s: s.activation(onesb[:, 0:1], onesctmp, ACT.Copy), [("wmisc", c_id)], 1)
    setup_sc = pr.emit("scalar", lambda s: s.activation(vmb, vmf, ACT.Copy), [("wmisc", c_id)], 1)
    setup_ve = pr.emit("vector", lambda v: v.tensor_scalar(bbh2, bbt, 0.5, None, ALU.mult),
                       [("wmisc", c_id)], 1)

    def ln_prep(src_r, ntok, base, entry=(), hook=None):
        a = Arena(arena_t, base)
        xf = [a.f32(NIF * 128, (NIF, 128)), a.f32(NIF * 128, (NIF, 128))]
        xb = [a.bf16(NIF * 256, (NIF, 256)), a.bf16(NIF * 256, (NIF, 256))]
        nch = ntok // 128
        apply2 = [0] * nch
        rankc = [0] * nch
        stats = [0] * nch
        rowrd = [0] * nch
        s1l = [0] * nch
        for i in range(nch):
            par = i % 2
            ps_st = [PSM[0:1, 0:256], PS1[0:1, 0:256]][par]
            ps_iv = [PS4[:, 0:128], PS2[:, 0:128]][par]
            ps_mi = [PSM[:, 0:128], PS1[:, 0:128]][par]
            wld = list(entry) if i == 0 else ([("vector", apply2[i - 2])] if i >= 2 else [])
            wld = wld + store_q()
            sem = "xload%d" % par
            g = pr.dma(sem, xf[par][:, 0:16, :], src_r[:, 0:16, i * 128:(i + 1) * 128], wld)
            g = pr.dma(sem, xf[par][:, 16:32, :], src_r[:, 16:32, i * 128:(i + 1) * 128], [])
            wsc = [(sem, g)]
            if i >= 2:
                wsc.append(("tensor", stats[i - 2]))
            pr.emit("scalar", lambda s, p=par: s.activation(xb[p][:, :, 0:128], xf[p], ACT.Copy), wsc, 1)
            s1l[i] = s1 = pr.emit("scalar", lambda s, p=par: s.activation(
                xb[p][:, :, 128:256], xf[p], ACT.Square), [], 1)
            wmm = [("scalar", s1)]
            if i >= 2:
                wmm.append(("vector", apply2[i - 2]))
            for c in range(NIF):
                stats[i] = pr.emit("tensor", lambda t, c=c, p=par, ps=ps_st: t.matmul(
                    ps, onesb[:, 0:1], xb[p][:, c, :],
                    start=(c == 0), stop=(c == NIF - 1)), wmm if c == 0 else [],
                    1 if c == NIF - 1 else 0)
            wv = [("tensor", stats[i])]
            if i >= 1:
                wv.append(("tensor", rankc[i - 1]))
            pr.emit("vector", lambda v, ps=ps_st: v.tensor_scalar(
                murow, ps[0:1, 0:128], 1.0 / D, None, ALU.mult), wv, 1)
            pr.emit("vector", lambda v, ps=ps_st: v.tensor_scalar(
                msrow, ps[0:1, 128:256], 1.0 / D, None, ALU.mult), [], 1)
            pr.drain()
            pr.emit("vector", lambda v: v.tensor_tensor(t1row, murow, murow, ALU.mult), [], 1)
            pr.drain()
            pr.emit("vector", lambda v: v.tensor_tensor(varrow, msrow, t1row, ALU.subtract), [], 1)
            pr.drain()
            rv = pr.emit("vector", lambda v: v.tensor_scalar(varrow, varrow, 1e-5, None, ALU.add), [], 1)
            si = pr.emit("scalar", lambda s: s.activation(t1row, varrow, ACT.Sqrt),
                         [("vector", rv)], 1)
            vr = pr.emit("vector", lambda v: v.reciprocal(invsrow, t1row), [("scalar", si)], 1)
            pr.drain()
            rowrd[i] = pr.emit("vector", lambda v: v.tensor_tensor(misrow, murow, invsrow, ALU.mult),
                               [], 1)
            pr.emit("tensor", lambda t, ps=ps_iv: t.matmul(ps, ones1, invsrow, start=True, stop=True),
                    [("vector", rowrd[i])], 1)
            rankc[i] = pr.emit("tensor", lambda t, ps=ps_mi: t.matmul(
                ps, ones1, misrow, start=True, stop=True), [], 1)
            pr.emit("vector", lambda v, p=par, ps=ps_iv: v.tensor_tensor(
                xf[p], xf[p], ps.unsqueeze(1).broadcast_to([128, NIF, 128]), ALU.mult),
                [("tensor", rankc[i])], 1)
            pr.drain()
            apply2[i] = pr.emit("vector", lambda v, p=par, i=i, ps=ps_mi: v.tensor_tensor(
                xlnb[:, :, i * 128:(i + 1) * 128], xf[p],
                ps.unsqueeze(1).broadcast_to([128, NIF, 128]), ALU.subtract), [], 1)
            if hook is not None:
                hook(i, apply2)
        return {"vector": apply2[nch - 1], "tensor": max(stats[nch - 1], rankc[nch - 1]),
                "scalar": s1l[nch - 1]}

    def make_stager(wstage):
        nb = len(wstage)
        state = {"i": 0, "cast": {}}

        def load_cast(src_ap, dst_bf_ap, extra_dma_waits=(), extra_cast_waits=()):
            i = state["i"]
            state["i"] += 1
            wdma = list(extra_dma_waits)
            if i - nb in state["cast"]:
                wdma.append(("scalar", state["cast"][i - nb]))
            sem = "wload%d" % (i % nb)
            g = pr.dma(sem, wstage[i % nb], src_ap, wdma)
            wc = [(sem, g)] + list(extra_cast_waits)
            c = pr.emit("scalar", lambda s, si=i % nb, d=dst_bf_ap: s.activation(d, wstage[si], ACT.Copy), wc, 1)
            state["cast"][i] = c
            return c
        return load_cast

    def ffn(src_r, dst_r, ntok):
        snap0 = [("scalar", pr.cnt["scalar"]), ("vector", pr.cnt["vector"]),
                 ("tensor", pr.cnt["tensor"])]
        a0a = Arena(arena_t, PBASE)
        ab16 = a0a.bf16(NIF * 128, (NIF, 128))
        bb16 = a0a.bf16(D)
        rb16 = a0a.bf16(NIF * 8, (NIF, 8))
        hwb = a0a.bf16(EXT)
        PREP_B = a0a.off
        st_a = Arena(arena_t, PREP_B)
        atmp = st_a.f32(NIF * 128, (NIF, 128))
        bstmp = st_a.f32(D)
        rtmp = st_a.f32(NIF * 8, (NIF, 8))
        c_a = pr.dma("wmisc", atmp, aT_r, snap0)
        c_bs = pr.dma("wmisc", bstmp, bstk, [])
        c_r = pr.dma("wmisc", rtmp, wrT_r, [])
        pr.emit("scalar", lambda s: s.activation(ab16[:], atmp, ACT.Copy), [("wmisc", c_r)] + snap0, 1)
        pr.emit("scalar", lambda s: s.activation(bb16[:], bstmp, ACT.Copy), [("wmisc", c_r)], 1)
        sm_cast = pr.emit("scalar", lambda s: s.activation(rb16[:], rtmp, ACT.Copy), [("wmisc", c_r)], 1)
        snap1 = [("scalar", sm_cast), ("vector", pr.cnt["vector"]), ("tensor", pr.cnt["tensor"])]
        ga = Arena(arena_t, PREP_B)
        gxb = ga.f32(512)
        lg = arena_t[0:8, ga.alloc(512) + 0:ga.off]
        ltm = ga.f32(32, (4, 8))
        gate = ga.f32(32, (4, 8))
        eq1 = ga.f32(32, (4, 8))
        eq2 = ga.f32(32, (4, 8))
        mskb = ga.f32(32)
        m1c = ga.f32(4)
        m2c = ga.f32(4)
        p1c = ga.f32(4)
        p2c = ga.f32(4)
        OFL_B = ga.off

        NTS = ntok // 512
        st_aph = {"hw": 0, "hw_hist": [0] * (NTS + 2), "se": 0, "sg": 0}

        def aphase(ts, wait_v):
            cols = slice(ts * 512, (ts + 1) * 512)
            hslot = PS3[:, (ts % 2) * 512:(ts % 2) * 512 + 512]
            wmm = [("vector", wait_v), ("scalar", sm_cast)]
            if ts >= 2:
                wmm.append(("vector", st_aph["hw_hist"][ts - 2]))
            for c in range(NIF):
                pr.emit("tensor", lambda t, c=c, h=hslot, cc=cols: t.matmul(
                    h, ab16[:, c, :], xlnb[:, c, cc], start=(c == 0), stop=(c == NIF - 1)),
                    wmm if c == 0 else [], 0)
            wrt = [("scalar", st_aph["se"])] if ts > 0 else [("vector", wait_v)]
            h_cnt = 0
            for c in range(NIF):
                h_cnt = pr.emit("tensor", lambda t, c=c, cc=cols: t.matmul(
                    PS1[0:8, 512:1024], rb16[:, c, :], xlnb[:, c, cc],
                    start=(c == 0), stop=(c == NIF - 1)), wrt if c == 0 else [],
                    1 if c == NIF - 1 else 0)
            se = pr.emit("scalar", lambda s: s.activation(lg, PS1[0:8, 512:1024], ACT.Copy),
                         [("tensor", h_cnt)], 1)
            st_aph["se"] = se
            tp = 0
            for qc in range(4):
                tp = pr.emit("tensor", lambda t, qc=qc: t.transpose(
                    PS2[:, 512 + qc * 8:512 + (qc + 1) * 8], lg[0:8, qc * 128:(qc + 1) * 128],
                    id32[0:8, 0:8]),
                    ([("scalar", se), ("scalar", st_aph["sg"])] if qc == 0 else []),
                    1 if qc == 3 else 0)
            se2 = pr.emit("scalar", lambda s: s.activation(
                ltm, _resh(PS2[:, 512:544], (4, 8)), ACT.Copy), [("tensor", tp)], 1)
            pr.emit("vector", lambda v: v.reduce_max(m1c, ltm, axis=AX.X), [("scalar", se2)], 1)
            pr.drain()
            for qc in range(4):
                pr.emit("vector", lambda v, qc=qc: v.tensor_scalar(
                    eq1[:, qc, :], ltm[:, qc, :], m1c[:, qc:qc + 1], None, ALU.is_equal), [], 1)
            pr.drain()
            pr.emit("vector", lambda v: v.scalar_tensor_tensor(
                mskb, eq1.rearrange("p c e -> p (c e)"), -30.0, ltm.rearrange("p c e -> p (c e)"),
                ALU.mult, ALU.add), [], 1)
            pr.drain()
            vm2 = pr.emit("vector", lambda v: v.reduce_max(m2c, _resh(mskb, (4, 8)), axis=AX.X), [], 1)
            sp = 0
            for qc in range(4):
                sp = pr.emit("scalar", lambda s, qc=qc: s.activation(
                    p1c[:, qc:qc + 1], m2c[:, qc:qc + 1], ACT.Sigmoid, bias=m1c[:, qc:qc + 1], scale=-1.0),
                    [("vector", vm2)] if qc == 0 else [], 1)
                sp = pr.emit("scalar", lambda s, qc=qc: s.activation(
                    p2c[:, qc:qc + 1], m1c[:, qc:qc + 1], ACT.Sigmoid, bias=m2c[:, qc:qc + 1], scale=-1.0),
                    [], 1)
            for qc in range(4):
                pr.emit("vector", lambda v, qc=qc: v.tensor_scalar(
                    gate[:, qc, :], eq1[:, qc, :], p1c[:, qc:qc + 1], 2.0, ALU.mult, ALU.mult),
                    [("scalar", sp)] if qc == 0 else [], 1)
            for qc in range(4):
                pr.emit("vector", lambda v, qc=qc: v.tensor_scalar(
                    eq2[:, qc, :], _resh(mskb, (4, 8))[:, qc, :], m2c[:, qc:qc + 1], None, ALU.is_equal),
                    [], 1)
            pr.drain()
            for qc in range(4):
                pr.emit("vector", lambda v, qc=qc: v.tensor_scalar(
                    eq2[:, qc, :], eq2[:, qc, :], p2c[:, qc:qc + 1], 2.0, ALU.mult, ALU.mult), [], 1)
            pr.drain()
            pr.emit("vector", lambda v: v.tensor_tensor(
                gate.rearrange("p c e -> p (c e)"), gate.rearrange("p c e -> p (c e)"),
                eq2.rearrange("p c e -> p (c e)"), ALU.add), [], 1)
            pr.drain()
            vgx = pr.emit("vector", lambda v: v.tensor_copy(
                _resh(gxb, (4, 8, 16)), gate.unsqueeze(3).broadcast_to([128, 4, 8, 16])), [], 1)
            tg = 0
            for qc in range(4):
                tg = pr.emit("tensor", lambda t, qc=qc: t.transpose(
                    PS2[:, 512 + qc * 128:512 + (qc + 1) * 128], gxb[:, qc * 128:(qc + 1) * 128], id32),
                    [("vector", vgx), ("scalar", se2)] if qc == 0 else [], 1 if qc == 3 else 0)
            sg = pr.emit("scalar", lambda s: s.activation(gxb, PS2[:, 512:1024], ACT.Copy),
                         [("tensor", tg)], 1)
            st_aph["sg"] = sg
            st_aph["hw_hist"][ts] = st_aph["hw"] = pr.emit(
                "vector", lambda v, h=hslot, cc=cols: v.tensor_tensor(
                    hwb[:, cc], h, gxb, ALU.mult), [("scalar", sg)], 1)

        def hook(i, ap2):
            if i % 4 == 3 and (i // 4) < NTS:
                aphase(i // 4, ap2[i])

        pd = ln_prep(src_r, ntok, OFL_B, snap1, hook=hook)
        prep_v, prep_t = pd["vector"], pd["tensor"]
        hw_cnt = st_aph["hw"]
        a = Arena(arena_t, OFL_B)
        wstage = [a.f32(8 * 128, (8, 128)) for _ in range(3)]
        wb = [a.bf16(NIF * 128, (NIF, 128)), a.bf16(NIF * 128, (NIF, 128))]
        sl_sizes = []
        t0 = 0
        while t0 < ntok:
            n = min(512, ntok - t0)
            sl_sizes.append((t0, n))
            t0 += n
        slabs = [[a.f32(n) for (s0, n) in sl_sizes] for _ in range(2)]
        stage = make_stager(wstage)

        grp_cnt = {}
        evac_cnt = {}
        first_slab = True
        for of in range(NIF):
            extra = [("vector", prep_v)] if of == 0 else []
            ecast = [("tensor", prep_t)] if of == 0 else []
            if of >= 2:
                ecast = [("tensor", grp_cnt[(of - 2, len(sl_sizes) - 1)])]
            for s4 in range(4):
                stage(wbaseT_r[:, s4 * 8:(s4 + 1) * 8, of * 128:(of + 1) * 128],
                      wb[of % 2][:, s4 * 8:(s4 + 1) * 8, :],
                      extra if s4 == 0 else (), ecast if s4 == 0 else ())
            cast_done = pr.cnt["scalar"]
            xsem = "xload%d" % (of % 2)
            for tsl, (t0, n) in enumerate(sl_sizes):
                wld = [("vector", prep_v)] if first_slab else []
                first_slab = False
                wld = wld + store_q()
                gld = pr.dma(xsem, slabs[of % 2][tsl], src_r[:, of, t0:t0 + n], wld)
            for tsl, (t0, n) in enumerate(sl_sizes):
                gidx = of * len(sl_sizes) + tsl
                pspair = [PS1, PS2][(gidx // 2) % 2]
                off = (gidx % 2) * 512
                slot = pspair[:, off:off + n]
                wmm = [("scalar", cast_done), ("vector", hw_cnt)]
                if gidx >= 4:
                    wmm.append(("vector", evac_cnt[gidx - 4]))
                for c in range(NIF):
                    pr.emit("tensor", lambda t, c=c, sl=slot, of=of, t0=t0, n=n: t.matmul(
                        sl, wb[of % 2][:, c, :], xlnb[:, c, t0:t0 + n],
                        start=(c == 0), stop=False), wmm if c == 0 else [], 0)
                gmm = pr.emit("tensor", lambda t, sl=slot, of=of, t0=t0, n=n: t.matmul(
                    sl, bb16[:, of * 128:(of + 1) * 128], hwb[:, t0:t0 + n],
                    start=False, stop=True), [], 1)
                grp_cnt[(of, tsl)] = gmm
                pr.emit("vector", lambda v, of=of, tsl=tsl: v.tensor_scalar(
                    slabs[of % 2][tsl], slabs[of % 2][tsl], 1.0, bbh2[:, of:of + 1],
                    ALU.mult, ALU.add), [(xsem, gld)], 1)
                pr.drain()
                ev = pr.emit("vector", lambda v, of=of, tsl=tsl, sl=slot: v.scalar_tensor_tensor(
                    slabs[of % 2][tsl], sl, 0.5, slabs[of % 2][tsl],
                    ALU.mult, ALU.add), [("tensor", gmm)], 1)
                evac_cnt[gidx] = ev
                pr.dma("xstore", dst_r[:, of, t0:t0 + n], slabs[of % 2][tsl], [("vector", ev)])

    def attn():
        snap0 = [("scalar", pr.cnt["scalar"]), ("vector", pr.cnt["vector"]),
                 ("tensor", pr.cnt["tensor"])]
        pd = ln_prep(x1h_r, EXT, PBASE, snap0)
        prep_v, prep_t = pd["vector"], pd["tensor"]
        a = Arena(arena_t, PBASE)
        cosx = a.f32(EXT)
        s2x = a.f32(EXT)
        krb = a.bf16(EXT)
        vext = a.bf16(NKC * 132, (NKC, 132))
        wstage = [a.f32(4 * 128, (4, 128)) for _ in range(3)]
        wqb = [a.bf16(NIF * 128, (NIF, 128)), a.bf16(NIF * 128, (NIF, 128))]
        qt1 = a.f32(1024)
        qt2 = a.f32(1024)
        qrb = [a.bf16(1024), a.bf16(1024)]
        vbst = a.bf16(1024)
        sgb = [a.bf16(4 * 512, (4, 512)), a.bf16(4 * 512, (4, 512))]
        wnr = [a.f32(4 * 129, (4, 129)), a.f32(4 * 129, (4, 129))]
        rca = [a.f32(4), a.f32(4)]
        wout8 = a.bf16(8 * 1024, (8, 1024))
        slabs = [a.f32(512), a.f32(512)]
        diag = a.f32(4 * 128, (4, 128))
        stage = make_stager(wstage)

        cl = pr.dma("wmisc", cosx, cosx_h, [("vector", prep_v)])
        cos_l = pr.dma("wmisc", s2x, s2x_h, [])
        pr.drain()
        pr.emit("vector", lambda v: v.memset(vext[:, :, 129:132], 0.0), [], 1)

        for s8 in range(8):
            stage(wkT_r[:, s8 * 4:(s8 + 1) * 4, :], wqb[0][:, s8 * 4:(s8 + 1) * 4, :],
                  [("vector", prep_v)] if s8 == 0 else (), [("tensor", prep_t)] if s8 == 0 else ())
        for s8 in range(8):
            stage(wvT_r[:, s8 * 4:(s8 + 1) * 4, :], wqb[1][:, s8 * 4:(s8 + 1) * 4, :])
        kv_cast = pr.cnt["scalar"]
        kvw = pr.cnt["wmisc"]

        kv_ev = 0
        vmm = 0
        kev = 0
        for grp in range(3):
            t0 = grp * 512
            n = 512
            cols = slice(t0, t0 + n)
            off = (grp % 2) * 512
            wmm = [("scalar", kv_cast), ("vector", prep_v)]
            if grp >= 1:
                wmm.append(("vector", kev))
            kmm = 0
            for c in range(NIF):
                kmm = pr.emit("tensor", lambda t, c=c, cc=cols, off=off: t.matmul(
                    PS1[:, off:off + 512], wqb[0][:, c, :], xlnb[:, c, cc],
                    start=(c == 0), stop=(c == NIF - 1)),
                    wmm if c == 0 else [], 1 if c == NIF - 1 else 0)
            for c in range(NIF):
                vmm = pr.emit("tensor", lambda t, c=c, cc=cols, off=off: t.matmul(
                    PS2[:, off:off + 512], wqb[1][:, c, :], xlnb[:, c, cc],
                    start=(c == 0), stop=(c == NIF - 1)),
                    [], 1 if c == NIF - 1 else 0)
            pr.emit("vector", lambda v, cc=cols, off=off: v.tensor_tensor(
                qt1[:, 0:512], PS1[:, off:off + 512], cosx[:, cc], ALU.mult),
                [("tensor", kmm), ("wmisc", kvw)], 1)
            pr.emit("vector", lambda v, cc=cols, off=off: v.tensor_tensor(
                qt2[0:64, 0:512], PS1[64:128, off:off + 512], s2x[0:64, cc], ALU.mult), [], 1)
            pr.emit("vector", lambda v, cc=cols, off=off: v.tensor_tensor(
                qt2[64:128, 0:512], PS1[0:64, off:off + 512], s2x[64:128, cc], ALU.mult), [], 1)
            pr.drain()
            kev = pr.emit("vector", lambda v, cc=cols: v.tensor_tensor(
                krb[:, cc], qt1[:, 0:512], qt2[:, 0:512], ALU.add), [], 1)
            sv = pr.emit("scalar", lambda s, off=off: s.activation(
                vbst[:, 0:512], PS2[:, off:off + 512], ACT.Copy), [("tensor", vmm)], 1)
            tv = 0
            for k2 in range(4):
                tv = pr.emit("tensor", lambda t, k2=k2: t.transpose(
                    PS4b[:, k2 * 128:(k2 + 1) * 128], vbst[:, k2 * 128:(k2 + 1) * 128],
                    idb16), [("scalar", sv), ("vector", kv_ev)] if k2 == 0 else [],
                    1 if k2 == 3 else 0)
            kcg0 = t0 // 128
            pr.emit("vector", lambda v, kcg0=kcg0: v.tensor_tensor(
                vext[:, kcg0:kcg0 + 4, 0:128], _resh(PS4b, (8, 128))[:, 0:4, :],
                vmb[:, kcg0:kcg0 + 4].unsqueeze(2).broadcast_to([128, 4, 128]), ALU.mult),
                [("tensor", tv)], 1)
            kv_ev = pr.emit("vector", lambda v, kcg0=kcg0: v.tensor_copy(
                vext[:, kcg0:kcg0 + 4, 128:129], vmb[:, kcg0:kcg0 + 4].unsqueeze(2)), [], 1)

        pc = pr.cnt["xstore"]
        for i2 in range(64):
            of, ts2 = i2 // 2, i2 % 2
            par = i2 % 2
            w = [("xstore", pc), ("vector", prep_v)] if i2 == 0 else ([("xstore", pr.cnt["xstore"])] if i2 >= 2 else [])
            gl = pr.dma("xload%d" % par, slabs[par],
                        x1h_r[:, of, 256 + ts2 * 512:256 + (ts2 + 1) * 512], w)
            pr.dma("xstore", x2h_r[:, of, ts2 * 512:(ts2 + 1) * 512], slabs[par],
                   [("xload%d" % par, gl)])
        precopy_done = pr.cnt["xstore"]

        qmm = [0] * (H + 2)
        rope = [0] * (H + 2)
        wqc = [0] * (H + 2)
        sig_cnt = 0
        sig2_prev = 0
        av_hist = {}
        wn_last = kv_ev
        woutev = 0
        womm_prev_grp = max(vmm, 1)
        wo_ev = {}
        wo_done = 0
        dg_guard = 0

        wq_sub = {}

        def stage_wq(h, extra=()):
            subs = []
            for s8 in range(8):
                ew = []
                if h < 2 and s8 == 0:
                    ew = [("tensor", vmm)]
                elif s8 == 0:
                    ew = [("tensor", qmm[h - 2])]
                ew = ew + (list(extra) if s8 == 0 else [])
                subs.append(stage(wqT_r[:, s8 * 4:(s8 + 1) * 4, h * 128:(h + 1) * 128],
                            wqb[h % 2][:, s8 * 4:(s8 + 1) * 4, :], (), ew))
            wq_sub[h] = subs
            wqc[h] = pr.cnt["scalar"]

        def make_qfeeder(h, extra_first=()):
            stepctr = {"n": 0}

            def feed(k):
                while k > 0 and stepctr["n"] < 64:
                    j = stepctr["n"]
                    hf, c = j // NIF, j % NIF
                    w = []
                    if hf == 0 and c % 4 == 0:
                        w.append(("scalar", wq_sub[h][c // 4]))
                    if j == 0:
                        w += [("vector", prep_v)] + list(extra_first)
                        if h >= 1:
                            w.append(("vector", rope[h - 1]))
                        if h % 8 == 0 and h > 0:
                            w.append(("scalar", wo_done))
                        if h == 0:
                            w.append(("vector", kv_ev))
                    qmm[h] = pr.emit("tensor", lambda t, c=c, par=h % 2, hf=hf: t.matmul(
                        PS1[:, hf * 512:(hf + 1) * 512], wqb[par][:, c, :],
                        xlnb[:, c, 256 + hf * 512:768 + hf * 512],
                        start=(c == 0), stop=(c == NIF - 1)), w,
                        1 if (c == NIF - 1 and hf == 1) else 0)
                    stepctr["n"] += 1
                    k -= 1
            return feed

        stage_wq(0)
        feed = make_qfeeder(0)
        feed(64)
        for hp in range(4):
            womm_last = 0
            for hh in range(8):
                h = hp * 8 + hh
                par = h % 2
                pr.emit("vector", lambda v, par=par: v.tensor_tensor(
                    qt1, PS1[:, 0:1024], cosx[:, 256:1280], ALU.mult), [("tensor", qmm[h])], 1)
                pr.emit("vector", lambda v: v.tensor_tensor(qt2[0:64, :], PS1[64:128, 0:1024],
                                                            s2x[0:64, 256:1280], ALU.mult), [], 1)
                pr.emit("vector", lambda v: v.tensor_tensor(qt2[64:128, :], PS1[0:64, 0:1024],
                                                            s2x[64:128, 256:1280], ALU.mult), [], 1)
                pr.drain()
                rope[h] = pr.emit("vector", lambda v, par=par: v.tensor_tensor(
                    qrb[par], qt1, qt2, ALU.add), [], 1)
                if hh < 7:
                    stage_wq(h + 1)
                    feed = make_qfeeder(h + 1)
                else:
                    feed = lambda k: None
                for wi in range(5):
                    q0w = max(0, wi * 256 - 256)
                    q1w = min(1024, wi * 256 + 256)
                    qn = q1w - q0w
                    npp = qn // 128
                    kc0 = wi * 2
                    sp = wi % 2
                    w01 = [("vector", rope[h]), ("scalar", sig2_prev)]
                    for kc in (0, 1):
                        s01 = pr.emit("tensor", lambda t, kc=kc, q0w=q0w, qn=qn, kc0=kc0, par=par: t.matmul(
                            PS2[:, kc * 512:kc * 512 + qn], krb[:, (kc0 + kc) * 128:(kc0 + kc + 1) * 128],
                            qrb[par][:, q0w:q0w + qn], start=True, stop=True),
                            w01 if kc == 0 else [], 1 if kc == 1 else 0)
                    feed(6)
                    wsg1 = [("tensor", s01)]
                    if (h, wi - 2) in av_hist:
                        wsg1.append(("tensor", av_hist[(h, wi - 2)]))
                    elif wi < 2 and h >= 1:
                        wsg1.append(("tensor", av_hist[(h - 1, 4 - wi)]))
                    pr.emit("scalar", lambda s, qn=qn, sp=sp: s.activation(
                        sgb[sp][:, 0:2, 0:qn], PS2[:].rearrange("p (c t) -> p c t", c=2)[:, :, 0:qn],
                        ACT.Sigmoid, scale=SCALE), wsg1, 1)
                    sig1 = pr.cnt["scalar"]
                    for kc in (2, 3):
                        s23 = pr.emit("tensor", lambda t, kc=kc, q0w=q0w, qn=qn, kc0=kc0, par=par: t.matmul(
                            PS2[:, (kc - 2) * 512:(kc - 2) * 512 + qn],
                            krb[:, (kc0 + kc) * 128:(kc0 + kc + 1) * 128],
                            qrb[par][:, q0w:q0w + qn], start=True, stop=True),
                            [("scalar", sig1)] if kc == 2 else [], 1 if kc == 3 else 0)
                    feed(6)
                    sig_cnt = pr.emit("scalar", lambda s, qn=qn, sp=sp: s.activation(
                        sgb[sp][:, 2:4, 0:qn], PS2[:].rearrange("p (c t) -> p c t", c=2)[:, :, 0:qn],
                        ACT.Sigmoid, scale=SCALE), [("tensor", s23)], 1)
                    sig2_prev = sig_cnt
                    avm = 0
                    for p in range(npp):
                        for kc in range(4):
                            st = (p % 2 == 0 and kc == 0)
                            sp_ = (p % 2 == 1 and kc == 3)
                            avm = pr.emit("tensor", lambda t, p=p, kc=kc, kc0=kc0, st=st, sp_=sp_, sp=sp: t.matmul(
                                PS3[:, p * 256:p * 256 + 130], sgb[sp][:, kc, p * 128:(p + 1) * 128],
                                vext[:, kc0 + kc, 0:130], start=st, stop=sp_),
                                ([("scalar", sig_cnt), ("vector", wn_last)] if (p == 0 and kc == 0) else []),
                                1 if (p == npp - 1 and kc == 3) else 0)
                    av_hist[(h, wi)] = avm
                    feed(4)
                    wn_last = pr.emit("vector", lambda v, npp=npp, wi=wi: v.tensor_copy(
                        wnr[wi % 2][:, 0:npp, :], PS3v[:, 0:npp, 0:129]), [("tensor", avm)], 1)
                    pr.drain()
                    pr.emit("vector", lambda v, npp=npp, wi=wi: v.reciprocal(
                        rca[wi % 2][:, 0:npp], wnr[wi % 2][:, 0:npp, 128:129][:, :, 0]), [], 1)
                    pr.drain()
                    if wi >= 1:
                        b = wi - 1
                        sA = 0 if b == 0 else 2
                        dgc = 0
                        for j in range(2):
                            pr.emit("vector", lambda v, b=b, j=j, sA=sA: v.tensor_scalar(
                                diag[:, j, :], id32, a0t[:, 2 * b + j:2 * b + j + 1],
                                rca[b % 2][:, sA + j:sA + j + 1], ALU.mult, ALU.mult),
                                [("tensor", dg_guard)] if j == 0 else [], 1)
                            dgc = pr.emit("vector", lambda v, b=b, j=j, wi=wi: v.tensor_scalar(
                                diag[:, 2 + j, :], id32, a1t[:, 2 * b + j:2 * b + j + 1],
                                rca[wi % 2][:, j:j + 1], ALU.mult, ALU.mult), [], 1)
                        tw = 0
                        for j in range(2):
                            pr.emit("tensor", lambda t, b=b, j=j, sA=sA: t.matmul(
                                PS4[:, j * 128:(j + 1) * 128], wnr[b % 2][:, sA + j, 0:128],
                                diag[:, j, :], start=True, stop=False),
                                [("vector", dgc), ("scalar", woutev)] if j == 0 else [], 0)
                            tw = pr.emit("tensor", lambda t, b=b, j=j, wi=wi: t.matmul(
                                PS4[:, j * 128:(j + 1) * 128], wnr[wi % 2][:, j, 0:128],
                                diag[:, 2 + j, :], start=False, stop=True), [], 1)
                        dg_guard = tw
                        woutev = pr.emit("scalar", lambda s, hh=hh, b=b: s.activation(
                            wout8[:, hh, b * 256:(b + 1) * 256], PS4[:, 0:256], ACT.Copy),
                            [("tensor", tw), ("tensor", womm_prev_grp)], 1)
                feed(64)
            for of in range(NIF):
                for s2_ in range(2):
                    ew = []
                    if of == 0 and s2_ == 0:
                        ew = [("tensor", qmm[hp * 8 + 7])]
                    elif of >= 2 and s2_ == 0:
                        ew = [("tensor", wo_grp)]
                    stage(woT_r[:, hp * 8 + s2_ * 4:hp * 8 + (s2_ + 1) * 4, of * 128:(of + 1) * 128],
                          wqb[of % 2][:, s2_ * 4:(s2_ + 1) * 4, :], (), ew)
                wo_cast = pr.cnt["scalar"]
                for ts2 in range(2):
                    gidx = of * 2 + ts2
                    slot = [PS1, PS2][gidx % 2]
                    sli = gidx % 2
                    wmm = [("scalar", wo_cast), ("scalar", woutev), ("scalar", sig_cnt)]
                    if gidx == 0:
                        wmm.append(("vector", rope[hp * 8 + 7]))
                    if gidx >= 2:
                        wmm.append(("scalar", wo_ev[gidx - 2]))
                    elif hp > 0:
                        wmm.append(("scalar", wo_ev[62 + gidx]))
                    mmc = 0
                    for gg in range(8):
                        mmc = pr.emit("tensor", lambda t, gg=gg, of=of, ts2=ts2, sl=slot: t.matmul(
                            sl[:, 0:512], wqb[of % 2][:, gg, :],
                            wout8[:, gg, ts2 * 512:(ts2 + 1) * 512],
                            start=(gg == 0), stop=(gg == 7)), wmm if gg == 0 else [],
                            1 if gg == 7 else 0)
                    wo_grp = mmc
                    womm_last = mmc
                    xsem = ["xsta", "xstb"][gidx % 2]
                    ev = pr.emit("scalar", lambda s, sli=sli, sl=slot: s.activation(
                        slabs[sli], sl[:, 0:512], ACT.Copy),
                        [("tensor", mmc), (xsem, pr.cnt[xsem]), ("xstore", precopy_done)], 1)
                    wo_ev[gidx] = ev
                    pr.dma(xsem, x2h_r[:, of, ts2 * 512:(ts2 + 1) * 512], slabs[sli],
                           [("scalar", ev)], accum=True)
            womm_prev_grp = womm_last
            wo_done = pr.cnt["scalar"]
            wo_ev = {62: wo_ev[62], 63: wo_ev[63]} if 63 in wo_ev else wo_ev
            if hp < 3:
                stage_wq(hp * 8 + 8, extra=[("tensor", womm_last)])
                feed = make_qfeeder(hp * 8 + 8, extra_first=[("scalar", wo_ev[63])])
                feed(64)

    ffn(xin_r, x1h_r, EXT)
    attn()
    ffn(x2h_r, out_r, OWN)

    total_st = {s: pr.cnt[s] for s in ("xstore", "xsta", "xstb")}

    with nc.Block() as block:
        def mk(e):
            def run(eng):
                hw = {}
                for fn, waits, inc, isem in pr.ops[e]:
                    for (we, wv) in waits:
                        if we == e:
                            continue
                        if wv > 0 and hw.get(we, -1) < wv:
                            eng.wait_ge(sems[we], wv)
                            hw[we] = wv
                    ins = fn(eng)
                    if inc:
                        ins.then_inc(sems[isem], inc)
                if e == "gpsimd":
                    for s, v in total_st.items():
                        if v:
                            eng.wait_ge(sems[s], v)
            return run
        block.sync(mk("sync"))
        block.gpsimd(mk("gpsimd"))
        block.scalar(mk("scalar"))
        block.vector(mk("vector"))
        block.tensor(mk("tensor"))
    return nc, ctx


def _rope_tables(pos):
    inv = 1.0 / (10000.0 ** (np.arange(0, DH, 2, dtype=np.float32) / np.float32(DH)))
    f = np.outer(pos.astype(np.float32), inv)
    cos = np.zeros((len(pos), DH), dtype=np.float32)
    sin = np.zeros((len(pos), DH), dtype=np.float32)
    cos[:, 0::2] = np.cos(f); cos[:, 1::2] = np.cos(f)
    sin[:, 0::2] = np.sin(f); sin[:, 1::2] = np.sin(f)
    return cos, sin


def kernel(L0_attn_x, L0_attn_Wq, L0_attn_Wk, L0_attn_Wv, L0_attn_Wo,
           L0_ffn_W_base, L0_ffn_b_base, L0_ffn_W_router, L0_ffn_A, L0_ffn_B):
    x = np.asarray(L0_attn_x, np.float32)
    B, S, _ = x.shape
    shared = dict(
        wbaseT=np.ascontiguousarray(np.asarray(L0_ffn_W_base, np.float32).T),
        wqT=np.ascontiguousarray(np.asarray(L0_attn_Wq, np.float32).T),
        woT=np.ascontiguousarray(np.asarray(L0_attn_Wo, np.float32).T),
        wkT=np.ascontiguousarray(np.asarray(L0_attn_Wk, np.float32).T),
        wvT=np.ascontiguousarray(np.asarray(L0_attn_Wv, np.float32).T),
        aT=np.ascontiguousarray(np.asarray(L0_ffn_A, np.float32).reshape(128, D).T),
        bstk=np.ascontiguousarray(np.transpose(np.asarray(L0_ffn_B, np.float32), (0, 2, 1)).reshape(128, D)),
        wrT=np.ascontiguousarray(np.asarray(L0_ffn_W_router, np.float32).T),
        bbc=np.ascontiguousarray(np.asarray(L0_ffn_b_base, np.float32).reshape(NIF, 128).T),
        ones1=np.ones((1, 128), np.float32),
        onesc=np.ones((128, 1), np.float32),
        eye=np.eye(128, dtype=np.float32),
    )
    lin = np.linspace(0.0, 1.0, 256, dtype=np.float32)
    nc, ctx = build()
    in_maps = []
    for c in range(8):
        b, qc = c // 4, c % 4
        q0 = qc * OWN
        lo, hi = q0 - 256, q0 + OWN + 256
        xp = np.zeros((EXT, D), np.float32)
        slo, shi = max(0, lo), min(S, hi)
        xp[slo - lo:shi - lo] = x[b, slo:shi]
        pos = np.clip(np.arange(lo, hi), 0, S - 1)
        cos, sin = _rope_tables(pos)
        s2 = np.empty((DH, EXT), np.float32)
        s2[0:64] = -sin.T[0:64]
        s2[64:128] = sin.T[64:128]
        vm = np.zeros((128, NKC), np.float32)
        for k in range(NKC):
            p0 = lo + k * 128
            vm[:, k] = 1.0 if (p0 >= 0 and p0 + 128 <= S) else 0.0
        a1 = np.empty((128, 8), np.float32)
        for qch in range(8):
            a1[:, qch] = lin[np.arange(128) + (qch % 2) * 128]
        a0 = (np.float32(1.0) - a1).astype(np.float32)
        if qc == 0:
            a0[:, 0:2] = 0.0
            a1[:, 0:2] = 1.0
        m = dict(shared)
        m.update(xin=np.ascontiguousarray(xp.T),
                 cosx=np.ascontiguousarray(cos.T), s2x=s2,
                 a0=a0, a1=a1, vm=vm)
        in_maps.append(m)
    res = run_bass_kernel_spmd(nc, in_maps, core_ids=list(range(8)))
    out = np.empty((B, S, D), np.float32)
    for c in range(8):
        b, qc = c // 4, c % 4
        out[b, qc * OWN:(qc + 1) * OWN, :] = res.results[c]["out"].T
    ctx.close()
    return out


# version 10
# speedup vs baseline: 1.2591x; 1.0267x over previous
import numpy as np
import concourse.bass as bass
import concourse.mybir as mybir
from concourse.bass_utils import run_bass_kernel_spmd
from contextlib import ExitStack

F32 = mybir.dt.float32
BF16 = mybir.dt.bfloat16
AX = mybir.AxisListType
ALU = mybir.AluOpType
ACT = mybir.ActivationFunctionType

D = 4096
NIF = 32
H = 32
DH = 128
OWN = 1024
EXT = 1536
NKC = EXT // 128
SCALE = 1.0 / float(np.sqrt(128.0))
AW = 49152
ENGS = ("sync", "gpsimd", "scalar", "vector", "tensor")


STREAM_ENG = {"wload0": "sync", "wload1": "sync", "wload2": "sync", "wmisc": "sync",
              "xload0": "gpsimd", "xload1": "gpsimd", "xload2": "gpsimd", "xload3": "gpsimd", "xstore": "gpsimd",
              "xsta": "gpsimd", "xstb": "gpsimd"}
SEMN = tuple(ENGS) + tuple(STREAM_ENG)


class Prog:
    def __init__(self):
        self.ops = {e: [] for e in ENGS}
        self.cnt = {s: 0 for s in SEMN}

    def emit(self, e, fn, waits=(), inc=0, isem=None):
        isem = isem or e
        self.ops[e].append((fn, tuple(waits), inc, isem))
        self.cnt[isem] += inc
        return self.cnt[isem]

    def dma(self, stream, out, in_, waits=(), accum=False):
        e = STREAM_ENG[stream]
        if accum:
            fn = lambda q, o=out, i=in_: q.dma_start(out=o, in_=i, accum_op=ALU.add)
        else:
            fn = lambda q, o=out, i=in_: q.dma_start(out=o, in_=i)
        return self.emit(e, fn, waits, 16, isem=stream)

    def drain(self):
        self.emit("vector", lambda v: v.drain(), (), 0)


class Arena:
    def __init__(self, t, off=0):
        self.t = t
        self.off = off

    def alloc(self, words):
        o = self.off
        self.off += words
        assert self.off <= AW, f"arena overflow {self.off} > {AW}"
        return o

    def f32(self, words, shape=None):
        o = self.alloc(words)
        ap = self.t[:, o:o + words]
        if shape is not None:
            ap = _resh(ap, shape)
        return ap

    def bf16(self, n, shape=None):
        o = self.alloc((n + 1) // 2)
        ap = self.t[:, o:o + (n + 1) // 2].bitcast(BF16)[:, 0:n]
        if shape is not None:
            ap = _resh(ap, shape)
        return ap

    def row(self, words):
        o = self.alloc(words)
        return self.t[0:1, o:o + words]


def _resh(ap, shape):
    names = [chr(97 + i) for i in range(len(shape))]
    return ap.rearrange("p (" + " ".join(names) + ") -> p " + " ".join(names),
                        **{n: s for n, s in zip(names, shape)})


def build():
    nc = bass.Bass("TRN2", target_bir_lowering=False)
    dt = lambda n, s, d=F32, k="ExternalInput": nc.dram_tensor(n, s, d, kind=k).ap()

    xin = dt("xin", [D, EXT])
    wbaseT = dt("wbaseT", [D, D])
    wqT = dt("wqT", [D, D])
    woT = dt("woT", [D, D])
    wkT = dt("wkT", [D, DH])
    wvT = dt("wvT", [D, DH])
    aT = dt("aT", [D, 128])
    bstk = dt("bstk", [128, D])
    wrT = dt("wrT", [D, 8])
    bbc = dt("bbc", [128, NIF])
    cosx_h = dt("cosx", [128, EXT])
    s2x_h = dt("s2x", [128, EXT])
    a0_h = dt("a0", [128, 8])
    a1_h = dt("a1", [128, 8])
    vm_h = dt("vm", [128, NKC])
    ones1_h = dt("ones1", [1, 128])
    onesc_h = dt("onesc", [128, 1])
    eye_h = dt("eye", [128, 128])
    out_h = dt("out", [D, OWN], F32, "ExternalOutput")
    x1h = dt("x1h", [D, EXT], F32, "Internal")
    x2h = dt("x2h", [D, OWN], F32, "Internal")

    xin_r = xin.rearrange("(c p) t -> p c t", p=128)
    x1h_r = x1h.rearrange("(c p) t -> p c t", p=128)
    x2h_r = x2h.rearrange("(c p) t -> p c t", p=128)
    out_r = out_h.rearrange("(c p) t -> p c t", p=128)
    wbaseT_r = wbaseT.rearrange("(c p) o -> p c o", p=128)
    wqT_r = wqT.rearrange("(c p) o -> p c o", p=128)
    woT_r = woT.rearrange("(c p) o -> p c o", p=128)
    wkT_r = wkT.rearrange("(c p) o -> p c o", p=128)
    wvT_r = wvT.rearrange("(c p) o -> p c o", p=128)
    aT_r = aT.rearrange("(c p) o -> p c o", p=128)
    wrT_r = wrT.rearrange("(c p) o -> p c o", p=128)

    ctx = ExitStack()
    E = ctx.enter_context
    arena_t = E(nc.sbuf_tensor("arena", [128, AW], F32))
    ar = Arena(arena_t)

    PS1 = E(nc.psum_tensor("PS1", [128, 1024], F32))
    PS2 = E(nc.psum_tensor("PS2", [128, 1024], F32))
    PS3 = E(nc.psum_tensor("PS3", [128, 1024], F32))
    PS4 = E(nc.psum_tensor("PS4", [128, 512], F32))
    PSM = E(nc.psum_tensor("PSM", [128, 512], F32))
    PS4b = PS4[:].bitcast(BF16)
    PS3v = PS3[:].rearrange("p (c t) -> p c t", c=4)

    sems = {e: E(nc.semaphore("s_" + e)) for e in SEMN}
    pr = Prog()

    def store_q():
        return [(s, pr.cnt[s]) for s in ("xstore", "xsta", "xstb") if pr.cnt[s]]

    xlnb = ar.bf16(NIF * EXT, (NIF, EXT))
    bbt = ar.f32(NIF)
    bbh2 = ar.f32(NIF)
    id32 = ar.f32(128)
    idb16 = ar.bf16(128)
    onesb = ar.bf16(2)
    vmb = ar.bf16(NKC)
    vmf = ar.f32(NKC)
    a0t = ar.f32(8)
    a1t = ar.f32(8)
    RO = ar.alloc(7 * 128)
    ones1 = arena_t[0:1, RO:RO + 128]
    murow = arena_t[0:1, RO + 128:RO + 256]
    msrow = arena_t[0:1, RO + 256:RO + 384]
    t1row = arena_t[0:1, RO + 384:RO + 512]
    varrow = arena_t[0:1, RO + 512:RO + 640]
    invsrow = arena_t[0:1, RO + 640:RO + 768]
    misrow = arena_t[0:1, RO + 768:RO + 896]
    PBASE = ar.off

    sa = Arena(arena_t, PBASE)
    eyetmp = sa.f32(128)
    onesctmp = sa.f32(1)
    c_eye = pr.dma("wmisc", eyetmp, eye_h)
    c_b = pr.dma("wmisc", bbt, bbc)
    c_vm = pr.dma("wmisc", vmf, vm_h)
    c_a0 = pr.dma("wmisc", a0t, a0_h)
    c_a1 = pr.dma("wmisc", a1t, a1_h)
    c_o1 = pr.dma("wmisc", ones1, ones1_h)
    c_oc = pr.dma("wmisc", onesctmp, onesc_h)
    c_id = pr.dma("wmisc", id32, eye_h)
    pr.emit("scalar", lambda s: s.activation(idb16, eyetmp, ACT.Copy), [("wmisc", c_id)], 1)
    pr.emit("scalar", lambda s: s.activation(onesb[:, 0:1], onesctmp, ACT.Copy), [("wmisc", c_id)], 1)
    setup_sc = pr.emit("scalar", lambda s: s.activation(vmb, vmf, ACT.Copy), [("wmisc", c_id)], 1)
    setup_ve = pr.emit("vector", lambda v: v.tensor_scalar(bbh2, bbt, 0.5, None, ALU.mult),
                       [("wmisc", c_id)], 1)

    def ln_prep(src_r, ntok, base, entry=(), hook=None):
        a = Arena(arena_t, base)
        xf = [a.f32(NIF * 128, (NIF, 128)), a.f32(NIF * 128, (NIF, 128))]
        xb = [a.bf16(NIF * 256, (NIF, 256)), a.bf16(NIF * 256, (NIF, 256))]
        nch = ntok // 128
        apply2 = [0] * nch
        rankc = [0] * nch
        stats = [0] * nch
        rowrd = [0] * nch
        s1l = [0] * nch
        for i in range(nch):
            par = i % 2
            ps_st = [PSM[0:1, 0:256], PS1[0:1, 0:256]][par]
            ps_iv = [PS4[:, 0:128], PS2[:, 0:128]][par]
            ps_mi = [PSM[:, 0:128], PS1[:, 0:128]][par]
            wld = list(entry) if i == 0 else ([("vector", apply2[i - 2])] if i >= 2 else [])
            wld = wld + store_q()
            sem = "xload%d" % par
            semb = "xload%d" % (2 + par)
            g0 = pr.dma(sem, xf[par][:, 0:16, :], src_r[:, 0:16, i * 128:(i + 1) * 128], wld)
            g = pr.dma(semb, xf[par][:, 16:32, :], src_r[:, 16:32, i * 128:(i + 1) * 128], [])
            wsc = [(sem, g0)]
            if i >= 2:
                wsc.append(("tensor", stats[i - 2]))
            pr.emit("scalar", lambda s, p=par: s.activation(
                xb[p][:, 0:16, 0:128], xf[p][:, 0:16, :], ACT.Copy), wsc, 1)
            sh0 = pr.emit("scalar", lambda s, p=par: s.activation(
                xb[p][:, 0:16, 128:256], xf[p][:, 0:16, :], ACT.Square), [], 1)
            pr.emit("scalar", lambda s, p=par: s.activation(
                xb[p][:, 16:32, 0:128], xf[p][:, 16:32, :], ACT.Copy), [(semb, g)], 1)
            s1l[i] = s1 = pr.emit("scalar", lambda s, p=par: s.activation(
                xb[p][:, 16:32, 128:256], xf[p][:, 16:32, :], ACT.Square), [], 1)
            wmm = [("scalar", sh0)]
            if i >= 2:
                wmm.append(("vector", apply2[i - 2]))
            for c in range(NIF):
                w_ = wmm if c == 0 else ([("scalar", s1)] if c == 16 else [])
                stats[i] = pr.emit("tensor", lambda t, c=c, p=par, ps=ps_st: t.matmul(
                    ps, onesb[:, 0:1], xb[p][:, c, :],
                    start=(c == 0), stop=(c == NIF - 1)), w_,
                    1 if c == NIF - 1 else 0)
            wv = [("tensor", stats[i])]
            if i >= 1:
                wv.append(("tensor", rankc[i - 1]))
            pr.emit("vector", lambda v, ps=ps_st: v.tensor_scalar(
                murow, ps[0:1, 0:128], 1.0 / D, None, ALU.mult), wv, 1)
            pr.emit("vector", lambda v, ps=ps_st: v.tensor_scalar(
                msrow, ps[0:1, 128:256], 1.0 / D, None, ALU.mult), [], 1)
            pr.drain()
            pr.emit("vector", lambda v: v.tensor_tensor(t1row, murow, murow, ALU.mult), [], 1)
            pr.drain()
            pr.emit("vector", lambda v: v.tensor_tensor(varrow, msrow, t1row, ALU.subtract), [], 1)
            pr.drain()
            rv = pr.emit("vector", lambda v: v.tensor_scalar(varrow, varrow, 1e-5, None, ALU.add), [], 1)
            si = pr.emit("scalar", lambda s: s.activation(t1row, varrow, ACT.Sqrt),
                         [("vector", rv)], 1)
            vr = pr.emit("vector", lambda v: v.reciprocal(invsrow, t1row), [("scalar", si)], 1)
            pr.drain()
            rowrd[i] = pr.emit("vector", lambda v: v.tensor_tensor(misrow, murow, invsrow, ALU.mult),
                               [], 1)
            pr.emit("tensor", lambda t, ps=ps_iv: t.matmul(ps, ones1, invsrow, start=True, stop=True),
                    [("vector", rowrd[i])], 1)
            rankc[i] = pr.emit("tensor", lambda t, ps=ps_mi: t.matmul(
                ps, ones1, misrow, start=True, stop=True), [], 1)
            pr.emit("vector", lambda v, p=par, ps=ps_iv: v.tensor_tensor(
                xf[p], xf[p], ps.unsqueeze(1).broadcast_to([128, NIF, 128]), ALU.mult),
                [("tensor", rankc[i])], 1)
            pr.drain()
            apply2[i] = pr.emit("vector", lambda v, p=par, i=i, ps=ps_mi: v.tensor_tensor(
                xlnb[:, :, i * 128:(i + 1) * 128], xf[p],
                ps.unsqueeze(1).broadcast_to([128, NIF, 128]), ALU.subtract), [], 1)
            if hook is not None:
                hook(i, apply2)
        return {"vector": apply2[nch - 1], "tensor": max(stats[nch - 1], rankc[nch - 1]),
                "scalar": s1l[nch - 1]}

    def make_stager(wstage):
        nb = len(wstage)
        state = {"i": 0, "cast": {}}

        def load_cast(src_ap, dst_bf_ap, extra_dma_waits=(), extra_cast_waits=()):
            i = state["i"]
            state["i"] += 1
            wdma = list(extra_dma_waits)
            if i - nb in state["cast"]:
                wdma.append(("scalar", state["cast"][i - nb]))
            sem = "wload%d" % (i % nb)
            g = pr.dma(sem, wstage[i % nb], src_ap, wdma)
            wc = [(sem, g)] + list(extra_cast_waits)
            c = pr.emit("scalar", lambda s, si=i % nb, d=dst_bf_ap: s.activation(d, wstage[si], ACT.Copy), wc, 1)
            state["cast"][i] = c
            return c
        return load_cast

    def ffn(src_r, dst_r, ntok):
        snap0 = [("scalar", pr.cnt["scalar"]), ("vector", pr.cnt["vector"]),
                 ("tensor", pr.cnt["tensor"])]
        a0a = Arena(arena_t, PBASE)
        ab16 = a0a.bf16(NIF * 128, (NIF, 128))
        bb16 = a0a.bf16(D)
        rb16 = a0a.bf16(NIF * 8, (NIF, 8))
        hwb = a0a.bf16(EXT)
        PREP_B = a0a.off
        st_a = Arena(arena_t, PREP_B)
        atmp = st_a.f32(NIF * 128, (NIF, 128))
        bstmp = st_a.f32(D)
        rtmp = st_a.f32(NIF * 8, (NIF, 8))
        c_a = pr.dma("wmisc", atmp, aT_r, snap0)
        c_bs = pr.dma("wmisc", bstmp, bstk, [])
        c_r = pr.dma("wmisc", rtmp, wrT_r, [])
        pr.emit("scalar", lambda s: s.activation(ab16[:], atmp, ACT.Copy), [("wmisc", c_r)] + snap0, 1)
        pr.emit("scalar", lambda s: s.activation(bb16[:], bstmp, ACT.Copy), [("wmisc", c_r)], 1)
        sm_cast = pr.emit("scalar", lambda s: s.activation(rb16[:], rtmp, ACT.Copy), [("wmisc", c_r)], 1)
        snap1 = [("scalar", sm_cast), ("vector", pr.cnt["vector"]), ("tensor", pr.cnt["tensor"])]
        ga = Arena(arena_t, PREP_B)
        gxb = ga.f32(512)
        lg = arena_t[0:8, ga.alloc(512) + 0:ga.off]
        ltm = ga.f32(32, (4, 8))
        gate = ga.f32(32, (4, 8))
        eq1 = ga.f32(32, (4, 8))
        eq2 = ga.f32(32, (4, 8))
        mskb = ga.f32(32)
        m1c = ga.f32(4)
        m2c = ga.f32(4)
        p1c = ga.f32(4)
        p2c = ga.f32(4)
        OFL_B = ga.off

        NTS = ntok // 512
        st_aph = {"hw": 0, "hw_hist": [0] * (NTS + 2), "se": 0, "sg": 0}

        def aphase(ts, wait_v):
            cols = slice(ts * 512, (ts + 1) * 512)
            hslot = PS3[:, (ts % 2) * 512:(ts % 2) * 512 + 512]
            wmm = [("vector", wait_v), ("scalar", sm_cast)]
            if ts >= 2:
                wmm.append(("vector", st_aph["hw_hist"][ts - 2]))
            for c in range(NIF):
                pr.emit("tensor", lambda t, c=c, h=hslot, cc=cols: t.matmul(
                    h, ab16[:, c, :], xlnb[:, c, cc], start=(c == 0), stop=(c == NIF - 1)),
                    wmm if c == 0 else [], 0)
            wrt = [("scalar", st_aph["se"])] if ts > 0 else [("vector", wait_v)]
            h_cnt = 0
            for c in range(NIF):
                h_cnt = pr.emit("tensor", lambda t, c=c, cc=cols: t.matmul(
                    PS1[0:8, 512:1024], rb16[:, c, :], xlnb[:, c, cc],
                    start=(c == 0), stop=(c == NIF - 1)), wrt if c == 0 else [],
                    1 if c == NIF - 1 else 0)
            se = pr.emit("scalar", lambda s: s.activation(lg, PS1[0:8, 512:1024], ACT.Copy),
                         [("tensor", h_cnt)], 1)
            st_aph["se"] = se
            tp = 0
            for qc in range(4):
                tp = pr.emit("tensor", lambda t, qc=qc: t.transpose(
                    PS2[:, 512 + qc * 8:512 + (qc + 1) * 8], lg[0:8, qc * 128:(qc + 1) * 128],
                    id32[0:8, 0:8]),
                    ([("scalar", se), ("scalar", st_aph["sg"])] if qc == 0 else []),
                    1 if qc == 3 else 0)
            se2 = pr.emit("scalar", lambda s: s.activation(
                ltm, _resh(PS2[:, 512:544], (4, 8)), ACT.Copy), [("tensor", tp)], 1)
            pr.emit("vector", lambda v: v.reduce_max(m1c, ltm, axis=AX.X), [("scalar", se2)], 1)
            pr.drain()
            for qc in range(4):
                pr.emit("vector", lambda v, qc=qc: v.tensor_scalar(
                    eq1[:, qc, :], ltm[:, qc, :], m1c[:, qc:qc + 1], None, ALU.is_equal), [], 1)
            pr.drain()
            pr.emit("vector", lambda v: v.scalar_tensor_tensor(
                mskb, eq1.rearrange("p c e -> p (c e)"), -30.0, ltm.rearrange("p c e -> p (c e)"),
                ALU.mult, ALU.add), [], 1)
            pr.drain()
            vm2 = pr.emit("vector", lambda v: v.reduce_max(m2c, _resh(mskb, (4, 8)), axis=AX.X), [], 1)
            sp = 0
            for qc in range(4):
                sp = pr.emit("scalar", lambda s, qc=qc: s.activation(
                    p1c[:, qc:qc + 1], m2c[:, qc:qc + 1], ACT.Sigmoid, bias=m1c[:, qc:qc + 1], scale=-1.0),
                    [("vector", vm2)] if qc == 0 else [], 1)
                sp = pr.emit("scalar", lambda s, qc=qc: s.activation(
                    p2c[:, qc:qc + 1], m1c[:, qc:qc + 1], ACT.Sigmoid, bias=m2c[:, qc:qc + 1], scale=-1.0),
                    [], 1)
            for qc in range(4):
                pr.emit("vector", lambda v, qc=qc: v.tensor_scalar(
                    gate[:, qc, :], eq1[:, qc, :], p1c[:, qc:qc + 1], 2.0, ALU.mult, ALU.mult),
                    [("scalar", sp)] if qc == 0 else [], 1)
            for qc in range(4):
                pr.emit("vector", lambda v, qc=qc: v.tensor_scalar(
                    eq2[:, qc, :], _resh(mskb, (4, 8))[:, qc, :], m2c[:, qc:qc + 1], None, ALU.is_equal),
                    [], 1)
            pr.drain()
            for qc in range(4):
                pr.emit("vector", lambda v, qc=qc: v.tensor_scalar(
                    eq2[:, qc, :], eq2[:, qc, :], p2c[:, qc:qc + 1], 2.0, ALU.mult, ALU.mult), [], 1)
            pr.drain()
            pr.emit("vector", lambda v: v.tensor_tensor(
                gate.rearrange("p c e -> p (c e)"), gate.rearrange("p c e -> p (c e)"),
                eq2.rearrange("p c e -> p (c e)"), ALU.add), [], 1)
            pr.drain()
            vgx = pr.emit("vector", lambda v: v.tensor_copy(
                _resh(gxb, (4, 8, 16)), gate.unsqueeze(3).broadcast_to([128, 4, 8, 16])), [], 1)
            tg = 0
            for qc in range(4):
                tg = pr.emit("tensor", lambda t, qc=qc: t.transpose(
                    PS2[:, 512 + qc * 128:512 + (qc + 1) * 128], gxb[:, qc * 128:(qc + 1) * 128], id32),
                    [("vector", vgx), ("scalar", se2)] if qc == 0 else [], 1 if qc == 3 else 0)
            sg = pr.emit("scalar", lambda s: s.activation(gxb, PS2[:, 512:1024], ACT.Copy),
                         [("tensor", tg)], 1)
            st_aph["sg"] = sg
            st_aph["hw_hist"][ts] = st_aph["hw"] = pr.emit(
                "vector", lambda v, h=hslot, cc=cols: v.tensor_tensor(
                    hwb[:, cc], h, gxb, ALU.mult), [("scalar", sg)], 1)

        def hook(i, ap2):
            if i % 4 == 3 and (i // 4) < NTS:
                aphase(i // 4, ap2[i])

        pd = ln_prep(src_r, ntok, OFL_B, snap1, hook=hook)
        prep_v, prep_t = pd["vector"], pd["tensor"]
        hw_cnt = st_aph["hw"]
        a = Arena(arena_t, OFL_B)
        wstage = [a.f32(8 * 128, (8, 128)) for _ in range(3)]
        wb = [a.bf16(NIF * 128, (NIF, 128)), a.bf16(NIF * 128, (NIF, 128))]
        sl_sizes = []
        t0 = 0
        while t0 < ntok:
            n = min(512, ntok - t0)
            sl_sizes.append((t0, n))
            t0 += n
        slabs = [[a.f32(n) for (s0, n) in sl_sizes] for _ in range(2)]
        stage = make_stager(wstage)

        grp_cnt = {}
        evac_cnt = {}
        first_slab = True
        for of in range(NIF):
            extra = [("vector", prep_v)] if of == 0 else []
            ecast = [("tensor", prep_t)] if of == 0 else []
            if of >= 2:
                ecast = [("tensor", grp_cnt[(of - 2, len(sl_sizes) - 1)])]
            for s4 in range(4):
                stage(wbaseT_r[:, s4 * 8:(s4 + 1) * 8, of * 128:(of + 1) * 128],
                      wb[of % 2][:, s4 * 8:(s4 + 1) * 8, :],
                      extra if s4 == 0 else (), ecast if s4 == 0 else ())
            cast_done = pr.cnt["scalar"]
            xsem = "xload%d" % (of % 2)
            for tsl, (t0, n) in enumerate(sl_sizes):
                wld = [("vector", prep_v)] if first_slab else []
                first_slab = False
                wld = wld + store_q()
                gld = pr.dma(xsem, slabs[of % 2][tsl], src_r[:, of, t0:t0 + n], wld)
            for tsl, (t0, n) in enumerate(sl_sizes):
                gidx = of * len(sl_sizes) + tsl
                pspair = [PS1, PS2][(gidx // 2) % 2]
                off = (gidx % 2) * 512
                slot = pspair[:, off:off + n]
                wmm = [("scalar", cast_done), ("vector", hw_cnt)]
                if gidx >= 4:
                    wmm.append(("vector", evac_cnt[gidx - 4]))
                for c in range(NIF):
                    pr.emit("tensor", lambda t, c=c, sl=slot, of=of, t0=t0, n=n: t.matmul(
                        sl, wb[of % 2][:, c, :], xlnb[:, c, t0:t0 + n],
                        start=(c == 0), stop=False), wmm if c == 0 else [], 0)
                gmm = pr.emit("tensor", lambda t, sl=slot, of=of, t0=t0, n=n: t.matmul(
                    sl, bb16[:, of * 128:(of + 1) * 128], hwb[:, t0:t0 + n],
                    start=False, stop=True), [], 1)
                grp_cnt[(of, tsl)] = gmm
                pr.emit("vector", lambda v, of=of, tsl=tsl: v.tensor_scalar(
                    slabs[of % 2][tsl], slabs[of % 2][tsl], 1.0, bbh2[:, of:of + 1],
                    ALU.mult, ALU.add), [(xsem, gld)], 1)
                pr.drain()
                ev = pr.emit("vector", lambda v, of=of, tsl=tsl, sl=slot: v.scalar_tensor_tensor(
                    slabs[of % 2][tsl], sl, 0.5, slabs[of % 2][tsl],
                    ALU.mult, ALU.add), [("tensor", gmm)], 1)
                evac_cnt[gidx] = ev
                pr.dma("xstore", dst_r[:, of, t0:t0 + n], slabs[of % 2][tsl], [("vector", ev)])

    def attn():
        snap0 = [("scalar", pr.cnt["scalar"]), ("vector", pr.cnt["vector"]),
                 ("tensor", pr.cnt["tensor"])]
        pd = ln_prep(x1h_r, EXT, PBASE, snap0)
        prep_v, prep_t = pd["vector"], pd["tensor"]
        a = Arena(arena_t, PBASE)
        cosx = a.f32(EXT)
        s2x = a.f32(EXT)
        krb = a.bf16(EXT)
        vext = a.bf16(NKC * 132, (NKC, 132))
        wstage = [a.f32(4 * 128, (4, 128)) for _ in range(3)]
        wqb = [a.bf16(NIF * 128, (NIF, 128)), a.bf16(NIF * 128, (NIF, 128))]
        qt1 = a.f32(1024)
        qt2 = a.f32(1024)
        qrb = [a.bf16(1024), a.bf16(1024)]
        vbst = a.bf16(1024)
        sgb = [a.bf16(4 * 512, (4, 512)), a.bf16(4 * 512, (4, 512))]
        wnr = [a.f32(4 * 129, (4, 129)), a.f32(4 * 129, (4, 129))]
        rca = [a.f32(4), a.f32(4)]
        wout8 = a.bf16(8 * 1024, (8, 1024))
        slabs = [a.f32(512), a.f32(512)]
        diag = a.f32(4 * 128, (4, 128))
        stage = make_stager(wstage)

        cl = pr.dma("wmisc", cosx, cosx_h, [("vector", prep_v)])
        cos_l = pr.dma("wmisc", s2x, s2x_h, [])
        pr.drain()
        pr.emit("vector", lambda v: v.memset(vext[:, :, 129:132], 0.0), [], 1)

        for s8 in range(8):
            stage(wkT_r[:, s8 * 4:(s8 + 1) * 4, :], wqb[0][:, s8 * 4:(s8 + 1) * 4, :],
                  [("vector", prep_v)] if s8 == 0 else (), [("tensor", prep_t)] if s8 == 0 else ())
        for s8 in range(8):
            stage(wvT_r[:, s8 * 4:(s8 + 1) * 4, :], wqb[1][:, s8 * 4:(s8 + 1) * 4, :])
        kv_cast = pr.cnt["scalar"]
        kvw = pr.cnt["wmisc"]

        kv_ev = 0
        vmm = 0
        kev = 0
        for grp in range(3):
            t0 = grp * 512
            n = 512
            cols = slice(t0, t0 + n)
            off = (grp % 2) * 512
            wmm = [("scalar", kv_cast), ("vector", prep_v)]
            if grp >= 1:
                wmm.append(("vector", kev))
            kmm = 0
            for c in range(NIF):
                kmm = pr.emit("tensor", lambda t, c=c, cc=cols, off=off: t.matmul(
                    PS1[:, off:off + 512], wqb[0][:, c, :], xlnb[:, c, cc],
                    start=(c == 0), stop=(c == NIF - 1)),
                    wmm if c == 0 else [], 1 if c == NIF - 1 else 0)
            for c in range(NIF):
                vmm = pr.emit("tensor", lambda t, c=c, cc=cols, off=off: t.matmul(
                    PS2[:, off:off + 512], wqb[1][:, c, :], xlnb[:, c, cc],
                    start=(c == 0), stop=(c == NIF - 1)),
                    [], 1 if c == NIF - 1 else 0)
            pr.emit("vector", lambda v, cc=cols, off=off: v.tensor_tensor(
                qt1[:, 0:512], PS1[:, off:off + 512], cosx[:, cc], ALU.mult),
                [("tensor", kmm), ("wmisc", kvw)], 1)
            pr.emit("vector", lambda v, cc=cols, off=off: v.tensor_tensor(
                qt2[0:64, 0:512], PS1[64:128, off:off + 512], s2x[0:64, cc], ALU.mult), [], 1)
            pr.emit("vector", lambda v, cc=cols, off=off: v.tensor_tensor(
                qt2[64:128, 0:512], PS1[0:64, off:off + 512], s2x[64:128, cc], ALU.mult), [], 1)
            pr.drain()
            kev = pr.emit("vector", lambda v, cc=cols: v.tensor_tensor(
                krb[:, cc], qt1[:, 0:512], qt2[:, 0:512], ALU.add), [], 1)
            sv = pr.emit("scalar", lambda s, off=off: s.activation(
                vbst[:, 0:512], PS2[:, off:off + 512], ACT.Copy), [("tensor", vmm)], 1)
            tv = 0
            for k2 in range(4):
                tv = pr.emit("tensor", lambda t, k2=k2: t.transpose(
                    PS4b[:, k2 * 128:(k2 + 1) * 128], vbst[:, k2 * 128:(k2 + 1) * 128],
                    idb16), [("scalar", sv), ("vector", kv_ev)] if k2 == 0 else [],
                    1 if k2 == 3 else 0)
            kcg0 = t0 // 128
            pr.emit("vector", lambda v, kcg0=kcg0: v.tensor_tensor(
                vext[:, kcg0:kcg0 + 4, 0:128], _resh(PS4b, (8, 128))[:, 0:4, :],
                vmb[:, kcg0:kcg0 + 4].unsqueeze(2).broadcast_to([128, 4, 128]), ALU.mult),
                [("tensor", tv)], 1)
            kv_ev = pr.emit("vector", lambda v, kcg0=kcg0: v.tensor_copy(
                vext[:, kcg0:kcg0 + 4, 128:129], vmb[:, kcg0:kcg0 + 4].unsqueeze(2)), [], 1)

        pc = pr.cnt["xstore"]
        for i2 in range(64):
            of, ts2 = i2 // 2, i2 % 2
            par = i2 % 2
            w = [("xstore", pc), ("vector", prep_v)] if i2 == 0 else ([("xstore", pr.cnt["xstore"])] if i2 >= 2 else [])
            gl = pr.dma("xload%d" % par, slabs[par],
                        x1h_r[:, of, 256 + ts2 * 512:256 + (ts2 + 1) * 512], w)
            pr.dma("xstore", x2h_r[:, of, ts2 * 512:(ts2 + 1) * 512], slabs[par],
                   [("xload%d" % par, gl)])
        precopy_done = pr.cnt["xstore"]

        qmm = [0] * (H + 2)
        rope = [0] * (H + 2)
        wqc = [0] * (H + 2)
        sig_cnt = 0
        sig2_prev = 0
        av_hist = {}
        wn_last = kv_ev
        woutev = 0
        womm_prev_grp = max(vmm, 1)
        wo_ev = {}
        wo_done = 0
        dg_guard = 0
        pend_blend = None

        wq_sub = {}

        def stage_wq(h, extra=()):
            subs = []
            for s8 in range(8):
                ew = []
                if h < 2 and s8 == 0:
                    ew = [("tensor", vmm)]
                elif s8 == 0:
                    ew = [("tensor", qmm[h - 2])]
                ew = ew + (list(extra) if s8 == 0 else [])
                subs.append(stage(wqT_r[:, s8 * 4:(s8 + 1) * 4, h * 128:(h + 1) * 128],
                            wqb[h % 2][:, s8 * 4:(s8 + 1) * 4, :], (), ew))
            wq_sub[h] = subs
            wqc[h] = pr.cnt["scalar"]

        def make_qfeeder(h, extra_first=()):
            stepctr = {"n": 0}

            def feed(k):
                while k > 0 and stepctr["n"] < 64:
                    j = stepctr["n"]
                    hf, c = j // NIF, j % NIF
                    w = []
                    if hf == 0 and c % 4 == 0:
                        w.append(("scalar", wq_sub[h][c // 4]))
                    if j == 0:
                        w += [("vector", prep_v)] + list(extra_first)
                        if h >= 1:
                            w.append(("vector", rope[h - 1]))
                        if h % 8 == 0 and h > 0:
                            w.append(("scalar", wo_done))
                        if h == 0:
                            w.append(("vector", kv_ev))
                    qmm[h] = pr.emit("tensor", lambda t, c=c, par=h % 2, hf=hf: t.matmul(
                        PS1[:, hf * 512:(hf + 1) * 512], wqb[par][:, c, :],
                        xlnb[:, c, 256 + hf * 512:768 + hf * 512],
                        start=(c == 0), stop=(c == NIF - 1)), w,
                        1 if (c == NIF - 1 and hf == 1) else 0)
                    stepctr["n"] += 1
                    k -= 1
            return feed

        stage_wq(0)
        feed = make_qfeeder(0)
        feed(64)
        for hp in range(4):
            womm_last = 0
            for hh in range(8):
                h = hp * 8 + hh
                par = h % 2
                pr.emit("vector", lambda v, par=par: v.tensor_tensor(
                    qt1, PS1[:, 0:1024], cosx[:, 256:1280], ALU.mult), [("tensor", qmm[h])], 1)
                pr.emit("vector", lambda v: v.tensor_tensor(qt2[0:64, :], PS1[64:128, 0:1024],
                                                            s2x[0:64, 256:1280], ALU.mult), [], 1)
                pr.emit("vector", lambda v: v.tensor_tensor(qt2[64:128, :], PS1[0:64, 0:1024],
                                                            s2x[64:128, 256:1280], ALU.mult), [], 1)
                pr.drain()
                rope[h] = pr.emit("vector", lambda v, par=par: v.tensor_tensor(
                    qrb[par], qt1, qt2, ALU.add), [], 1)
                if hh < 7:
                    stage_wq(h + 1)
                    feed = make_qfeeder(h + 1)
                else:
                    feed = lambda k: None
                for wi in range(5):
                    q0w = max(0, wi * 256 - 256)
                    q1w = min(1024, wi * 256 + 256)
                    qn = q1w - q0w
                    npp = qn // 128
                    kc0 = wi * 2
                    sp = wi % 2
                    w01 = [("vector", rope[h]), ("scalar", sig2_prev)]
                    for kc in (0, 1):
                        s01 = pr.emit("tensor", lambda t, kc=kc, q0w=q0w, qn=qn, kc0=kc0, par=par: t.matmul(
                            PS2[:, kc * 512:kc * 512 + qn], krb[:, (kc0 + kc) * 128:(kc0 + kc + 1) * 128],
                            qrb[par][:, q0w:q0w + qn], start=True, stop=True),
                            w01 if kc == 0 else [], 1 if kc == 1 else 0)
                    if pend_blend is not None:
                        b, sA, wpar, dgc, hh_ = pend_blend
                        pend_blend = None
                        tw = 0
                        for j in range(2):
                            pr.emit("tensor", lambda t, b=b, j=j, sA=sA: t.matmul(
                                PS4[:, j * 128:(j + 1) * 128], wnr[b % 2][:, sA + j, 0:128],
                                diag[:, j, :], start=True, stop=False),
                                [("vector", dgc), ("scalar", woutev)] if j == 0 else [], 0)
                            tw = pr.emit("tensor", lambda t, b=b, j=j, wpar=wpar: t.matmul(
                                PS4[:, j * 128:(j + 1) * 128], wnr[wpar][:, j, 0:128],
                                diag[:, 2 + j, :], start=False, stop=True), [], 1)
                        dg_guard = tw
                        woutev = pr.emit("scalar", lambda s, hh_=hh_, b=b: s.activation(
                            wout8[:, hh_, b * 256:(b + 1) * 256], PS4[:, 0:256], ACT.Copy),
                            [("tensor", tw), ("tensor", womm_prev_grp)], 1)
                    feed(6)
                    wsg1 = [("tensor", s01)]
                    if (h, wi - 2) in av_hist:
                        wsg1.append(("tensor", av_hist[(h, wi - 2)]))
                    elif wi < 2 and h >= 1:
                        wsg1.append(("tensor", av_hist[(h - 1, 4 - wi)]))
                    pr.emit("scalar", lambda s, qn=qn, sp=sp: s.activation(
                        sgb[sp][:, 0:2, 0:qn], PS2[:].rearrange("p (c t) -> p c t", c=2)[:, :, 0:qn],
                        ACT.Sigmoid, scale=SCALE), wsg1, 1)
                    sig1 = pr.cnt["scalar"]
                    for kc in (2, 3):
                        s23 = pr.emit("tensor", lambda t, kc=kc, q0w=q0w, qn=qn, kc0=kc0, par=par: t.matmul(
                            PS2[:, (kc - 2) * 512:(kc - 2) * 512 + qn],
                            krb[:, (kc0 + kc) * 128:(kc0 + kc + 1) * 128],
                            qrb[par][:, q0w:q0w + qn], start=True, stop=True),
                            [("scalar", sig1)] if kc == 2 else [], 1 if kc == 3 else 0)
                    feed(6)
                    sig_cnt = pr.emit("scalar", lambda s, qn=qn, sp=sp: s.activation(
                        sgb[sp][:, 2:4, 0:qn], PS2[:].rearrange("p (c t) -> p c t", c=2)[:, :, 0:qn],
                        ACT.Sigmoid, scale=SCALE), [("tensor", s23)], 1)
                    sig2_prev = sig_cnt
                    avm = 0
                    for p in range(npp):
                        for kc in range(4):
                            st = (p % 2 == 0 and kc == 0)
                            sp_ = (p % 2 == 1 and kc == 3)
                            avm = pr.emit("tensor", lambda t, p=p, kc=kc, kc0=kc0, st=st, sp_=sp_, sp=sp: t.matmul(
                                PS3[:, p * 256:p * 256 + 130], sgb[sp][:, kc, p * 128:(p + 1) * 128],
                                vext[:, kc0 + kc, 0:130], start=st, stop=sp_),
                                ([("scalar", sig_cnt), ("vector", wn_last)] if (p == 0 and kc == 0) else []),
                                1 if (p == npp - 1 and kc == 3) else 0)
                    av_hist[(h, wi)] = avm
                    feed(4)
                    wn_last = pr.emit("vector", lambda v, npp=npp, wi=wi: v.tensor_copy(
                        wnr[wi % 2][:, 0:npp, :], PS3v[:, 0:npp, 0:129]), [("tensor", avm)], 1)
                    pr.drain()
                    pr.emit("vector", lambda v, npp=npp, wi=wi: v.reciprocal(
                        rca[wi % 2][:, 0:npp], wnr[wi % 2][:, 0:npp, 128:129][:, :, 0]), [], 1)
                    pr.drain()
                    if wi >= 1:
                        b = wi - 1
                        sA = 0 if b == 0 else 2
                        dgc = 0
                        for j in range(2):
                            pr.emit("vector", lambda v, b=b, j=j, sA=sA: v.tensor_scalar(
                                diag[:, j, :], id32, a0t[:, 2 * b + j:2 * b + j + 1],
                                rca[b % 2][:, sA + j:sA + j + 1], ALU.mult, ALU.mult),
                                [("tensor", dg_guard)] if j == 0 else [], 1)
                            dgc = pr.emit("vector", lambda v, b=b, j=j, wi=wi: v.tensor_scalar(
                                diag[:, 2 + j, :], id32, a1t[:, 2 * b + j:2 * b + j + 1],
                                rca[wi % 2][:, j:j + 1], ALU.mult, ALU.mult), [], 1)
                        pend_blend = (b, sA, wi % 2, dgc, hh)
                if pend_blend is not None:
                    b, sA, wpar, dgc, hh_ = pend_blend
                    pend_blend = None
                    tw = 0
                    for j in range(2):
                        pr.emit("tensor", lambda t, b=b, j=j, sA=sA: t.matmul(
                            PS4[:, j * 128:(j + 1) * 128], wnr[b % 2][:, sA + j, 0:128],
                            diag[:, j, :], start=True, stop=False),
                            [("vector", dgc), ("scalar", woutev)] if j == 0 else [], 0)
                        tw = pr.emit("tensor", lambda t, b=b, j=j, wpar=wpar: t.matmul(
                            PS4[:, j * 128:(j + 1) * 128], wnr[wpar][:, j, 0:128],
                            diag[:, 2 + j, :], start=False, stop=True), [], 1)
                    dg_guard = tw
                    woutev = pr.emit("scalar", lambda s, hh_=hh_, b=b: s.activation(
                        wout8[:, hh_, b * 256:(b + 1) * 256], PS4[:, 0:256], ACT.Copy),
                        [("tensor", tw), ("tensor", womm_prev_grp)], 1)
                feed(64)
            for of in range(NIF):
                for s2_ in range(2):
                    ew = []
                    if of == 0 and s2_ == 0:
                        ew = [("tensor", qmm[hp * 8 + 7])]
                    elif of >= 2 and s2_ == 0:
                        ew = [("tensor", wo_grp)]
                    stage(woT_r[:, hp * 8 + s2_ * 4:hp * 8 + (s2_ + 1) * 4, of * 128:(of + 1) * 128],
                          wqb[of % 2][:, s2_ * 4:(s2_ + 1) * 4, :], (), ew)
                wo_cast = pr.cnt["scalar"]
                for ts2 in range(2):
                    gidx = of * 2 + ts2
                    slot = [PS1, PS2][gidx % 2]
                    sli = gidx % 2
                    wmm = [("scalar", wo_cast), ("scalar", woutev), ("scalar", sig_cnt)]
                    if gidx == 0:
                        wmm.append(("vector", rope[hp * 8 + 7]))
                    if gidx >= 2:
                        wmm.append(("scalar", wo_ev[gidx - 2]))
                    elif hp > 0:
                        wmm.append(("scalar", wo_ev[62 + gidx]))
                    mmc = 0
                    for gg in range(8):
                        mmc = pr.emit("tensor", lambda t, gg=gg, of=of, ts2=ts2, sl=slot: t.matmul(
                            sl[:, 0:512], wqb[of % 2][:, gg, :],
                            wout8[:, gg, ts2 * 512:(ts2 + 1) * 512],
                            start=(gg == 0), stop=(gg == 7)), wmm if gg == 0 else [],
                            1 if gg == 7 else 0)
                    wo_grp = mmc
                    womm_last = mmc
                    xsem = ["xsta", "xstb"][gidx % 2]
                    ev = pr.emit("scalar", lambda s, sli=sli, sl=slot: s.activation(
                        slabs[sli], sl[:, 0:512], ACT.Copy),
                        [("tensor", mmc), (xsem, pr.cnt[xsem]), ("xstore", precopy_done)], 1)
                    wo_ev[gidx] = ev
                    pr.dma(xsem, x2h_r[:, of, ts2 * 512:(ts2 + 1) * 512], slabs[sli],
                           [("scalar", ev)], accum=True)
            womm_prev_grp = womm_last
            wo_done = pr.cnt["scalar"]
            wo_ev = {62: wo_ev[62], 63: wo_ev[63]} if 63 in wo_ev else wo_ev
            if hp < 3:
                stage_wq(hp * 8 + 8, extra=[("tensor", womm_last)])
                feed = make_qfeeder(hp * 8 + 8, extra_first=[("scalar", wo_ev[63])])
                feed(64)

    ffn(xin_r, x1h_r, EXT)
    attn()
    ffn(x2h_r, out_r, OWN)

    total_st = {s: pr.cnt[s] for s in ("xstore", "xsta", "xstb")}

    with nc.Block() as block:
        def mk(e):
            def run(eng):
                hw = {}
                for fn, waits, inc, isem in pr.ops[e]:
                    for (we, wv) in waits:
                        if we == e:
                            continue
                        if wv > 0 and hw.get(we, -1) < wv:
                            eng.wait_ge(sems[we], wv)
                            hw[we] = wv
                    ins = fn(eng)
                    if inc:
                        ins.then_inc(sems[isem], inc)
                if e == "gpsimd":
                    for s, v in total_st.items():
                        if v:
                            eng.wait_ge(sems[s], v)
            return run
        block.sync(mk("sync"))
        block.gpsimd(mk("gpsimd"))
        block.scalar(mk("scalar"))
        block.vector(mk("vector"))
        block.tensor(mk("tensor"))
    return nc, ctx


def _rope_tables(pos):
    inv = 1.0 / (10000.0 ** (np.arange(0, DH, 2, dtype=np.float32) / np.float32(DH)))
    f = np.outer(pos.astype(np.float32), inv)
    cos = np.zeros((len(pos), DH), dtype=np.float32)
    sin = np.zeros((len(pos), DH), dtype=np.float32)
    cos[:, 0::2] = np.cos(f); cos[:, 1::2] = np.cos(f)
    sin[:, 0::2] = np.sin(f); sin[:, 1::2] = np.sin(f)
    return cos, sin


def kernel(L0_attn_x, L0_attn_Wq, L0_attn_Wk, L0_attn_Wv, L0_attn_Wo,
           L0_ffn_W_base, L0_ffn_b_base, L0_ffn_W_router, L0_ffn_A, L0_ffn_B):
    x = np.asarray(L0_attn_x, np.float32)
    B, S, _ = x.shape
    shared = dict(
        wbaseT=np.ascontiguousarray(np.asarray(L0_ffn_W_base, np.float32).T),
        wqT=np.ascontiguousarray(np.asarray(L0_attn_Wq, np.float32).T),
        woT=np.ascontiguousarray(np.asarray(L0_attn_Wo, np.float32).T),
        wkT=np.ascontiguousarray(np.asarray(L0_attn_Wk, np.float32).T),
        wvT=np.ascontiguousarray(np.asarray(L0_attn_Wv, np.float32).T),
        aT=np.ascontiguousarray(np.asarray(L0_ffn_A, np.float32).reshape(128, D).T),
        bstk=np.ascontiguousarray(np.transpose(np.asarray(L0_ffn_B, np.float32), (0, 2, 1)).reshape(128, D)),
        wrT=np.ascontiguousarray(np.asarray(L0_ffn_W_router, np.float32).T),
        bbc=np.ascontiguousarray(np.asarray(L0_ffn_b_base, np.float32).reshape(NIF, 128).T),
        ones1=np.ones((1, 128), np.float32),
        onesc=np.ones((128, 1), np.float32),
        eye=np.eye(128, dtype=np.float32),
    )
    lin = np.linspace(0.0, 1.0, 256, dtype=np.float32)
    nc, ctx = build()
    in_maps = []
    for c in range(8):
        b, qc = c // 4, c % 4
        q0 = qc * OWN
        lo, hi = q0 - 256, q0 + OWN + 256
        xp = np.zeros((EXT, D), np.float32)
        slo, shi = max(0, lo), min(S, hi)
        xp[slo - lo:shi - lo] = x[b, slo:shi]
        pos = np.clip(np.arange(lo, hi), 0, S - 1)
        cos, sin = _rope_tables(pos)
        s2 = np.empty((DH, EXT), np.float32)
        s2[0:64] = -sin.T[0:64]
        s2[64:128] = sin.T[64:128]
        vm = np.zeros((128, NKC), np.float32)
        for k in range(NKC):
            p0 = lo + k * 128
            vm[:, k] = 1.0 if (p0 >= 0 and p0 + 128 <= S) else 0.0
        a1 = np.empty((128, 8), np.float32)
        for qch in range(8):
            a1[:, qch] = lin[np.arange(128) + (qch % 2) * 128]
        a0 = (np.float32(1.0) - a1).astype(np.float32)
        if qc == 0:
            a0[:, 0:2] = 0.0
            a1[:, 0:2] = 1.0
        m = dict(shared)
        m.update(xin=np.ascontiguousarray(xp.T),
                 cosx=np.ascontiguousarray(cos.T), s2x=s2,
                 a0=a0, a1=a1, vm=vm)
        in_maps.append(m)
    res = run_bass_kernel_spmd(nc, in_maps, core_ids=list(range(8)))
    out = np.empty((B, S, D), np.float32)
    for c in range(8):
        b, qc = c // 4, c % 4
        out[b, qc * OWN:(qc + 1) * OWN, :] = res.results[c]["out"].T
    ctx.close()
    return out


# version 11
# speedup vs baseline: 1.2717x; 1.0100x over previous
import numpy as np
import concourse.bass as bass
import concourse.mybir as mybir
from concourse.bass_utils import run_bass_kernel_spmd
from contextlib import ExitStack

F32 = mybir.dt.float32
BF16 = mybir.dt.bfloat16
AX = mybir.AxisListType
ALU = mybir.AluOpType
ACT = mybir.ActivationFunctionType

D = 4096
NIF = 32
H = 32
DH = 128
OWN = 1024
EXT = 1536
NKC = EXT // 128
SCALE = 1.0 / float(np.sqrt(128.0))
AW = 49152
ENGS = ("sync", "gpsimd", "scalar", "vector", "tensor")


STREAM_ENG = {"wload0": "sync", "wload1": "sync", "wload2": "sync", "wmisc": "sync",
              "xload0": "gpsimd", "xload1": "gpsimd", "xload2": "gpsimd",
              "xload3": "gpsimd", "xload4": "gpsimd", "xload5": "gpsimd", "xstore": "gpsimd",
              "xsta": "gpsimd", "xstb": "gpsimd"}
SEMN = tuple(ENGS) + tuple(STREAM_ENG)


class Prog:
    def __init__(self):
        self.ops = {e: [] for e in ENGS}
        self.cnt = {s: 0 for s in SEMN}

    def emit(self, e, fn, waits=(), inc=0, isem=None):
        isem = isem or e
        self.ops[e].append((fn, tuple(waits), inc, isem))
        self.cnt[isem] += inc
        return self.cnt[isem]

    def dma(self, stream, out, in_, waits=(), accum=False):
        e = STREAM_ENG[stream]
        if accum:
            fn = lambda q, o=out, i=in_: q.dma_start(out=o, in_=i, accum_op=ALU.add)
        else:
            fn = lambda q, o=out, i=in_: q.dma_start(out=o, in_=i)
        return self.emit(e, fn, waits, 16, isem=stream)

    def drain(self):
        self.emit("vector", lambda v: v.drain(), (), 0)


class Arena:
    def __init__(self, t, off=0):
        self.t = t
        self.off = off

    def alloc(self, words):
        o = self.off
        self.off += words
        assert self.off <= AW, f"arena overflow {self.off} > {AW}"
        return o

    def f32(self, words, shape=None):
        o = self.alloc(words)
        ap = self.t[:, o:o + words]
        if shape is not None:
            ap = _resh(ap, shape)
        return ap

    def bf16(self, n, shape=None):
        o = self.alloc((n + 1) // 2)
        ap = self.t[:, o:o + (n + 1) // 2].bitcast(BF16)[:, 0:n]
        if shape is not None:
            ap = _resh(ap, shape)
        return ap

    def row(self, words):
        o = self.alloc(words)
        return self.t[0:1, o:o + words]


def _resh(ap, shape):
    names = [chr(97 + i) for i in range(len(shape))]
    return ap.rearrange("p (" + " ".join(names) + ") -> p " + " ".join(names),
                        **{n: s for n, s in zip(names, shape)})


def build():
    nc = bass.Bass("TRN2", target_bir_lowering=False)
    dt = lambda n, s, d=F32, k="ExternalInput": nc.dram_tensor(n, s, d, kind=k).ap()

    xin = dt("xin", [D, EXT])
    wbaseT = dt("wbaseT", [D, D])
    wqT = dt("wqT", [D, D])
    woT = dt("woT", [D, D])
    wkT = dt("wkT", [D, DH])
    wvT = dt("wvT", [D, DH])
    aT = dt("aT", [D, 128])
    bstk = dt("bstk", [128, D])
    wrT = dt("wrT", [D, 8])
    bbc = dt("bbc", [128, NIF])
    cosx_h = dt("cosx", [128, EXT])
    s2x_h = dt("s2x", [128, EXT])
    a0_h = dt("a0", [128, 8])
    a1_h = dt("a1", [128, 8])
    vm_h = dt("vm", [128, NKC])
    ones1_h = dt("ones1", [1, 128])
    onesc_h = dt("onesc", [128, 1])
    eye_h = dt("eye", [128, 128])
    out_h = dt("out", [D, OWN], F32, "ExternalOutput")
    x1h = dt("x1h", [D, EXT], F32, "Internal")
    x2h = dt("x2h", [D, OWN], F32, "Internal")

    xin_r = xin.rearrange("(c p) t -> p c t", p=128)
    x1h_r = x1h.rearrange("(c p) t -> p c t", p=128)
    x2h_r = x2h.rearrange("(c p) t -> p c t", p=128)
    out_r = out_h.rearrange("(c p) t -> p c t", p=128)
    wbaseT_r = wbaseT.rearrange("(c p) o -> p c o", p=128)
    wqT_r = wqT.rearrange("(c p) o -> p c o", p=128)
    woT_r = woT.rearrange("(c p) o -> p c o", p=128)
    wkT_r = wkT.rearrange("(c p) o -> p c o", p=128)
    wvT_r = wvT.rearrange("(c p) o -> p c o", p=128)
    aT_r = aT.rearrange("(c p) o -> p c o", p=128)
    wrT_r = wrT.rearrange("(c p) o -> p c o", p=128)

    ctx = ExitStack()
    E = ctx.enter_context
    arena_t = E(nc.sbuf_tensor("arena", [128, AW], F32))
    ar = Arena(arena_t)

    PS1 = E(nc.psum_tensor("PS1", [128, 1024], F32))
    PS2 = E(nc.psum_tensor("PS2", [128, 1024], F32))
    PS3 = E(nc.psum_tensor("PS3", [128, 1024], F32))
    PS4 = E(nc.psum_tensor("PS4", [128, 512], F32))
    PSM = E(nc.psum_tensor("PSM", [128, 512], F32))
    PS4b = PS4[:].bitcast(BF16)
    PS3v = PS3[:].rearrange("p (c t) -> p c t", c=4)

    sems = {e: E(nc.semaphore("s_" + e)) for e in SEMN}
    pr = Prog()

    def store_q():
        return [(s, pr.cnt[s]) for s in ("xstore", "xsta", "xstb") if pr.cnt[s]]

    xlnb = ar.bf16(NIF * EXT, (NIF, EXT))
    bbt = ar.f32(NIF)
    bbh2 = ar.f32(NIF)
    id32 = ar.f32(128)
    idb16 = ar.bf16(128)
    onesb = ar.bf16(2)
    vmb = ar.bf16(NKC)
    vmf = ar.f32(NKC)
    a0t = ar.f32(8)
    a1t = ar.f32(8)
    RO = ar.alloc(7 * 128)
    ones1 = arena_t[0:1, RO:RO + 128]
    murow = arena_t[0:1, RO + 128:RO + 256]
    msrow = arena_t[0:1, RO + 256:RO + 384]
    t1row = arena_t[0:1, RO + 384:RO + 512]
    varrow = arena_t[0:1, RO + 512:RO + 640]
    invsrow = arena_t[0:1, RO + 640:RO + 768]
    misrow = arena_t[0:1, RO + 768:RO + 896]
    PBASE = ar.off

    sa = Arena(arena_t, PBASE)
    eyetmp = sa.f32(128)
    onesctmp = sa.f32(1)
    c_eye = pr.dma("wmisc", eyetmp, eye_h)
    c_b = pr.dma("wmisc", bbt, bbc)
    c_vm = pr.dma("wmisc", vmf, vm_h)
    c_a0 = pr.dma("wmisc", a0t, a0_h)
    c_a1 = pr.dma("wmisc", a1t, a1_h)
    c_o1 = pr.dma("wmisc", ones1, ones1_h)
    c_oc = pr.dma("wmisc", onesctmp, onesc_h)
    c_id = pr.dma("wmisc", id32, eye_h)
    pr.emit("scalar", lambda s: s.activation(idb16, eyetmp, ACT.Copy), [("wmisc", c_id)], 1)
    pr.emit("scalar", lambda s: s.activation(onesb[:, 0:1], onesctmp, ACT.Copy), [("wmisc", c_id)], 1)
    setup_sc = pr.emit("scalar", lambda s: s.activation(vmb, vmf, ACT.Copy), [("wmisc", c_id)], 1)
    setup_ve = pr.emit("vector", lambda v: v.tensor_scalar(bbh2, bbt, 0.5, None, ALU.mult),
                       [("wmisc", c_id)], 1)

    def ln_prep(src_r, ntok, base, entry=(), hook=None):
        a = Arena(arena_t, base)
        xf = [a.f32(NIF * 128, (NIF, 128)) for _ in range(2)]
        xb = [a.bf16(NIF * 256, (NIF, 256)), a.bf16(NIF * 256, (NIF, 256))]
        nch = ntok // 128
        apply2 = [0] * nch
        rankc = [0] * nch
        stats = [0] * nch
        rowrd = [0] * nch
        s1l = [0] * nch
        for i in range(nch):
            par = i % 2
            xpar = i % 2
            ps_st = [PSM[0:1, 0:256], PS1[0:1, 0:256]][par]
            ps_iv = [PS4[:, 0:128], PS2[:, 0:128]][par]
            ps_mi = [PSM[:, 0:128], PS1[:, 0:128]][par]
            wld = list(entry) if i == 0 else ([("vector", apply2[i - 2])] if i >= 2 else [])
            wld = wld + store_q()
            sem = "xload%d" % xpar
            semb = "xload%d" % (2 + xpar)
            g0 = pr.dma(sem, xf[xpar][:, 0:16, :], src_r[:, 0:16, i * 128:(i + 1) * 128], wld)
            g = pr.dma(semb, xf[xpar][:, 16:32, :], src_r[:, 16:32, i * 128:(i + 1) * 128], [])
            wsc = [(sem, g0)]
            if i >= 2:
                wsc.append(("tensor", stats[i - 2]))
            pr.emit("scalar", lambda s, p=par, xp=xpar: s.activation(
                xb[p][:, 0:16, 0:128], xf[xp][:, 0:16, :], ACT.Copy), wsc, 1)
            sh0 = pr.emit("scalar", lambda s, p=par, xp=xpar: s.activation(
                xb[p][:, 0:16, 128:256], xf[xp][:, 0:16, :], ACT.Square), [], 1)
            pr.emit("scalar", lambda s, p=par, xp=xpar: s.activation(
                xb[p][:, 16:32, 0:128], xf[xp][:, 16:32, :], ACT.Copy), [(semb, g)], 1)
            s1l[i] = s1 = pr.emit("scalar", lambda s, p=par, xp=xpar: s.activation(
                xb[p][:, 16:32, 128:256], xf[xp][:, 16:32, :], ACT.Square), [], 1)
            wmm = [("scalar", sh0)]
            if i >= 2:
                wmm.append(("vector", apply2[i - 2]))
            for c in range(NIF):
                w_ = wmm if c == 0 else ([("scalar", s1)] if c == 16 else [])
                stats[i] = pr.emit("tensor", lambda t, c=c, p=par, ps=ps_st: t.matmul(
                    ps, onesb[:, 0:1], xb[p][:, c, :],
                    start=(c == 0), stop=(c == NIF - 1)), w_,
                    1 if c == NIF - 1 else 0)
            wv = [("tensor", stats[i])]
            if i >= 1:
                wv.append(("tensor", rankc[i - 1]))
            pr.emit("vector", lambda v, ps=ps_st: v.tensor_scalar(
                murow, ps[0:1, 0:128], 1.0 / D, None, ALU.mult), wv, 1)
            pr.emit("vector", lambda v, ps=ps_st: v.tensor_scalar(
                msrow, ps[0:1, 128:256], 1.0 / D, None, ALU.mult), [], 1)
            pr.drain()
            pr.emit("vector", lambda v: v.tensor_tensor(t1row, murow, murow, ALU.mult), [], 1)
            pr.drain()
            pr.emit("vector", lambda v: v.tensor_tensor(varrow, msrow, t1row, ALU.subtract), [], 1)
            pr.drain()
            rv = pr.emit("vector", lambda v: v.tensor_scalar(varrow, varrow, 1e-5, None, ALU.add), [], 1)
            si = pr.emit("scalar", lambda s: s.activation(t1row, varrow, ACT.Sqrt),
                         [("vector", rv)], 1)
            vr = pr.emit("vector", lambda v: v.reciprocal(invsrow, t1row), [("scalar", si)], 1)
            pr.drain()
            rowrd[i] = pr.emit("vector", lambda v: v.tensor_tensor(misrow, murow, invsrow, ALU.mult),
                               [], 1)
            pr.emit("tensor", lambda t, ps=ps_iv: t.matmul(ps, ones1, invsrow, start=True, stop=True),
                    [("vector", rowrd[i])], 1)
            rankc[i] = pr.emit("tensor", lambda t, ps=ps_mi: t.matmul(
                ps, ones1, misrow, start=True, stop=True), [], 1)
            pr.emit("vector", lambda v, xp=xpar, ps=ps_iv: v.tensor_tensor(
                xf[xp], xf[xp], ps.unsqueeze(1).broadcast_to([128, NIF, 128]), ALU.mult),
                [("tensor", rankc[i])], 1)
            pr.drain()
            apply2[i] = pr.emit("vector", lambda v, xp=xpar, i=i, ps=ps_mi: v.tensor_tensor(
                xlnb[:, :, i * 128:(i + 1) * 128], xf[xp],
                ps.unsqueeze(1).broadcast_to([128, NIF, 128]), ALU.subtract), [], 1)
            if hook is not None:
                hook(i, apply2)
        return {"vector": apply2[nch - 1], "tensor": max(stats[nch - 1], rankc[nch - 1]),
                "scalar": s1l[nch - 1]}

    def make_stager(wstage):
        nb = len(wstage)
        state = {"i": 0, "cast": {}}

        def load_cast(src_ap, dst_bf_ap, extra_dma_waits=(), extra_cast_waits=()):
            i = state["i"]
            state["i"] += 1
            wdma = list(extra_dma_waits)
            if i - nb in state["cast"]:
                wdma.append(("scalar", state["cast"][i - nb]))
            sem = "wload%d" % (i % nb)
            g = pr.dma(sem, wstage[i % nb], src_ap, wdma)
            wc = [(sem, g)] + list(extra_cast_waits)
            c = pr.emit("scalar", lambda s, si=i % nb, d=dst_bf_ap: s.activation(d, wstage[si], ACT.Copy), wc, 1)
            state["cast"][i] = c
            return c
        return load_cast

    def ffn(src_r, dst_r, ntok):
        snap0 = [("scalar", pr.cnt["scalar"]), ("vector", pr.cnt["vector"]),
                 ("tensor", pr.cnt["tensor"])]
        a0a = Arena(arena_t, PBASE)
        ab16 = a0a.bf16(NIF * 128, (NIF, 128))
        bb16 = a0a.bf16(D)
        rb16 = a0a.bf16(NIF * 8, (NIF, 8))
        hwb = a0a.bf16(EXT)
        PREP_B = a0a.off
        st_a = Arena(arena_t, PREP_B)
        atmp = st_a.f32(NIF * 128, (NIF, 128))
        bstmp = st_a.f32(D)
        rtmp = st_a.f32(NIF * 8, (NIF, 8))
        c_a = pr.dma("wmisc", atmp, aT_r, snap0)
        c_bs = pr.dma("wmisc", bstmp, bstk, [])
        c_r = pr.dma("wmisc", rtmp, wrT_r, [])
        pr.emit("scalar", lambda s: s.activation(ab16[:], atmp, ACT.Copy), [("wmisc", c_r)] + snap0, 1)
        pr.emit("scalar", lambda s: s.activation(bb16[:], bstmp, ACT.Copy), [("wmisc", c_r)], 1)
        sm_cast = pr.emit("scalar", lambda s: s.activation(rb16[:], rtmp, ACT.Copy), [("wmisc", c_r)], 1)
        snap1 = [("scalar", sm_cast), ("vector", pr.cnt["vector"]), ("tensor", pr.cnt["tensor"])]
        ga = Arena(arena_t, PREP_B)
        gxb = ga.f32(512)
        lg = arena_t[0:8, ga.alloc(512) + 0:ga.off]
        ltm = ga.f32(32, (4, 8))
        gate = ga.f32(32, (4, 8))
        eq1 = ga.f32(32, (4, 8))
        eq2 = ga.f32(32, (4, 8))
        mskb = ga.f32(32)
        m1c = ga.f32(4)
        m2c = ga.f32(4)
        p1c = ga.f32(4)
        p2c = ga.f32(4)
        OFL_B = ga.off

        NTS = ntok // 512
        st_aph = {"hw": 0, "hw_hist": [0] * (NTS + 2), "se": 0, "sg": 0}

        def aphase(ts, wait_v):
            cols = slice(ts * 512, (ts + 1) * 512)
            hslot = PS3[:, (ts % 2) * 512:(ts % 2) * 512 + 512]
            wmm = [("vector", wait_v), ("scalar", sm_cast)]
            if ts >= 2:
                wmm.append(("vector", st_aph["hw_hist"][ts - 2]))
            for c in range(NIF):
                pr.emit("tensor", lambda t, c=c, h=hslot, cc=cols: t.matmul(
                    h, ab16[:, c, :], xlnb[:, c, cc], start=(c == 0), stop=(c == NIF - 1)),
                    wmm if c == 0 else [], 0)
            wrt = [("scalar", st_aph["se"])] if ts > 0 else [("vector", wait_v)]
            h_cnt = 0
            for c in range(NIF):
                h_cnt = pr.emit("tensor", lambda t, c=c, cc=cols: t.matmul(
                    PS1[0:8, 512:1024], rb16[:, c, :], xlnb[:, c, cc],
                    start=(c == 0), stop=(c == NIF - 1)), wrt if c == 0 else [],
                    1 if c == NIF - 1 else 0)
            se = pr.emit("scalar", lambda s: s.activation(lg, PS1[0:8, 512:1024], ACT.Copy),
                         [("tensor", h_cnt)], 1)
            st_aph["se"] = se
            tp = 0
            for qc in range(4):
                tp = pr.emit("tensor", lambda t, qc=qc: t.transpose(
                    PS2[:, 512 + qc * 8:512 + (qc + 1) * 8], lg[0:8, qc * 128:(qc + 1) * 128],
                    id32[0:8, 0:8]),
                    ([("scalar", se), ("scalar", st_aph["sg"])] if qc == 0 else []),
                    1 if qc == 3 else 0)
            se2 = pr.emit("scalar", lambda s: s.activation(
                ltm, _resh(PS2[:, 512:544], (4, 8)), ACT.Copy), [("tensor", tp)], 1)
            pr.emit("vector", lambda v: v.reduce_max(m1c, ltm, axis=AX.X), [("scalar", se2)], 1)
            pr.drain()
            for qc in range(4):
                pr.emit("vector", lambda v, qc=qc: v.tensor_scalar(
                    eq1[:, qc, :], ltm[:, qc, :], m1c[:, qc:qc + 1], None, ALU.is_equal), [], 1)
            pr.drain()
            pr.emit("vector", lambda v: v.scalar_tensor_tensor(
                mskb, eq1.rearrange("p c e -> p (c e)"), -30.0, ltm.rearrange("p c e -> p (c e)"),
                ALU.mult, ALU.add), [], 1)
            pr.drain()
            vm2 = pr.emit("vector", lambda v: v.reduce_max(m2c, _resh(mskb, (4, 8)), axis=AX.X), [], 1)
            sp = 0
            for qc in range(4):
                sp = pr.emit("scalar", lambda s, qc=qc: s.activation(
                    p1c[:, qc:qc + 1], m2c[:, qc:qc + 1], ACT.Sigmoid, bias=m1c[:, qc:qc + 1], scale=-1.0),
                    [("vector", vm2)] if qc == 0 else [], 1)
                sp = pr.emit("scalar", lambda s, qc=qc: s.activation(
                    p2c[:, qc:qc + 1], m1c[:, qc:qc + 1], ACT.Sigmoid, bias=m2c[:, qc:qc + 1], scale=-1.0),
                    [], 1)
            for qc in range(4):
                pr.emit("vector", lambda v, qc=qc: v.tensor_scalar(
                    gate[:, qc, :], eq1[:, qc, :], p1c[:, qc:qc + 1], 2.0, ALU.mult, ALU.mult),
                    [("scalar", sp)] if qc == 0 else [], 1)
            for qc in range(4):
                pr.emit("vector", lambda v, qc=qc: v.tensor_scalar(
                    eq2[:, qc, :], _resh(mskb, (4, 8))[:, qc, :], m2c[:, qc:qc + 1], None, ALU.is_equal),
                    [], 1)
            pr.drain()
            for qc in range(4):
                pr.emit("vector", lambda v, qc=qc: v.tensor_scalar(
                    eq2[:, qc, :], eq2[:, qc, :], p2c[:, qc:qc + 1], 2.0, ALU.mult, ALU.mult), [], 1)
            pr.drain()
            pr.emit("vector", lambda v: v.tensor_tensor(
                gate.rearrange("p c e -> p (c e)"), gate.rearrange("p c e -> p (c e)"),
                eq2.rearrange("p c e -> p (c e)"), ALU.add), [], 1)
            pr.drain()
            vgx = pr.emit("vector", lambda v: v.tensor_copy(
                _resh(gxb, (4, 8, 16)), gate.unsqueeze(3).broadcast_to([128, 4, 8, 16])), [], 1)
            tg = 0
            for qc in range(4):
                tg = pr.emit("tensor", lambda t, qc=qc: t.transpose(
                    PS2[:, 512 + qc * 128:512 + (qc + 1) * 128], gxb[:, qc * 128:(qc + 1) * 128], id32),
                    [("vector", vgx), ("scalar", se2)] if qc == 0 else [], 1 if qc == 3 else 0)
            sg = pr.emit("scalar", lambda s: s.activation(gxb, PS2[:, 512:1024], ACT.Copy),
                         [("tensor", tg)], 1)
            st_aph["sg"] = sg
            st_aph["hw_hist"][ts] = st_aph["hw"] = pr.emit(
                "vector", lambda v, h=hslot, cc=cols: v.tensor_tensor(
                    hwb[:, cc], h, gxb, ALU.mult), [("scalar", sg)], 1)

        def hook(i, ap2):
            if i % 4 == 3 and (i // 4) < NTS:
                aphase(i // 4, ap2[i])

        pd = ln_prep(src_r, ntok, OFL_B, snap1, hook=hook)
        prep_v, prep_t = pd["vector"], pd["tensor"]
        hw_cnt = st_aph["hw"]
        a = Arena(arena_t, OFL_B)
        wstage = [a.f32(8 * 128, (8, 128)) for _ in range(3)]
        wb = [a.bf16(NIF * 128, (NIF, 128)), a.bf16(NIF * 128, (NIF, 128))]
        sl_sizes = []
        t0 = 0
        while t0 < ntok:
            n = min(512, ntok - t0)
            sl_sizes.append((t0, n))
            t0 += n
        slabs = [[a.f32(n) for (s0, n) in sl_sizes] for _ in range(2)]
        stage = make_stager(wstage)

        grp_cnt = {}
        evac_cnt = {}
        first_slab = True
        for of in range(NIF):
            extra = [("vector", prep_v)] if of == 0 else []
            ecast = [("tensor", prep_t)] if of == 0 else []
            if of >= 2:
                ecast = [("tensor", grp_cnt[(of - 2, len(sl_sizes) - 1)])]
            for s4 in range(4):
                stage(wbaseT_r[:, s4 * 8:(s4 + 1) * 8, of * 128:(of + 1) * 128],
                      wb[of % 2][:, s4 * 8:(s4 + 1) * 8, :],
                      extra if s4 == 0 else (), ecast if s4 == 0 else ())
            cast_done = pr.cnt["scalar"]
            xsem = "xload%d" % (of % 2)
            for tsl, (t0, n) in enumerate(sl_sizes):
                wld = [("vector", prep_v)] if first_slab else []
                first_slab = False
                wld = wld + store_q()
                gld = pr.dma(xsem, slabs[of % 2][tsl], src_r[:, of, t0:t0 + n], wld)
            for tsl, (t0, n) in enumerate(sl_sizes):
                gidx = of * len(sl_sizes) + tsl
                pspair = [PS1, PS2][(gidx // 2) % 2]
                off = (gidx % 2) * 512
                slot = pspair[:, off:off + n]
                wmm = [("scalar", cast_done), ("vector", hw_cnt)]
                if gidx >= 4:
                    wmm.append(("vector", evac_cnt[gidx - 4]))
                for c in range(NIF):
                    pr.emit("tensor", lambda t, c=c, sl=slot, of=of, t0=t0, n=n: t.matmul(
                        sl, wb[of % 2][:, c, :], xlnb[:, c, t0:t0 + n],
                        start=(c == 0), stop=False), wmm if c == 0 else [], 0)
                gmm = pr.emit("tensor", lambda t, sl=slot, of=of, t0=t0, n=n: t.matmul(
                    sl, bb16[:, of * 128:(of + 1) * 128], hwb[:, t0:t0 + n],
                    start=False, stop=True), [], 1)
                grp_cnt[(of, tsl)] = gmm
                pr.emit("vector", lambda v, of=of, tsl=tsl: v.tensor_scalar(
                    slabs[of % 2][tsl], slabs[of % 2][tsl], 1.0, bbh2[:, of:of + 1],
                    ALU.mult, ALU.add), [(xsem, gld)], 1)
                pr.drain()
                ev = pr.emit("vector", lambda v, of=of, tsl=tsl, sl=slot: v.scalar_tensor_tensor(
                    slabs[of % 2][tsl], sl, 0.5, slabs[of % 2][tsl],
                    ALU.mult, ALU.add), [("tensor", gmm)], 1)
                evac_cnt[gidx] = ev
                pr.dma("xstore", dst_r[:, of, t0:t0 + n], slabs[of % 2][tsl], [("vector", ev)])

    def attn():
        snap0 = [("scalar", pr.cnt["scalar"]), ("vector", pr.cnt["vector"]),
                 ("tensor", pr.cnt["tensor"])]
        pd = ln_prep(x1h_r, EXT, PBASE, snap0)
        prep_v, prep_t = pd["vector"], pd["tensor"]
        a = Arena(arena_t, PBASE)
        cosx = a.f32(EXT)
        s2x = a.f32(EXT)
        krb = a.bf16(EXT)
        vext = a.bf16(NKC * 132, (NKC, 132))
        wstage = [a.f32(4 * 128, (4, 128)) for _ in range(3)]
        wqb = [a.bf16(NIF * 128, (NIF, 128)), a.bf16(NIF * 128, (NIF, 128))]
        qt1 = a.f32(1024)
        qt2 = a.f32(1024)
        qrb = [a.bf16(1024), a.bf16(1024)]
        vbst = a.bf16(1024)
        sgb = [a.bf16(4 * 512, (4, 512)), a.bf16(4 * 512, (4, 512))]
        wnr = [a.f32(4 * 129, (4, 129)), a.f32(4 * 129, (4, 129))]
        rca = [a.f32(4), a.f32(4)]
        wout8 = a.bf16(8 * 1024, (8, 1024))
        slabs = [a.f32(512), a.f32(512)]
        diag = a.f32(4 * 128, (4, 128))
        stage = make_stager(wstage)

        cl = pr.dma("wmisc", cosx, cosx_h, [("vector", prep_v)])
        cos_l = pr.dma("wmisc", s2x, s2x_h, [])
        pr.drain()
        pr.emit("vector", lambda v: v.memset(vext[:, :, 129:132], 0.0), [], 1)

        for s8 in range(8):
            stage(wkT_r[:, s8 * 4:(s8 + 1) * 4, :], wqb[0][:, s8 * 4:(s8 + 1) * 4, :],
                  [("vector", prep_v)] if s8 == 0 else (), [("tensor", prep_t)] if s8 == 0 else ())
        for s8 in range(8):
            stage(wvT_r[:, s8 * 4:(s8 + 1) * 4, :], wqb[1][:, s8 * 4:(s8 + 1) * 4, :])
        kv_cast = pr.cnt["scalar"]
        kvw = pr.cnt["wmisc"]

        kv_ev = 0
        vmm = 0
        kev = 0
        for grp in range(3):
            t0 = grp * 512
            n = 512
            cols = slice(t0, t0 + n)
            off = (grp % 2) * 512
            wmm = [("scalar", kv_cast), ("vector", prep_v)]
            if grp >= 1:
                wmm.append(("vector", kev))
            kmm = 0
            for c in range(NIF):
                kmm = pr.emit("tensor", lambda t, c=c, cc=cols, off=off: t.matmul(
                    PS1[:, off:off + 512], wqb[0][:, c, :], xlnb[:, c, cc],
                    start=(c == 0), stop=(c == NIF - 1)),
                    wmm if c == 0 else [], 1 if c == NIF - 1 else 0)
            for c in range(NIF):
                vmm = pr.emit("tensor", lambda t, c=c, cc=cols, off=off: t.matmul(
                    PS2[:, off:off + 512], wqb[1][:, c, :], xlnb[:, c, cc],
                    start=(c == 0), stop=(c == NIF - 1)),
                    [], 1 if c == NIF - 1 else 0)
            pr.emit("vector", lambda v, cc=cols, off=off: v.tensor_tensor(
                qt1[:, 0:512], PS1[:, off:off + 512], cosx[:, cc], ALU.mult),
                [("tensor", kmm), ("wmisc", kvw)], 1)
            pr.emit("vector", lambda v, cc=cols, off=off: v.tensor_tensor(
                qt2[0:64, 0:512], PS1[64:128, off:off + 512], s2x[0:64, cc], ALU.mult), [], 1)
            pr.emit("vector", lambda v, cc=cols, off=off: v.tensor_tensor(
                qt2[64:128, 0:512], PS1[0:64, off:off + 512], s2x[64:128, cc], ALU.mult), [], 1)
            pr.drain()
            kev = pr.emit("vector", lambda v, cc=cols: v.tensor_tensor(
                krb[:, cc], qt1[:, 0:512], qt2[:, 0:512], ALU.add), [], 1)
            sv = pr.emit("scalar", lambda s, off=off: s.activation(
                vbst[:, 0:512], PS2[:, off:off + 512], ACT.Copy), [("tensor", vmm)], 1)
            tv = 0
            for k2 in range(4):
                tv = pr.emit("tensor", lambda t, k2=k2: t.transpose(
                    PS4b[:, k2 * 128:(k2 + 1) * 128], vbst[:, k2 * 128:(k2 + 1) * 128],
                    idb16), [("scalar", sv), ("vector", kv_ev)] if k2 == 0 else [],
                    1 if k2 == 3 else 0)
            kcg0 = t0 // 128
            pr.emit("vector", lambda v, kcg0=kcg0: v.tensor_tensor(
                vext[:, kcg0:kcg0 + 4, 0:128], _resh(PS4b, (8, 128))[:, 0:4, :],
                vmb[:, kcg0:kcg0 + 4].unsqueeze(2).broadcast_to([128, 4, 128]), ALU.mult),
                [("tensor", tv)], 1)
            kv_ev = pr.emit("vector", lambda v, kcg0=kcg0: v.tensor_copy(
                vext[:, kcg0:kcg0 + 4, 128:129], vmb[:, kcg0:kcg0 + 4].unsqueeze(2)), [], 1)

        pc = pr.cnt["xstore"]
        for i2 in range(64):
            of, ts2 = i2 // 2, i2 % 2
            par = i2 % 2
            w = [("xstore", pc), ("vector", prep_v)] if i2 == 0 else ([("xstore", pr.cnt["xstore"])] if i2 >= 2 else [])
            gl = pr.dma("xload%d" % par, slabs[par],
                        x1h_r[:, of, 256 + ts2 * 512:256 + (ts2 + 1) * 512], w)
            pr.dma("xstore", x2h_r[:, of, ts2 * 512:(ts2 + 1) * 512], slabs[par],
                   [("xload%d" % par, gl)])
        precopy_done = pr.cnt["xstore"]

        qmm = [0] * (H + 2)
        rope = [0] * (H + 2)
        wqc = [0] * (H + 2)
        sig_cnt = 0
        sig2_prev = 0
        av_hist = {}
        wn_last = kv_ev
        woutev = 0
        womm_prev_grp = max(vmm, 1)
        wo_ev = {}
        wo_done = 0
        dg_guard = 0
        pend_blend = None
        sig_kc = [0, 0, 0, 0]

        wq_sub = {}

        def stage_wq(h, extra=()):
            subs = []
            for s8 in range(8):
                ew = []
                if h < 2 and s8 == 0:
                    ew = [("tensor", vmm)]
                elif s8 == 0:
                    ew = [("tensor", qmm[h - 2])]
                ew = ew + (list(extra) if s8 == 0 else [])
                subs.append(stage(wqT_r[:, s8 * 4:(s8 + 1) * 4, h * 128:(h + 1) * 128],
                            wqb[h % 2][:, s8 * 4:(s8 + 1) * 4, :], (), ew))
            wq_sub[h] = subs
            wqc[h] = pr.cnt["scalar"]

        def make_qfeeder(h, extra_first=()):
            stepctr = {"n": 0}

            def feed(k):
                while k > 0 and stepctr["n"] < 64:
                    j = stepctr["n"]
                    hf, c = j // NIF, j % NIF
                    w = []
                    if hf == 0 and c % 4 == 0:
                        w.append(("scalar", wq_sub[h][c // 4]))
                    if j == 0:
                        w += [("vector", prep_v)] + list(extra_first)
                        if h >= 1:
                            w.append(("vector", rope[h - 1]))
                        if h % 8 == 0 and h > 0:
                            w.append(("scalar", wo_done))
                        if h == 0:
                            w.append(("vector", kv_ev))
                    qmm[h] = pr.emit("tensor", lambda t, c=c, par=h % 2, hf=hf: t.matmul(
                        PS1[:, hf * 512:(hf + 1) * 512], wqb[par][:, c, :],
                        xlnb[:, c, 256 + hf * 512:768 + hf * 512],
                        start=(c == 0), stop=(c == NIF - 1)), w,
                        1 if (c == NIF - 1 and hf == 1) else 0)
                    stepctr["n"] += 1
                    k -= 1
            return feed

        stage_wq(0)
        feed = make_qfeeder(0)
        feed(64)
        for hp in range(4):
            womm_last = 0
            for hh in range(8):
                h = hp * 8 + hh
                par = h % 2
                pr.emit("vector", lambda v, par=par: v.tensor_tensor(
                    qt1, PS1[:, 0:1024], cosx[:, 256:1280], ALU.mult), [("tensor", qmm[h])], 1)
                pr.emit("vector", lambda v: v.tensor_tensor(qt2[0:64, :], PS1[64:128, 0:1024],
                                                            s2x[0:64, 256:1280], ALU.mult), [], 1)
                pr.emit("vector", lambda v: v.tensor_tensor(qt2[64:128, :], PS1[0:64, 0:1024],
                                                            s2x[64:128, 256:1280], ALU.mult), [], 1)
                pr.drain()
                rope[h] = pr.emit("vector", lambda v, par=par: v.tensor_tensor(
                    qrb[par], qt1, qt2, ALU.add), [], 1)
                if hh < 7:
                    stage_wq(h + 1)
                    feed = make_qfeeder(h + 1)
                else:
                    feed = lambda k: None
                for wi in range(5):
                    q0w = max(0, wi * 256 - 256)
                    q1w = min(1024, wi * 256 + 256)
                    qn = q1w - q0w
                    npp = qn // 128
                    kc0 = wi * 2
                    sp = wi % 2
                    if pend_blend is not None:
                        b, sA, wpar, dgc, hh_ = pend_blend
                        pend_blend = None
                        tw = 0
                        for j in range(2):
                            pr.emit("tensor", lambda t, b=b, j=j, sA=sA: t.matmul(
                                PS4[:, j * 128:(j + 1) * 128], wnr[b % 2][:, sA + j, 0:128],
                                diag[:, j, :], start=True, stop=False),
                                [("vector", dgc), ("scalar", woutev)] if j == 0 else [], 0)
                            tw = pr.emit("tensor", lambda t, b=b, j=j, wpar=wpar: t.matmul(
                                PS4[:, j * 128:(j + 1) * 128], wnr[wpar][:, j, 0:128],
                                diag[:, 2 + j, :], start=False, stop=True), [], 1)
                        dg_guard = tw
                        woutev = pr.emit("scalar", lambda s, hh_=hh_, b=b: s.activation(
                            wout8[:, hh_, b * 256:(b + 1) * 256], PS4[:, 0:256], ACT.Copy),
                            [("tensor", tw), ("tensor", womm_prev_grp)], 1)
                    sgw = []
                    if (h, wi - 2) in av_hist:
                        sgw = [("tensor", av_hist[(h, wi - 2)])]
                    elif wi < 2 and h >= 1:
                        sgw = [("tensor", av_hist[(h - 1, 4 - wi)])]
                    for kc in range(4):
                        wsc = []
                        if kc == 0:
                            wsc = [("vector", rope[h]), ("scalar", sig_cnt)]
                        elif kc >= 2:
                            wsc = [("scalar", sig_kc[kc - 2])]
                        smm = pr.emit("tensor", lambda t, kc=kc, q0w=q0w, qn=qn, kc0=kc0, par=par: t.matmul(
                            PS2[:, (kc % 2) * 512:(kc % 2) * 512 + qn],
                            krb[:, (kc0 + kc) * 128:(kc0 + kc + 1) * 128],
                            qrb[par][:, q0w:q0w + qn], start=True, stop=True), wsc, 1)
                        sig_kc[kc] = pr.emit("scalar", lambda s, qn=qn, sp=sp, kc=kc: s.activation(
                            sgb[sp][:, kc, 0:qn],
                            PS2[:, (kc % 2) * 512:(kc % 2) * 512 + qn],
                            ACT.Sigmoid, scale=SCALE),
                            [("tensor", smm)] + (sgw if kc == 0 else []), 1)
                        if kc == 1:
                            feed(6)
                    sig_cnt = pr.cnt["scalar"]
                    sig2_prev = sig_cnt
                    feed(6)
                    avm = 0
                    for p in range(npp):
                        for kc in range(4):
                            st = (p % 2 == 0 and kc == 0)
                            sp_ = (p % 2 == 1 and kc == 3)
                            avm = pr.emit("tensor", lambda t, p=p, kc=kc, kc0=kc0, st=st, sp_=sp_, sp=sp: t.matmul(
                                PS3[:, p * 256:p * 256 + 130], sgb[sp][:, kc, p * 128:(p + 1) * 128],
                                vext[:, kc0 + kc, 0:130], start=st, stop=sp_),
                                ([("scalar", sig_cnt), ("vector", wn_last)] if (p == 0 and kc == 0) else []),
                                1 if (p == npp - 1 and kc == 3) else 0)
                    av_hist[(h, wi)] = avm
                    feed(4)
                    wn_last = pr.emit("vector", lambda v, npp=npp, wi=wi: v.tensor_copy(
                        wnr[wi % 2][:, 0:npp, :], PS3v[:, 0:npp, 0:129]), [("tensor", avm)], 1)
                    pr.drain()
                    pr.emit("vector", lambda v, npp=npp, wi=wi: v.reciprocal(
                        rca[wi % 2][:, 0:npp], wnr[wi % 2][:, 0:npp, 128:129][:, :, 0]), [], 1)
                    pr.drain()
                    if wi >= 1:
                        b = wi - 1
                        sA = 0 if b == 0 else 2
                        dgc = 0
                        for j in range(2):
                            pr.emit("vector", lambda v, b=b, j=j, sA=sA: v.tensor_scalar(
                                diag[:, j, :], id32, a0t[:, 2 * b + j:2 * b + j + 1],
                                rca[b % 2][:, sA + j:sA + j + 1], ALU.mult, ALU.mult),
                                [("tensor", dg_guard)] if j == 0 else [], 1)
                            dgc = pr.emit("vector", lambda v, b=b, j=j, wi=wi: v.tensor_scalar(
                                diag[:, 2 + j, :], id32, a1t[:, 2 * b + j:2 * b + j + 1],
                                rca[wi % 2][:, j:j + 1], ALU.mult, ALU.mult), [], 1)
                        pend_blend = (b, sA, wi % 2, dgc, hh)
                if pend_blend is not None:
                    b, sA, wpar, dgc, hh_ = pend_blend
                    pend_blend = None
                    tw = 0
                    for j in range(2):
                        pr.emit("tensor", lambda t, b=b, j=j, sA=sA: t.matmul(
                            PS4[:, j * 128:(j + 1) * 128], wnr[b % 2][:, sA + j, 0:128],
                            diag[:, j, :], start=True, stop=False),
                            [("vector", dgc), ("scalar", woutev)] if j == 0 else [], 0)
                        tw = pr.emit("tensor", lambda t, b=b, j=j, wpar=wpar: t.matmul(
                            PS4[:, j * 128:(j + 1) * 128], wnr[wpar][:, j, 0:128],
                            diag[:, 2 + j, :], start=False, stop=True), [], 1)
                    dg_guard = tw
                    woutev = pr.emit("scalar", lambda s, hh_=hh_, b=b: s.activation(
                        wout8[:, hh_, b * 256:(b + 1) * 256], PS4[:, 0:256], ACT.Copy),
                        [("tensor", tw), ("tensor", womm_prev_grp)], 1)
                feed(64)
            for of in range(NIF):
                for s2_ in range(2):
                    ew = []
                    if of == 0 and s2_ == 0:
                        ew = [("tensor", qmm[hp * 8 + 7])]
                    elif of >= 2 and s2_ == 0:
                        ew = [("tensor", wo_grp)]
                    stage(woT_r[:, hp * 8 + s2_ * 4:hp * 8 + (s2_ + 1) * 4, of * 128:(of + 1) * 128],
                          wqb[of % 2][:, s2_ * 4:(s2_ + 1) * 4, :], (), ew)
                wo_cast = pr.cnt["scalar"]
                for ts2 in range(2):
                    gidx = of * 2 + ts2
                    slot = [PS1, PS2][gidx % 2]
                    sli = gidx % 2
                    wmm = [("scalar", wo_cast), ("scalar", woutev), ("scalar", sig_cnt)]
                    if gidx == 0:
                        wmm.append(("vector", rope[hp * 8 + 7]))
                    if gidx >= 2:
                        wmm.append(("scalar", wo_ev[gidx - 2]))
                    elif hp > 0:
                        wmm.append(("scalar", wo_ev[62 + gidx]))
                    mmc = 0
                    for gg in range(8):
                        mmc = pr.emit("tensor", lambda t, gg=gg, of=of, ts2=ts2, sl=slot: t.matmul(
                            sl[:, 0:512], wqb[of % 2][:, gg, :],
                            wout8[:, gg, ts2 * 512:(ts2 + 1) * 512],
                            start=(gg == 0), stop=(gg == 7)), wmm if gg == 0 else [],
                            1 if gg == 7 else 0)
                    wo_grp = mmc
                    womm_last = mmc
                    xsem = ["xsta", "xstb"][gidx % 2]
                    ev = pr.emit("scalar", lambda s, sli=sli, sl=slot: s.activation(
                        slabs[sli], sl[:, 0:512], ACT.Copy),
                        [("tensor", mmc), (xsem, pr.cnt[xsem]), ("xstore", precopy_done)], 1)
                    wo_ev[gidx] = ev
                    pr.dma(xsem, x2h_r[:, of, ts2 * 512:(ts2 + 1) * 512], slabs[sli],
                           [("scalar", ev)], accum=True)
            womm_prev_grp = womm_last
            wo_done = pr.cnt["scalar"]
            wo_ev = {62: wo_ev[62], 63: wo_ev[63]} if 63 in wo_ev else wo_ev
            if hp < 3:
                stage_wq(hp * 8 + 8, extra=[("tensor", womm_last)])
                feed = make_qfeeder(hp * 8 + 8, extra_first=[("scalar", wo_ev[63])])
                feed(64)

    ffn(xin_r, x1h_r, EXT)
    attn()
    ffn(x2h_r, out_r, OWN)

    total_st = {s: pr.cnt[s] for s in ("xstore", "xsta", "xstb")}

    with nc.Block() as block:
        def mk(e):
            def run(eng):
                hw = {}
                for fn, waits, inc, isem in pr.ops[e]:
                    for (we, wv) in waits:
                        if we == e:
                            continue
                        if wv > 0 and hw.get(we, -1) < wv:
                            eng.wait_ge(sems[we], wv)
                            hw[we] = wv
                    ins = fn(eng)
                    if inc:
                        ins.then_inc(sems[isem], inc)
                if e == "gpsimd":
                    for s, v in total_st.items():
                        if v:
                            eng.wait_ge(sems[s], v)
            return run
        block.sync(mk("sync"))
        block.gpsimd(mk("gpsimd"))
        block.scalar(mk("scalar"))
        block.vector(mk("vector"))
        block.tensor(mk("tensor"))
    return nc, ctx


def _rope_tables(pos):
    inv = 1.0 / (10000.0 ** (np.arange(0, DH, 2, dtype=np.float32) / np.float32(DH)))
    f = np.outer(pos.astype(np.float32), inv)
    cos = np.zeros((len(pos), DH), dtype=np.float32)
    sin = np.zeros((len(pos), DH), dtype=np.float32)
    cos[:, 0::2] = np.cos(f); cos[:, 1::2] = np.cos(f)
    sin[:, 0::2] = np.sin(f); sin[:, 1::2] = np.sin(f)
    return cos, sin


def kernel(L0_attn_x, L0_attn_Wq, L0_attn_Wk, L0_attn_Wv, L0_attn_Wo,
           L0_ffn_W_base, L0_ffn_b_base, L0_ffn_W_router, L0_ffn_A, L0_ffn_B):
    x = np.asarray(L0_attn_x, np.float32)
    B, S, _ = x.shape
    shared = dict(
        wbaseT=np.ascontiguousarray(np.asarray(L0_ffn_W_base, np.float32).T),
        wqT=np.ascontiguousarray(np.asarray(L0_attn_Wq, np.float32).T),
        woT=np.ascontiguousarray(np.asarray(L0_attn_Wo, np.float32).T),
        wkT=np.ascontiguousarray(np.asarray(L0_attn_Wk, np.float32).T),
        wvT=np.ascontiguousarray(np.asarray(L0_attn_Wv, np.float32).T),
        aT=np.ascontiguousarray(np.asarray(L0_ffn_A, np.float32).reshape(128, D).T),
        bstk=np.ascontiguousarray(np.transpose(np.asarray(L0_ffn_B, np.float32), (0, 2, 1)).reshape(128, D)),
        wrT=np.ascontiguousarray(np.asarray(L0_ffn_W_router, np.float32).T),
        bbc=np.ascontiguousarray(np.asarray(L0_ffn_b_base, np.float32).reshape(NIF, 128).T),
        ones1=np.ones((1, 128), np.float32),
        onesc=np.ones((128, 1), np.float32),
        eye=np.eye(128, dtype=np.float32),
    )
    lin = np.linspace(0.0, 1.0, 256, dtype=np.float32)
    nc, ctx = build()
    in_maps = []
    for c in range(8):
        b, qc = c // 4, c % 4
        q0 = qc * OWN
        lo, hi = q0 - 256, q0 + OWN + 256
        xp = np.zeros((EXT, D), np.float32)
        slo, shi = max(0, lo), min(S, hi)
        xp[slo - lo:shi - lo] = x[b, slo:shi]
        pos = np.clip(np.arange(lo, hi), 0, S - 1)
        cos, sin = _rope_tables(pos)
        s2 = np.empty((DH, EXT), np.float32)
        s2[0:64] = -sin.T[0:64]
        s2[64:128] = sin.T[64:128]
        vm = np.zeros((128, NKC), np.float32)
        for k in range(NKC):
            p0 = lo + k * 128
            vm[:, k] = 1.0 if (p0 >= 0 and p0 + 128 <= S) else 0.0
        a1 = np.empty((128, 8), np.float32)
        for qch in range(8):
            a1[:, qch] = lin[np.arange(128) + (qch % 2) * 128]
        a0 = (np.float32(1.0) - a1).astype(np.float32)
        if qc == 0:
            a0[:, 0:2] = 0.0
            a1[:, 0:2] = 1.0
        m = dict(shared)
        m.update(xin=np.ascontiguousarray(xp.T),
                 cosx=np.ascontiguousarray(cos.T), s2x=s2,
                 a0=a0, a1=a1, vm=vm)
        in_maps.append(m)
    res = run_bass_kernel_spmd(nc, in_maps, core_ids=list(range(8)))
    out = np.empty((B, S, D), np.float32)
    for c in range(8):
        b, qc = c // 4, c % 4
        out[b, qc * OWN:(qc + 1) * OWN, :] = res.results[c]["out"].T
    ctx.close()
    return out


# version 12 (final)
# speedup vs baseline: 1.2833x; 1.0092x over previous
import numpy as np
import concourse.bass as bass
import concourse.mybir as mybir
from concourse.bass_utils import run_bass_kernel_spmd
from contextlib import ExitStack

F32 = mybir.dt.float32
BF16 = mybir.dt.bfloat16
AX = mybir.AxisListType
ALU = mybir.AluOpType
ACT = mybir.ActivationFunctionType

D = 4096
NIF = 32
H = 32
DH = 128
OWN = 1024
EXT = 1536
NKC = EXT // 128
SCALE = 1.0 / float(np.sqrt(128.0))
AW = 49152
ENGS = ("sync", "gpsimd", "scalar", "vector", "tensor")


STREAM_ENG = {"wload0": "sync", "wload1": "sync", "wload2": "sync", "wmisc": "sync",
              "xload0": "gpsimd", "xload1": "gpsimd", "xload2": "gpsimd",
              "xload3": "gpsimd", "xload4": "gpsimd", "xload5": "gpsimd", "xstore": "gpsimd",
              "xsta": "gpsimd", "xstb": "gpsimd"}
SEMN = tuple(ENGS) + tuple(STREAM_ENG)


class Prog:
    def __init__(self):
        self.ops = {e: [] for e in ENGS}
        self.cnt = {s: 0 for s in SEMN}

    def emit(self, e, fn, waits=(), inc=0, isem=None):
        isem = isem or e
        self.ops[e].append((fn, tuple(waits), inc, isem))
        self.cnt[isem] += inc
        return self.cnt[isem]

    def dma(self, stream, out, in_, waits=(), accum=False):
        e = STREAM_ENG[stream]
        if accum:
            fn = lambda q, o=out, i=in_: q.dma_start(out=o, in_=i, accum_op=ALU.add)
        else:
            fn = lambda q, o=out, i=in_: q.dma_start(out=o, in_=i)
        return self.emit(e, fn, waits, 16, isem=stream)

    def drain(self):
        self.emit("vector", lambda v: v.drain(), (), 0)


class Arena:
    def __init__(self, t, off=0):
        self.t = t
        self.off = off

    def alloc(self, words):
        o = self.off
        self.off += words
        assert self.off <= AW, f"arena overflow {self.off} > {AW}"
        return o

    def f32(self, words, shape=None):
        o = self.alloc(words)
        ap = self.t[:, o:o + words]
        if shape is not None:
            ap = _resh(ap, shape)
        return ap

    def bf16(self, n, shape=None):
        o = self.alloc((n + 1) // 2)
        ap = self.t[:, o:o + (n + 1) // 2].bitcast(BF16)[:, 0:n]
        if shape is not None:
            ap = _resh(ap, shape)
        return ap

    def row(self, words):
        o = self.alloc(words)
        return self.t[0:1, o:o + words]


def _resh(ap, shape):
    names = [chr(97 + i) for i in range(len(shape))]
    return ap.rearrange("p (" + " ".join(names) + ") -> p " + " ".join(names),
                        **{n: s for n, s in zip(names, shape)})


def build():
    nc = bass.Bass("TRN2", target_bir_lowering=False)
    dt = lambda n, s, d=F32, k="ExternalInput": nc.dram_tensor(n, s, d, kind=k).ap()

    xin = dt("xin", [D, EXT])
    wbaseT = dt("wbaseT", [D, D])
    wqT = dt("wqT", [D, D])
    woT = dt("woT", [D, D])
    wkT = dt("wkT", [D, DH])
    wvT = dt("wvT", [D, DH])
    aT = dt("aT", [D, 128])
    bstk = dt("bstk", [128, D])
    wrT = dt("wrT", [D, 8])
    bbc = dt("bbc", [128, NIF])
    cosx_h = dt("cosx", [128, EXT])
    s2x_h = dt("s2x", [128, EXT])
    a0_h = dt("a0", [128, 8])
    a1_h = dt("a1", [128, 8])
    vm_h = dt("vm", [128, NKC])
    ones1_h = dt("ones1", [1, 128])
    onesc_h = dt("onesc", [128, 1])
    eye_h = dt("eye", [128, 128])
    out_h = dt("out", [D, OWN], F32, "ExternalOutput")
    x1h = dt("x1h", [D, EXT], F32, "Internal")
    x2h = dt("x2h", [D, OWN], F32, "Internal")

    xin_r = xin.rearrange("(c p) t -> p c t", p=128)
    x1h_r = x1h.rearrange("(c p) t -> p c t", p=128)
    x2h_r = x2h.rearrange("(c p) t -> p c t", p=128)
    out_r = out_h.rearrange("(c p) t -> p c t", p=128)
    wbaseT_r = wbaseT.rearrange("(c p) o -> p c o", p=128)
    wqT_r = wqT.rearrange("(c p) o -> p c o", p=128)
    woT_r = woT.rearrange("(c p) o -> p c o", p=128)
    wkT_r = wkT.rearrange("(c p) o -> p c o", p=128)
    wvT_r = wvT.rearrange("(c p) o -> p c o", p=128)
    aT_r = aT.rearrange("(c p) o -> p c o", p=128)
    wrT_r = wrT.rearrange("(c p) o -> p c o", p=128)

    ctx = ExitStack()
    E = ctx.enter_context
    arena_t = E(nc.sbuf_tensor("arena", [128, AW], F32))
    ar = Arena(arena_t)

    PS1 = E(nc.psum_tensor("PS1", [128, 1024], F32))
    PS2 = E(nc.psum_tensor("PS2", [128, 1024], F32))
    PS3 = E(nc.psum_tensor("PS3", [128, 1024], F32))
    PS4 = E(nc.psum_tensor("PS4", [128, 512], F32))
    PSM = E(nc.psum_tensor("PSM", [128, 512], F32))
    PS4b = PS4[:].bitcast(BF16)
    PS3v = PS3[:].rearrange("p (c t) -> p c t", c=4)

    sems = {e: E(nc.semaphore("s_" + e)) for e in SEMN}
    pr = Prog()

    def store_q():
        return [(s, pr.cnt[s]) for s in ("xstore", "xsta", "xstb") if pr.cnt[s]]

    xlnb = ar.bf16(NIF * EXT, (NIF, EXT))
    bbt = ar.f32(NIF)
    bbh2 = ar.f32(NIF)
    id32 = ar.f32(128)
    idb16 = ar.bf16(128)
    onesb = ar.bf16(2)
    vmb = ar.bf16(NKC)
    vmf = ar.f32(NKC)
    a0t = ar.f32(8)
    a1t = ar.f32(8)
    RO = ar.alloc(7 * 128)
    ones1 = arena_t[0:1, RO:RO + 128]
    murow = arena_t[0:1, RO + 128:RO + 256]
    msrow = arena_t[0:1, RO + 256:RO + 384]
    t1row = arena_t[0:1, RO + 384:RO + 512]
    varrow = arena_t[0:1, RO + 512:RO + 640]
    invsrow = arena_t[0:1, RO + 640:RO + 768]
    misrow = arena_t[0:1, RO + 768:RO + 896]
    PBASE = ar.off

    sa = Arena(arena_t, PBASE)
    eyetmp = sa.f32(128)
    onesctmp = sa.f32(1)
    c_eye = pr.dma("wmisc", eyetmp, eye_h)
    c_b = pr.dma("wmisc", bbt, bbc)
    c_vm = pr.dma("wmisc", vmf, vm_h)
    c_a0 = pr.dma("wmisc", a0t, a0_h)
    c_a1 = pr.dma("wmisc", a1t, a1_h)
    c_o1 = pr.dma("wmisc", ones1, ones1_h)
    c_oc = pr.dma("wmisc", onesctmp, onesc_h)
    c_id = pr.dma("wmisc", id32, eye_h)
    pr.emit("scalar", lambda s: s.activation(idb16, eyetmp, ACT.Copy), [("wmisc", c_id)], 1)
    pr.emit("scalar", lambda s: s.activation(onesb[:, 0:1], onesctmp, ACT.Copy), [("wmisc", c_id)], 1)
    setup_sc = pr.emit("scalar", lambda s: s.activation(vmb, vmf, ACT.Copy), [("wmisc", c_id)], 1)
    setup_ve = pr.emit("vector", lambda v: v.tensor_scalar(bbh2, bbt, 0.5, None, ALU.mult),
                       [("wmisc", c_id)], 1)

    def ln_prep(src_r, ntok, base, entry=(), hook=None):
        a = Arena(arena_t, base)
        xf = [a.f32(NIF * 128, (NIF, 128)) for _ in range(2)]
        xb = [a.bf16(NIF * 256, (NIF, 256)), a.bf16(NIF * 256, (NIF, 256))]
        nch = ntok // 128
        apply2 = [0] * nch
        rankc = [0] * nch
        stats = [0] * nch
        rowrd = [0] * nch
        s1l = [0] * nch
        for i in range(nch):
            par = i % 2
            xpar = i % 2
            ps_st = [PSM[0:1, 0:256], PS1[0:1, 0:256]][par]
            ps_iv = [PS4[:, 0:128], PS2[:, 0:128]][par]
            ps_mi = [PSM[:, 0:128], PS1[:, 0:128]][par]
            wld = list(entry) if i == 0 else ([("vector", apply2[i - 2])] if i >= 2 else [])
            wld = wld + store_q()
            sem = "xload%d" % xpar
            semb = "xload%d" % (2 + xpar)
            g0 = pr.dma(sem, xf[xpar][:, 0:16, :], src_r[:, 0:16, i * 128:(i + 1) * 128], wld)
            g = pr.dma(semb, xf[xpar][:, 16:32, :], src_r[:, 16:32, i * 128:(i + 1) * 128], [])
            wsc = [(sem, g0)]
            if i >= 2:
                wsc.append(("tensor", stats[i - 2]))
            pr.emit("scalar", lambda s, p=par, xp=xpar: s.activation(
                xb[p][:, 0:16, 0:128], xf[xp][:, 0:16, :], ACT.Copy), wsc, 1)
            sh0 = pr.emit("scalar", lambda s, p=par, xp=xpar: s.activation(
                xb[p][:, 0:16, 128:256], xf[xp][:, 0:16, :], ACT.Square), [], 1)
            pr.emit("scalar", lambda s, p=par, xp=xpar: s.activation(
                xb[p][:, 16:32, 0:128], xf[xp][:, 16:32, :], ACT.Copy), [(semb, g)], 1)
            s1l[i] = s1 = pr.emit("scalar", lambda s, p=par, xp=xpar: s.activation(
                xb[p][:, 16:32, 128:256], xf[xp][:, 16:32, :], ACT.Square), [], 1)
            wmm = [("scalar", sh0)]
            if i >= 2:
                wmm.append(("vector", apply2[i - 2]))
            for c in range(NIF):
                w_ = wmm if c == 0 else ([("scalar", s1)] if c == 16 else [])
                stats[i] = pr.emit("tensor", lambda t, c=c, p=par, ps=ps_st: t.matmul(
                    ps, onesb[:, 0:1], xb[p][:, c, :],
                    start=(c == 0), stop=(c == NIF - 1)), w_,
                    1 if c == NIF - 1 else 0)
            wv = [("tensor", stats[i])]
            if i >= 1:
                wv.append(("tensor", rankc[i - 1]))
            pr.emit("vector", lambda v, ps=ps_st: v.tensor_scalar(
                murow, ps[0:1, 0:128], 1.0 / D, None, ALU.mult), wv, 1)
            pr.emit("vector", lambda v, ps=ps_st: v.tensor_scalar(
                msrow, ps[0:1, 128:256], 1.0 / D, None, ALU.mult), [], 1)
            pr.drain()
            pr.emit("vector", lambda v: v.tensor_tensor(t1row, murow, murow, ALU.mult), [], 1)
            pr.drain()
            pr.emit("vector", lambda v: v.tensor_tensor(varrow, msrow, t1row, ALU.subtract), [], 1)
            pr.drain()
            rv = pr.emit("vector", lambda v: v.tensor_scalar(varrow, varrow, 1e-5, None, ALU.add), [], 1)
            si = pr.emit("scalar", lambda s: s.activation(t1row, varrow, ACT.Sqrt),
                         [("vector", rv)], 1)
            vr = pr.emit("vector", lambda v: v.reciprocal(invsrow, t1row), [("scalar", si)], 1)
            pr.drain()
            rowrd[i] = pr.emit("vector", lambda v: v.tensor_tensor(misrow, murow, invsrow, ALU.mult),
                               [], 1)
            pr.emit("tensor", lambda t, ps=ps_iv: t.matmul(ps, ones1, invsrow, start=True, stop=True),
                    [("vector", rowrd[i])], 1)
            rankc[i] = pr.emit("tensor", lambda t, ps=ps_mi: t.matmul(
                ps, ones1, misrow, start=True, stop=True), [], 1)
            pr.emit("vector", lambda v, xp=xpar, ps=ps_iv: v.tensor_tensor(
                xf[xp], xf[xp], ps.unsqueeze(1).broadcast_to([128, NIF, 128]), ALU.mult),
                [("tensor", rankc[i])], 1)
            pr.drain()
            apply2[i] = pr.emit("vector", lambda v, xp=xpar, i=i, ps=ps_mi: v.tensor_tensor(
                xlnb[:, :, i * 128:(i + 1) * 128], xf[xp],
                ps.unsqueeze(1).broadcast_to([128, NIF, 128]), ALU.subtract), [], 1)
            if hook is not None:
                hook(i, apply2)
        return {"vector": apply2[nch - 1], "tensor": max(stats[nch - 1], rankc[nch - 1]),
                "scalar": s1l[nch - 1]}

    def make_stager(wstage):
        nb = len(wstage)
        state = {"i": 0, "cast": {}}

        def load_cast(src_ap, dst_bf_ap, extra_dma_waits=(), extra_cast_waits=()):
            i = state["i"]
            state["i"] += 1
            wdma = list(extra_dma_waits)
            if i - nb in state["cast"]:
                wdma.append(("scalar", state["cast"][i - nb]))
            sem = "wload%d" % (i % nb)
            g = pr.dma(sem, wstage[i % nb], src_ap, wdma)
            wc = [(sem, g)] + list(extra_cast_waits)
            c = pr.emit("scalar", lambda s, si=i % nb, d=dst_bf_ap: s.activation(d, wstage[si], ACT.Copy), wc, 1)
            state["cast"][i] = c
            return c
        return load_cast

    def ffn(src_r, dst_r, ntok):
        snap0 = [("scalar", pr.cnt["scalar"]), ("vector", pr.cnt["vector"]),
                 ("tensor", pr.cnt["tensor"])]
        a0a = Arena(arena_t, PBASE)
        ab16 = a0a.bf16(NIF * 128, (NIF, 128))
        bb16 = a0a.bf16(D)
        rb16 = a0a.bf16(NIF * 8, (NIF, 8))
        hwb = a0a.bf16(EXT)
        PREP_B = a0a.off
        st_a = Arena(arena_t, PREP_B)
        atmp = st_a.f32(NIF * 128, (NIF, 128))
        bstmp = st_a.f32(D)
        rtmp = st_a.f32(NIF * 8, (NIF, 8))
        c_a = pr.dma("wmisc", atmp, aT_r, snap0)
        c_bs = pr.dma("wmisc", bstmp, bstk, [])
        c_r = pr.dma("wmisc", rtmp, wrT_r, [])
        pr.emit("scalar", lambda s: s.activation(ab16[:], atmp, ACT.Copy), [("wmisc", c_r)] + snap0, 1)
        pr.emit("scalar", lambda s: s.activation(bb16[:], bstmp, ACT.Copy), [("wmisc", c_r)], 1)
        sm_cast = pr.emit("scalar", lambda s: s.activation(rb16[:], rtmp, ACT.Copy), [("wmisc", c_r)], 1)
        snap1 = [("scalar", sm_cast), ("vector", pr.cnt["vector"]), ("tensor", pr.cnt["tensor"])]
        ga = Arena(arena_t, PREP_B)
        gxb = ga.f32(512)
        lg = arena_t[0:8, ga.alloc(512) + 0:ga.off]
        ltm = ga.f32(32, (4, 8))
        gate = ga.f32(32, (4, 8))
        eq1 = ga.f32(32, (4, 8))
        eq2 = ga.f32(32, (4, 8))
        mskb = ga.f32(32)
        m1c = ga.f32(4)
        m2c = ga.f32(4)
        p1c = ga.f32(4)
        p2c = ga.f32(4)
        OFL_B = ga.off

        NTS = ntok // 512
        st_aph = {"hw": 0, "hw_hist": [0] * (NTS + 2), "se": 0, "sg": 0}

        def aphase(ts, wait_v):
            cols = slice(ts * 512, (ts + 1) * 512)
            hslot = PS3[:, (ts % 2) * 512:(ts % 2) * 512 + 512]
            wmm = [("vector", wait_v), ("scalar", sm_cast)]
            if ts >= 2:
                wmm.append(("vector", st_aph["hw_hist"][ts - 2]))
            for c in range(NIF):
                pr.emit("tensor", lambda t, c=c, h=hslot, cc=cols: t.matmul(
                    h, ab16[:, c, :], xlnb[:, c, cc], start=(c == 0), stop=(c == NIF - 1)),
                    wmm if c == 0 else [], 0)
            wrt = [("scalar", st_aph["se"])] if ts > 0 else [("vector", wait_v)]
            h_cnt = 0
            for c in range(NIF):
                h_cnt = pr.emit("tensor", lambda t, c=c, cc=cols: t.matmul(
                    PS1[0:8, 512:1024], rb16[:, c, :], xlnb[:, c, cc],
                    start=(c == 0), stop=(c == NIF - 1)), wrt if c == 0 else [],
                    1 if c == NIF - 1 else 0)
            se = pr.emit("scalar", lambda s: s.activation(lg, PS1[0:8, 512:1024], ACT.Copy),
                         [("tensor", h_cnt)], 1)
            st_aph["se"] = se
            tp = 0
            for qc in range(4):
                tp = pr.emit("tensor", lambda t, qc=qc: t.transpose(
                    PS2[:, 512 + qc * 8:512 + (qc + 1) * 8], lg[0:8, qc * 128:(qc + 1) * 128],
                    id32[0:8, 0:8]),
                    ([("scalar", se), ("scalar", st_aph["sg"])] if qc == 0 else []),
                    1 if qc == 3 else 0)
            se2 = pr.emit("scalar", lambda s: s.activation(
                ltm, _resh(PS2[:, 512:544], (4, 8)), ACT.Copy), [("tensor", tp)], 1)
            pr.emit("vector", lambda v: v.reduce_max(m1c, ltm, axis=AX.X), [("scalar", se2)], 1)
            pr.drain()
            for qc in range(4):
                pr.emit("vector", lambda v, qc=qc: v.tensor_scalar(
                    eq1[:, qc, :], ltm[:, qc, :], m1c[:, qc:qc + 1], None, ALU.is_equal), [], 1)
            pr.drain()
            pr.emit("vector", lambda v: v.scalar_tensor_tensor(
                mskb, eq1.rearrange("p c e -> p (c e)"), -30.0, ltm.rearrange("p c e -> p (c e)"),
                ALU.mult, ALU.add), [], 1)
            pr.drain()
            vm2 = pr.emit("vector", lambda v: v.reduce_max(m2c, _resh(mskb, (4, 8)), axis=AX.X), [], 1)
            sp = 0
            for qc in range(4):
                sp = pr.emit("scalar", lambda s, qc=qc: s.activation(
                    p1c[:, qc:qc + 1], m2c[:, qc:qc + 1], ACT.Sigmoid, bias=m1c[:, qc:qc + 1], scale=-1.0),
                    [("vector", vm2)] if qc == 0 else [], 1)
                sp = pr.emit("scalar", lambda s, qc=qc: s.activation(
                    p2c[:, qc:qc + 1], m1c[:, qc:qc + 1], ACT.Sigmoid, bias=m2c[:, qc:qc + 1], scale=-1.0),
                    [], 1)
            for qc in range(4):
                pr.emit("vector", lambda v, qc=qc: v.tensor_scalar(
                    gate[:, qc, :], eq1[:, qc, :], p1c[:, qc:qc + 1], 2.0, ALU.mult, ALU.mult),
                    [("scalar", sp)] if qc == 0 else [], 1)
            for qc in range(4):
                pr.emit("vector", lambda v, qc=qc: v.tensor_scalar(
                    eq2[:, qc, :], _resh(mskb, (4, 8))[:, qc, :], m2c[:, qc:qc + 1], None, ALU.is_equal),
                    [], 1)
            pr.drain()
            for qc in range(4):
                pr.emit("vector", lambda v, qc=qc: v.tensor_scalar(
                    eq2[:, qc, :], eq2[:, qc, :], p2c[:, qc:qc + 1], 2.0, ALU.mult, ALU.mult), [], 1)
            pr.drain()
            pr.emit("vector", lambda v: v.tensor_tensor(
                gate.rearrange("p c e -> p (c e)"), gate.rearrange("p c e -> p (c e)"),
                eq2.rearrange("p c e -> p (c e)"), ALU.add), [], 1)
            pr.drain()
            vgx = pr.emit("vector", lambda v: v.tensor_copy(
                _resh(gxb, (4, 8, 16)), gate.unsqueeze(3).broadcast_to([128, 4, 8, 16])), [], 1)
            tg = 0
            for qc in range(4):
                tg = pr.emit("tensor", lambda t, qc=qc: t.transpose(
                    PS2[:, 512 + qc * 128:512 + (qc + 1) * 128], gxb[:, qc * 128:(qc + 1) * 128], id32),
                    [("vector", vgx), ("scalar", se2)] if qc == 0 else [], 1 if qc == 3 else 0)
            sg = pr.emit("scalar", lambda s: s.activation(gxb, PS2[:, 512:1024], ACT.Copy),
                         [("tensor", tg)], 1)
            st_aph["sg"] = sg
            st_aph["hw_hist"][ts] = st_aph["hw"] = pr.emit(
                "vector", lambda v, h=hslot, cc=cols: v.tensor_tensor(
                    hwb[:, cc], h, gxb, ALU.mult), [("scalar", sg)], 1)

        def hook(i, ap2):
            if i % 4 == 3 and (i // 4) < NTS:
                aphase(i // 4, ap2[i])

        pd = ln_prep(src_r, ntok, OFL_B, snap1, hook=hook)
        prep_v, prep_t = pd["vector"], pd["tensor"]
        hw_cnt = st_aph["hw"]
        a = Arena(arena_t, OFL_B)
        wstage = [a.f32(8 * 128, (8, 128)) for _ in range(3)]
        wb = [a.bf16(NIF * 128, (NIF, 128)), a.bf16(NIF * 128, (NIF, 128))]
        sl_sizes = []
        t0 = 0
        while t0 < ntok:
            n = min(512, ntok - t0)
            sl_sizes.append((t0, n))
            t0 += n
        slabs = [[a.f32(n) for (s0, n) in sl_sizes] for _ in range(2)]
        stage = make_stager(wstage)

        grp_cnt = {}
        evac_cnt = {}
        first_slab = True
        for of in range(NIF):
            extra = [("vector", prep_v)] if of == 0 else []
            ecast = [("tensor", prep_t)] if of == 0 else []
            if of >= 2:
                ecast = [("tensor", grp_cnt[(of - 2, len(sl_sizes) - 1)])]
            for s4 in range(4):
                stage(wbaseT_r[:, s4 * 8:(s4 + 1) * 8, of * 128:(of + 1) * 128],
                      wb[of % 2][:, s4 * 8:(s4 + 1) * 8, :],
                      extra if s4 == 0 else (), ecast if s4 == 0 else ())
            cast_done = pr.cnt["scalar"]
            xsem = "xload%d" % (of % 2)
            for tsl, (t0, n) in enumerate(sl_sizes):
                wld = [("vector", prep_v)] if first_slab else []
                first_slab = False
                wld = wld + store_q()
                gld = pr.dma(xsem, slabs[of % 2][tsl], src_r[:, of, t0:t0 + n], wld)
            for tsl, (t0, n) in enumerate(sl_sizes):
                gidx = of * len(sl_sizes) + tsl
                pspair = [PS1, PS2][(gidx // 2) % 2]
                off = (gidx % 2) * 512
                slot = pspair[:, off:off + n]
                wmm = [("scalar", cast_done), ("vector", hw_cnt)]
                if gidx >= 4:
                    wmm.append(("vector", evac_cnt[gidx - 4]))
                for c in range(NIF):
                    pr.emit("tensor", lambda t, c=c, sl=slot, of=of, t0=t0, n=n: t.matmul(
                        sl, wb[of % 2][:, c, :], xlnb[:, c, t0:t0 + n],
                        start=(c == 0), stop=False), wmm if c == 0 else [], 0)
                gmm = pr.emit("tensor", lambda t, sl=slot, of=of, t0=t0, n=n: t.matmul(
                    sl, bb16[:, of * 128:(of + 1) * 128], hwb[:, t0:t0 + n],
                    start=False, stop=True), [], 1)
                grp_cnt[(of, tsl)] = gmm
                pr.emit("vector", lambda v, of=of, tsl=tsl: v.tensor_scalar(
                    slabs[of % 2][tsl], slabs[of % 2][tsl], 1.0, bbh2[:, of:of + 1],
                    ALU.mult, ALU.add), [(xsem, gld)], 1)
                pr.drain()
                ev = pr.emit("vector", lambda v, of=of, tsl=tsl, sl=slot: v.scalar_tensor_tensor(
                    slabs[of % 2][tsl], sl, 0.5, slabs[of % 2][tsl],
                    ALU.mult, ALU.add), [("tensor", gmm)], 1)
                evac_cnt[gidx] = ev
                pr.dma("xstore", dst_r[:, of, t0:t0 + n], slabs[of % 2][tsl], [("vector", ev)])

    def attn():
        snap0 = [("scalar", pr.cnt["scalar"]), ("vector", pr.cnt["vector"]),
                 ("tensor", pr.cnt["tensor"])]
        pd = ln_prep(x1h_r, EXT, PBASE, snap0)
        prep_v, prep_t = pd["vector"], pd["tensor"]
        a = Arena(arena_t, PBASE)
        cosx = a.f32(EXT)
        s2x = a.f32(EXT)
        krb = a.bf16(EXT)
        vext = a.bf16(NKC * 132, (NKC, 132))
        wstage = [a.f32(4 * 128, (4, 128)) for _ in range(3)]
        wqb = [a.bf16(NIF * 128, (NIF, 128)), a.bf16(NIF * 128, (NIF, 128))]
        qt1 = a.f32(1024)
        qt2 = a.f32(1024)
        qrb = [a.bf16(1024), a.bf16(1024)]
        vbst = a.bf16(1024)
        sgb = [a.bf16(4 * 512, (4, 512)), a.bf16(4 * 512, (4, 512))]
        wnr = [a.f32(4 * 129, (4, 129)), a.f32(4 * 129, (4, 129))]
        rca = [a.f32(4), a.f32(4)]
        wout8 = a.bf16(8 * 1024, (8, 1024))
        slabs = [a.f32(512), a.f32(512)]
        diag = a.f32(4 * 128, (4, 128))
        stage = make_stager(wstage)

        cl = pr.dma("wmisc", cosx, cosx_h, [("vector", prep_v)])
        cos_l = pr.dma("wmisc", s2x, s2x_h, [])
        pr.drain()
        pr.emit("vector", lambda v: v.memset(vext[:, :, 129:132], 0.0), [], 1)

        for s8 in range(8):
            stage(wkT_r[:, s8 * 4:(s8 + 1) * 4, :], wqb[0][:, s8 * 4:(s8 + 1) * 4, :],
                  [("vector", prep_v)] if s8 == 0 else (), [("tensor", prep_t)] if s8 == 0 else ())
        for s8 in range(8):
            stage(wvT_r[:, s8 * 4:(s8 + 1) * 4, :], wqb[1][:, s8 * 4:(s8 + 1) * 4, :])
        kv_cast = pr.cnt["scalar"]
        kvw = pr.cnt["wmisc"]

        kv_ev = 0
        vmm = 0
        kev = 0
        for grp in range(3):
            t0 = grp * 512
            n = 512
            cols = slice(t0, t0 + n)
            off = (grp % 2) * 512
            wmm = [("scalar", kv_cast), ("vector", prep_v)]
            if grp >= 1:
                wmm.append(("vector", kev))
            kmm = 0
            for c in range(NIF):
                kmm = pr.emit("tensor", lambda t, c=c, cc=cols, off=off: t.matmul(
                    PS1[:, off:off + 512], wqb[0][:, c, :], xlnb[:, c, cc],
                    start=(c == 0), stop=(c == NIF - 1)),
                    wmm if c == 0 else [], 1 if c == NIF - 1 else 0)
            for c in range(NIF):
                vmm = pr.emit("tensor", lambda t, c=c, cc=cols, off=off: t.matmul(
                    PS2[:, off:off + 512], wqb[1][:, c, :], xlnb[:, c, cc],
                    start=(c == 0), stop=(c == NIF - 1)),
                    [], 1 if c == NIF - 1 else 0)
            pr.emit("vector", lambda v, cc=cols, off=off: v.tensor_tensor(
                qt1[:, 0:512], PS1[:, off:off + 512], cosx[:, cc], ALU.mult),
                [("tensor", kmm), ("wmisc", kvw)], 1)
            pr.emit("vector", lambda v, cc=cols, off=off: v.tensor_tensor(
                qt2[0:64, 0:512], PS1[64:128, off:off + 512], s2x[0:64, cc], ALU.mult), [], 1)
            pr.emit("vector", lambda v, cc=cols, off=off: v.tensor_tensor(
                qt2[64:128, 0:512], PS1[0:64, off:off + 512], s2x[64:128, cc], ALU.mult), [], 1)
            pr.drain()
            kev = pr.emit("vector", lambda v, cc=cols: v.tensor_tensor(
                krb[:, cc], qt1[:, 0:512], qt2[:, 0:512], ALU.add), [], 1)
            sv = pr.emit("scalar", lambda s, off=off: s.activation(
                vbst[:, 0:512], PS2[:, off:off + 512], ACT.Copy), [("tensor", vmm)], 1)
            tv = 0
            for k2 in range(4):
                tv = pr.emit("tensor", lambda t, k2=k2: t.transpose(
                    PS4b[:, k2 * 128:(k2 + 1) * 128], vbst[:, k2 * 128:(k2 + 1) * 128],
                    idb16), [("scalar", sv), ("vector", kv_ev)] if k2 == 0 else [],
                    1 if k2 == 3 else 0)
            kcg0 = t0 // 128
            pr.emit("vector", lambda v, kcg0=kcg0: v.tensor_tensor(
                vext[:, kcg0:kcg0 + 4, 0:128], _resh(PS4b, (8, 128))[:, 0:4, :],
                vmb[:, kcg0:kcg0 + 4].unsqueeze(2).broadcast_to([128, 4, 128]), ALU.mult),
                [("tensor", tv)], 1)
            kv_ev = pr.emit("vector", lambda v, kcg0=kcg0: v.tensor_copy(
                vext[:, kcg0:kcg0 + 4, 128:129], vmb[:, kcg0:kcg0 + 4].unsqueeze(2)), [], 1)

        pc = pr.cnt["xstore"]
        for i2 in range(64):
            of, ts2 = i2 // 2, i2 % 2
            par = i2 % 2
            w = [("xstore", pc), ("vector", prep_v)] if i2 == 0 else ([("xstore", pr.cnt["xstore"])] if i2 >= 2 else [])
            gl = pr.dma("xload%d" % par, slabs[par],
                        x1h_r[:, of, 256 + ts2 * 512:256 + (ts2 + 1) * 512], w)
            pr.dma("xstore", x2h_r[:, of, ts2 * 512:(ts2 + 1) * 512], slabs[par],
                   [("xload%d" % par, gl)])
        precopy_done = pr.cnt["xstore"]

        qmm = [0] * (H + 2)
        rope = [0] * (H + 2)
        wqc = [0] * (H + 2)
        sig_cnt = 0
        sig2_prev = 0
        av_hist = {}
        wn_last = kv_ev
        woutev = 0
        womm_prev_grp = max(vmm, 1)
        wo_ev = {}
        wo_done = 0
        dg_guard = 0
        pend_blend = None
        sig_kc = [0, 0, 0, 0]
        wo_final = 0

        wq_sub = {}

        def stage_wq(h, extra=()):
            subs = []
            for s8 in range(8):
                ew = []
                if h < 2 and s8 == 0:
                    ew = [("tensor", vmm)]
                elif s8 == 0:
                    ew = [("tensor", qmm[h - 2])]
                ew = ew + (list(extra) if s8 == 0 else [])
                subs.append(stage(wqT_r[:, s8 * 4:(s8 + 1) * 4, h * 128:(h + 1) * 128],
                            wqb[h % 2][:, s8 * 4:(s8 + 1) * 4, :], (), ew))
            wq_sub[h] = subs
            wqc[h] = pr.cnt["scalar"]

        def make_qfeeder(h, extra_first=()):
            stepctr = {"n": 0}

            def feed(k):
                while k > 0 and stepctr["n"] < 64:
                    j = stepctr["n"]
                    hf, c = j // NIF, j % NIF
                    w = []
                    if hf == 0 and c % 4 == 0:
                        w.append(("scalar", wq_sub[h][c // 4]))
                    if j == 0:
                        w += [("vector", prep_v)] + list(extra_first)
                        if h >= 1:
                            w.append(("vector", rope[h - 1]))
                        if h % 8 == 0 and h > 0:
                            w.append(("scalar", wo_done))
                        if h == 0:
                            w.append(("vector", kv_ev))
                    qmm[h] = pr.emit("tensor", lambda t, c=c, par=h % 2, hf=hf: t.matmul(
                        PS1[:, hf * 512:(hf + 1) * 512], wqb[par][:, c, :],
                        xlnb[:, c, 256 + hf * 512:768 + hf * 512],
                        start=(c == 0), stop=(c == NIF - 1)), w,
                        1 if (c == NIF - 1 and hf == 1) else 0)
                    stepctr["n"] += 1
                    k -= 1
            return feed

        stage_wq(0)
        feed = make_qfeeder(0)
        feed(64)
        for hp in range(4):
            womm_last = 0
            for hh in range(8):
                h = hp * 8 + hh
                par = h % 2
                pr.emit("vector", lambda v, par=par: v.tensor_tensor(
                    qt1, PS1[:, 0:1024], cosx[:, 256:1280], ALU.mult), [("tensor", qmm[h])], 1)
                pr.emit("vector", lambda v: v.tensor_tensor(qt2[0:64, :], PS1[64:128, 0:1024],
                                                            s2x[0:64, 256:1280], ALU.mult), [], 1)
                pr.emit("vector", lambda v: v.tensor_tensor(qt2[64:128, :], PS1[0:64, 0:1024],
                                                            s2x[64:128, 256:1280], ALU.mult), [], 1)
                pr.drain()
                rope[h] = pr.emit("vector", lambda v, par=par: v.tensor_tensor(
                    qrb[par], qt1, qt2, ALU.add), [], 1)
                if hh < 7:
                    stage_wq(h + 1)
                    feed = make_qfeeder(h + 1)
                else:
                    feed = lambda k: None
                for wi in range(5):
                    q0w = max(0, wi * 256 - 256)
                    q1w = min(1024, wi * 256 + 256)
                    qn = q1w - q0w
                    npp = qn // 128
                    kc0 = wi * 2
                    sp = wi % 2
                    if pend_blend is not None:
                        b, sA, wpar, dgc, hh_ = pend_blend
                        pend_blend = None
                        tw = 0
                        for j in range(2):
                            pr.emit("tensor", lambda t, b=b, j=j, sA=sA: t.matmul(
                                PS4[:, j * 128:(j + 1) * 128], wnr[b % 2][:, sA + j, 0:128],
                                diag[:, j, :], start=True, stop=False),
                                [("vector", dgc), ("scalar", woutev)] if j == 0 else [], 0)
                            tw = pr.emit("tensor", lambda t, b=b, j=j, wpar=wpar: t.matmul(
                                PS4[:, j * 128:(j + 1) * 128], wnr[wpar][:, j, 0:128],
                                diag[:, 2 + j, :], start=False, stop=True), [], 1)
                        dg_guard = tw
                        woutev = pr.emit("scalar", lambda s, hh_=hh_, b=b: s.activation(
                            wout8[:, hh_, b * 256:(b + 1) * 256], PS4[:, 0:256], ACT.Copy),
                            [("tensor", tw), ("tensor", womm_prev_grp)], 1)
                    sgw = []
                    if (h, wi - 2) in av_hist:
                        sgw = [("tensor", av_hist[(h, wi - 2)])]
                    elif wi < 2 and h >= 1:
                        sgw = [("tensor", av_hist[(h - 1, 4 - wi)])]
                    for kc in range(4):
                        wsc = []
                        if kc == 0:
                            wsc = [("vector", rope[h]), ("scalar", sig_cnt)]
                        elif kc >= 2:
                            wsc = [("scalar", sig_kc[kc - 2])]
                        smm = pr.emit("tensor", lambda t, kc=kc, q0w=q0w, qn=qn, kc0=kc0, par=par: t.matmul(
                            PS2[:, (kc % 2) * 512:(kc % 2) * 512 + qn],
                            krb[:, (kc0 + kc) * 128:(kc0 + kc + 1) * 128],
                            qrb[par][:, q0w:q0w + qn], start=True, stop=True), wsc, 1)
                        sig_kc[kc] = pr.emit("scalar", lambda s, qn=qn, sp=sp, kc=kc: s.activation(
                            sgb[sp][:, kc, 0:qn],
                            PS2[:, (kc % 2) * 512:(kc % 2) * 512 + qn],
                            ACT.Sigmoid, scale=SCALE),
                            [("tensor", smm)] + (sgw if kc == 0 else []), 1)
                        if kc == 1:
                            feed(6)
                    sig_cnt = pr.cnt["scalar"]
                    sig2_prev = sig_cnt
                    feed(6)
                    avm = 0
                    for p in range(npp):
                        for kc in range(4):
                            st = (p % 2 == 0 and kc == 0)
                            sp_ = (p % 2 == 1 and kc == 3)
                            avm = pr.emit("tensor", lambda t, p=p, kc=kc, kc0=kc0, st=st, sp_=sp_, sp=sp: t.matmul(
                                PS3[:, p * 256:p * 256 + 130], sgb[sp][:, kc, p * 128:(p + 1) * 128],
                                vext[:, kc0 + kc, 0:130], start=st, stop=sp_),
                                (([("scalar", sig_cnt), ("vector", wn_last)] +
                                  ([("scalar", wo_final)] if (hh == 0 and wi == 0) else []))
                                 if (p == 0 and kc == 0) else []),
                                1 if (p == npp - 1 and kc == 3) else 0)
                    av_hist[(h, wi)] = avm
                    feed(4)
                    wn_last = pr.emit("vector", lambda v, npp=npp, wi=wi: v.tensor_copy(
                        wnr[wi % 2][:, 0:npp, :], PS3v[:, 0:npp, 0:129]), [("tensor", avm)], 1)
                    pr.drain()
                    pr.emit("vector", lambda v, npp=npp, wi=wi: v.reciprocal(
                        rca[wi % 2][:, 0:npp], wnr[wi % 2][:, 0:npp, 128:129][:, :, 0]), [], 1)
                    pr.drain()
                    if wi >= 1:
                        b = wi - 1
                        sA = 0 if b == 0 else 2
                        dgc = 0
                        for j in range(2):
                            pr.emit("vector", lambda v, b=b, j=j, sA=sA: v.tensor_scalar(
                                diag[:, j, :], id32, a0t[:, 2 * b + j:2 * b + j + 1],
                                rca[b % 2][:, sA + j:sA + j + 1], ALU.mult, ALU.mult),
                                [("tensor", dg_guard)] if j == 0 else [], 1)
                            dgc = pr.emit("vector", lambda v, b=b, j=j, wi=wi: v.tensor_scalar(
                                diag[:, 2 + j, :], id32, a1t[:, 2 * b + j:2 * b + j + 1],
                                rca[wi % 2][:, j:j + 1], ALU.mult, ALU.mult), [], 1)
                        pend_blend = (b, sA, wi % 2, dgc, hh)
                if pend_blend is not None:
                    b, sA, wpar, dgc, hh_ = pend_blend
                    pend_blend = None
                    tw = 0
                    for j in range(2):
                        pr.emit("tensor", lambda t, b=b, j=j, sA=sA: t.matmul(
                            PS4[:, j * 128:(j + 1) * 128], wnr[b % 2][:, sA + j, 0:128],
                            diag[:, j, :], start=True, stop=False),
                            [("vector", dgc), ("scalar", woutev)] if j == 0 else [], 0)
                        tw = pr.emit("tensor", lambda t, b=b, j=j, wpar=wpar: t.matmul(
                            PS4[:, j * 128:(j + 1) * 128], wnr[wpar][:, j, 0:128],
                            diag[:, 2 + j, :], start=False, stop=True), [], 1)
                    dg_guard = tw
                    woutev = pr.emit("scalar", lambda s, hh_=hh_, b=b: s.activation(
                        wout8[:, hh_, b * 256:(b + 1) * 256], PS4[:, 0:256], ACT.Copy),
                        [("tensor", tw), ("tensor", womm_prev_grp)], 1)
                feed(64)
            for of in range(NIF):
                for s2_ in range(2):
                    ew = []
                    if of == 0 and s2_ == 0:
                        ew = [("tensor", qmm[hp * 8 + 7])]
                    elif of >= 2 and s2_ == 0:
                        ew = [("tensor", wo_grp)]
                    stage(woT_r[:, hp * 8 + s2_ * 4:hp * 8 + (s2_ + 1) * 4, of * 128:(of + 1) * 128],
                          wqb[of % 2][:, s2_ * 4:(s2_ + 1) * 4, :], (), ew)
                wo_cast = pr.cnt["scalar"]
                for ts2 in range(2):
                    gidx = of * 2 + ts2
                    slot = PS3[:, (gidx % 2) * 512:(gidx % 2) * 512 + 512]
                    sli = gidx % 2
                    wmm = [("scalar", wo_cast), ("scalar", woutev), ("scalar", sig_cnt)]
                    if gidx == 0:
                        wmm.append(("vector", wn_last))
                    if gidx >= 2:
                        wmm.append(("scalar", wo_ev[gidx - 2]))
                    elif hp > 0:
                        wmm.append(("scalar", wo_ev[62 + gidx]))
                    mmc = 0
                    for gg in range(8):
                        mmc = pr.emit("tensor", lambda t, gg=gg, of=of, ts2=ts2, sl=slot: t.matmul(
                            sl[:, 0:512], wqb[of % 2][:, gg, :],
                            wout8[:, gg, ts2 * 512:(ts2 + 1) * 512],
                            start=(gg == 0), stop=(gg == 7)), wmm if gg == 0 else [],
                            1 if gg == 7 else 0)
                    wo_grp = mmc
                    womm_last = mmc
                    xsem = ["xsta", "xstb"][gidx % 2]
                    ev = pr.emit("scalar", lambda s, sli=sli, sl=slot: s.activation(
                        slabs[sli], sl[:, 0:512], ACT.Copy),
                        [("tensor", mmc), (xsem, pr.cnt[xsem]), ("xstore", precopy_done)], 1)
                    wo_ev[gidx] = ev
                    pr.dma(xsem, x2h_r[:, of, ts2 * 512:(ts2 + 1) * 512], slabs[sli],
                           [("scalar", ev)], accum=True)
            womm_prev_grp = womm_last
            wo_final = pr.cnt["scalar"]
            wo_done = pr.cnt["scalar"]
            wo_ev = {62: wo_ev[62], 63: wo_ev[63]} if 63 in wo_ev else wo_ev
            if hp < 3:
                stage_wq(hp * 8 + 8, extra=[("tensor", womm_last)])
                feed = make_qfeeder(hp * 8 + 8, extra_first=[("scalar", wo_ev[63])])
                feed(64)

    ffn(xin_r, x1h_r, EXT)
    attn()
    ffn(x2h_r, out_r, OWN)

    total_st = {s: pr.cnt[s] for s in ("xstore", "xsta", "xstb")}

    with nc.Block() as block:
        def mk(e):
            def run(eng):
                hw = {}
                for fn, waits, inc, isem in pr.ops[e]:
                    for (we, wv) in waits:
                        if we == e:
                            continue
                        if wv > 0 and hw.get(we, -1) < wv:
                            eng.wait_ge(sems[we], wv)
                            hw[we] = wv
                    ins = fn(eng)
                    if inc:
                        ins.then_inc(sems[isem], inc)
                if e == "gpsimd":
                    for s, v in total_st.items():
                        if v:
                            eng.wait_ge(sems[s], v)
            return run
        block.sync(mk("sync"))
        block.gpsimd(mk("gpsimd"))
        block.scalar(mk("scalar"))
        block.vector(mk("vector"))
        block.tensor(mk("tensor"))
    return nc, ctx


def _rope_tables(pos):
    inv = 1.0 / (10000.0 ** (np.arange(0, DH, 2, dtype=np.float32) / np.float32(DH)))
    f = np.outer(pos.astype(np.float32), inv)
    cos = np.zeros((len(pos), DH), dtype=np.float32)
    sin = np.zeros((len(pos), DH), dtype=np.float32)
    cos[:, 0::2] = np.cos(f); cos[:, 1::2] = np.cos(f)
    sin[:, 0::2] = np.sin(f); sin[:, 1::2] = np.sin(f)
    return cos, sin


def kernel(L0_attn_x, L0_attn_Wq, L0_attn_Wk, L0_attn_Wv, L0_attn_Wo,
           L0_ffn_W_base, L0_ffn_b_base, L0_ffn_W_router, L0_ffn_A, L0_ffn_B):
    x = np.asarray(L0_attn_x, np.float32)
    B, S, _ = x.shape
    shared = dict(
        wbaseT=np.ascontiguousarray(np.asarray(L0_ffn_W_base, np.float32).T),
        wqT=np.ascontiguousarray(np.asarray(L0_attn_Wq, np.float32).T),
        woT=np.ascontiguousarray(np.asarray(L0_attn_Wo, np.float32).T),
        wkT=np.ascontiguousarray(np.asarray(L0_attn_Wk, np.float32).T),
        wvT=np.ascontiguousarray(np.asarray(L0_attn_Wv, np.float32).T),
        aT=np.ascontiguousarray(np.asarray(L0_ffn_A, np.float32).reshape(128, D).T),
        bstk=np.ascontiguousarray(np.transpose(np.asarray(L0_ffn_B, np.float32), (0, 2, 1)).reshape(128, D)),
        wrT=np.ascontiguousarray(np.asarray(L0_ffn_W_router, np.float32).T),
        bbc=np.ascontiguousarray(np.asarray(L0_ffn_b_base, np.float32).reshape(NIF, 128).T),
        ones1=np.ones((1, 128), np.float32),
        onesc=np.ones((128, 1), np.float32),
        eye=np.eye(128, dtype=np.float32),
    )
    lin = np.linspace(0.0, 1.0, 256, dtype=np.float32)
    nc, ctx = build()
    in_maps = []
    for c in range(8):
        b, qc = c // 4, c % 4
        q0 = qc * OWN
        lo, hi = q0 - 256, q0 + OWN + 256
        xp = np.zeros((EXT, D), np.float32)
        slo, shi = max(0, lo), min(S, hi)
        xp[slo - lo:shi - lo] = x[b, slo:shi]
        pos = np.clip(np.arange(lo, hi), 0, S - 1)
        cos, sin = _rope_tables(pos)
        s2 = np.empty((DH, EXT), np.float32)
        s2[0:64] = -sin.T[0:64]
        s2[64:128] = sin.T[64:128]
        vm = np.zeros((128, NKC), np.float32)
        for k in range(NKC):
            p0 = lo + k * 128
            vm[:, k] = 1.0 if (p0 >= 0 and p0 + 128 <= S) else 0.0
        a1 = np.empty((128, 8), np.float32)
        for qch in range(8):
            a1[:, qch] = lin[np.arange(128) + (qch % 2) * 128]
        a0 = (np.float32(1.0) - a1).astype(np.float32)
        if qc == 0:
            a0[:, 0:2] = 0.0
            a1[:, 0:2] = 1.0
        m = dict(shared)
        m.update(xin=np.ascontiguousarray(xp.T),
                 cosx=np.ascontiguousarray(cos.T), s2x=s2,
                 a0=a0, a1=a1, vm=vm)
        in_maps.append(m)
    res = run_bass_kernel_spmd(nc, in_maps, core_ids=list(range(8)))
    out = np.empty((B, S, D), np.float32)
    for c in range(8):
        b, qc = c // 4, c % 4
        out[b, qc * OWN:(qc + 1) * OWN, :] = res.results[c]["out"].T
    ctx.close()
    return out
